# version 22
# baseline (speedup 1.0000x reference)
import contextlib
import numpy as np
import ml_dtypes
import concourse.bass as bass
import concourse.mybir as mybir
from concourse.bass_utils import run_bass_kernel_spmd

F32 = mybir.dt.float32
BF16 = mybir.dt.bfloat16
AF = mybir.ActivationFunctionType
ALU = mybir.AluOpType

D = 1024
DI = 2048
NCH = 8
WC = 504
CH = 512
EPS_LN = 1e-5
DBG_TI = 0

ENG_SEM = {"pe": "e_pe", "act": "e_act", "dve": "e_dve", "pool": "e_pool",
           "sync": "e_sync"}


class Op:
    __slots__ = ("idx", "eng", "fn", "deps", "inc", "waits", "is_dma",
                 "dma_sem", "marker", "tag")


class Plan:

    ALWAYS_FINAL = {"d_ld", "d_ldp", "d_sf", "d_siw", "d_fuse", "d_hbw",
                    "d_pww"}
    AFTER_LAST = {"d_sm0", "d_sm1",
                  "d_sd0", "d_sd1"}

    def __init__(self):
        self.ops = []
        self.spaces = {}

    def emit(self, eng, fn, r=(), w=(), dma=None, tag=""):
        idx = len(self.ops)
        deps = set()
        for iv in r:
            sp = self.spaces.setdefault(iv[0], [])
            lo, hi = iv[1], iv[2]
            for e in sp:
                if e[0] < hi and lo < e[1]:
                    if e[2] is not None:
                        deps.add(e[2])
                    e[3].append(idx)
        for iv in w:
            sp = self.spaces.setdefault(iv[0], [])
            lo, hi = iv[1], iv[2]
            keep = []
            for e in sp:
                if e[0] < hi and lo < e[1]:
                    if e[2] is not None:
                        deps.add(e[2])
                    deps.update(e[3])
                    if e[0] < lo:
                        keep.append([e[0], lo, e[2], list(e[3])])
                    if hi < e[1]:
                        keep.append([hi, e[1], e[2], list(e[3])])
                else:
                    keep.append(e)
            keep.append([lo, hi, idx, []])
            self.spaces[iv[0]] = keep
        op = Op()
        op.idx = idx
        op.eng = eng
        op.fn = fn
        op.deps = deps
        op.is_dma = dma is not None
        op.dma_sem = dma
        op.tag = tag
        self.ops.append(op)
        return idx

    def finalize(self):
        referenced = set()
        for op in self.ops:
            pruned = set()
            for d in op.deps:
                p = self.ops[d]
                if (not p.is_dma) and p.eng == op.eng:
                    continue
                pruned.add(d)
            op.deps = pruned
            referenced.update(pruned)
        ctr = {}
        for op in self.ops:
            if op.is_dma:
                ctr[op.dma_sem] = ctr.get(op.dma_sem, 0) + 16
                op.marker = (op.dma_sem, ctr[op.dma_sem])
                op.inc = (op.dma_sem, 16)
            elif op.idx in referenced:
                s = ENG_SEM[op.eng]
                ctr[s] = ctr.get(s, 0) + 1
                op.marker = (s, ctr[s])
                op.inc = (s, 1)
            else:
                op.marker = None
                op.inc = None
        self.eng_ops = {}
        for op in self.ops:
            self.eng_ops.setdefault(op.eng, []).append(op)
        totals = dict(ctr)
        last_dma = {}
        for op in self.ops:
            if op.is_dma:
                last_dma[op.dma_sem] = op.idx
        for eng, ops in self.eng_ops.items():
            observed = {}
            for op in ops:
                ws = {}
                for d in op.deps:
                    s, v = self.ops[d].marker
                    if s in self.ALWAYS_FINAL or (
                            s in self.AFTER_LAST and op.idx > last_dma[s]):
                        v = totals[s]
                    if observed.get(s, 0) < v:
                        ws[s] = max(ws.get(s, 0), v)
                for s, v in ws.items():
                    observed[s] = v
                op.waits = sorted(ws.items())

    def run(self, eng_name, eng, sems):
        for op in self.eng_ops.get(eng_name, []):
            for s, v in op.waits:
                eng.wait_ge(sems[s], v)
            inst = op.fn(eng)
            if op.inc is not None:
                inst.then_inc(sems[op.inc[0]], op.inc[1])


class SBuf:
    def __init__(self, nc, name, shape, dtype, off):
        self.t = nc.alloc_sbuf_tensor_at(name, shape, dtype, offset=off)
        self.off = off
        self.esz = 2 if dtype == BF16 else 4
        n = 1
        for s in shape[1:]:
            n *= s
        self.ncols = n

    def __getitem__(self, key):
        return self.t[key]

    def rearrange(self, *a, **k):
        return self.t.rearrange(*a, **k)

    def iv(self, a=0, b=None):
        if b is None:
            b = self.ncols
        return ("sbuf", self.off + a * self.esz, self.off + b * self.esz)


def build_nc(L, dbg=False):
    n_tiles = (L + WC - 1) // WC
    nc = bass.Bass("TRN2", target_bir_lowering=False)

    def din(name, shape, dt=F32):
        return nc.dram_tensor(name, shape, dt, kind="ExternalInput")

    xdr = din("x", [D, L])
    inwT = din("inwT", [2048, 2048])
    normw = din("normw", [128, NCH])
    inb = din("inb", [128, 32])
    dwataps = din("dwataps", [128, 64])
    dwab = din("dwab", [128, 16])
    abuf = din("abuf", [DI, 16])
    bmT = din("bmT", [DI, 16])
    symtaps = din("symtaps", [128, 144])
    symbrow = din("symbrow", [1, DI])
    pww = din("pww", [DI, DI])
    pwb = din("pwb", [128, 16])
    hawT = din("hawT", [DI, D])
    hab = din("hab", [128, 8])
    hbw = din("hbw", [D, DI])
    hbb = din("hbb", [128, 8])
    fuseT = din("fuseT", [DI, DI])
    fbrow = din("fbrow", [1, DI])
    outwT = din("outwT", [1024, 2048])
    outb = din("outb", [128, 8])
    wrT = din("wrT", [4 * 128, 2048])
    wkT = din("wkT", [4 * 128, 2048])
    wvT = din("wvT", [4 * 128, 2048])
    mur = din("mur", [128, 8])
    muk = din("muk", [128, 8])
    ident_dr = din("ident", [128, 128], BF16)

    outdr = nc.dram_tensor("out", [D, L], F32, kind="ExternalOutput")
    dbt = {}
    if dbg:
        for nm, rows, cols in [("val", 128, 4096), ("gate", 128, 4096),
                               ("silu", 128, 4096), ("alow", 16, 512),
                               ("yg", 128, 4096), ("attn", 128, 4096),
                               ("x1", 128, 4096), ("war", 128, 4096),
                               ("xr", 128, 4096), ("ffo", 128, 4096),
                               ("zw", 128, 4096), ("ln1x", 128, 4096),
                               ("g", 16, 2048), ("bias", 1, 2048),
                               ("t2", 128, 4096), ("fuseb", 128, 4096),
                               ("esqb", 128, 4096), ("erows", 3, 512),
                               ("ezw", 128, 4096), ("exw", 128, 4096)]:
            dbt[nm] = nc.dram_tensor("dbg_" + nm, [rows, cols], F32,
                                     kind="ExternalOutput")

    inw_scr = nc.dram_tensor("inw_scr", [16, 128, 2048], BF16, kind="Internal")
    m_scr = nc.dram_tensor("m_scr", [16, 128, 2048], BF16, kind="Internal")
    ow_scr = nc.dram_tensor("ow_scr", [8, 128, 2048], BF16, kind="Internal")
    wr_scr = nc.dram_tensor("wr_scr", [4 * 128, 2048], BF16, kind="Internal")
    wk_scr = nc.dram_tensor("wk_scr", [4 * 128, 2048], BF16, kind="Internal")
    wv_scr = nc.dram_tensor("wv_scr", [4 * 128, 2048], BF16, kind="Internal")
    dg_scr = nc.dram_tensor("dg_scr", [16, 128, 1152], BF16, kind="Internal")
    ffn_scr = {"wr": wr_scr, "wk": wk_scr, "wv": wv_scr}

    off = 16512

    def nxt(sz):
        nonlocal off
        o = off
        off += (sz + 31) & ~31
        return o

    CONSTS = SBuf(nc, "consts", [128, 8], F32, nxt(32))
    ONESC = SBuf(nc, "onesc", [128, 1], BF16, nxt(32))
    ONES128 = SBuf(nc, "ones128", [1, 128], BF16, nxt(256))
    ONESROW = SBuf(nc, "onesrow", [33, CH], BF16, nxt(1024))
    IDENT = SBuf(nc, "identsb", [128, 128], BF16, nxt(256))
    NORMW = SBuf(nc, "normwsb", [128, NCH], F32, nxt(32))
    INB = SBuf(nc, "inbsb", [128, 32], F32, nxt(128))
    OUTB = SBuf(nc, "outbsb", [128, 8], F32, nxt(32))
    MUR = SBuf(nc, "mursb", [128, 8], F32, nxt(32))
    MUK = SBuf(nc, "muksb", [128, 8], F32, nxt(32))
    DWAT = SBuf(nc, "dwatsb", [128, 64], F32, nxt(256))
    SYMT = SBuf(nc, "symtsb", [128, 144], BF16, nxt(288))
    ABF = SBuf(nc, "abfsb", [128, 256], BF16, nxt(512))
    AJ = SBuf(nc, "ajsb", [128, 1024], BF16, nxt(2048))
    G = SBuf(nc, "gsb", [16, DI], BF16, nxt(4096))
    BIAS = SBuf(nc, "biassb", [33, DI], BF16, nxt(4096))
    ALOW = SBuf(nc, "alowsb", [16, CH], BF16, nxt(1024))
    DWABBF = SBuf(nc, "dwabbf", [128, 16], BF16, nxt(32))
    HABBF = SBuf(nc, "habbf", [128, 8], BF16, nxt(32))
    HBBBF = SBuf(nc, "hbbbf", [128, 8], BF16, nxt(32))
    PWBBF = SBuf(nc, "pwbbf", [128, 16], BF16, nxt(32))
    VA = SBuf(nc, "vasb", [16, 1], BF16, nxt(32))
    INVNW = SBuf(nc, "invnw", [128, 8], F32, nxt(32))
    CONST1 = SBuf(nc, "const1", [128, 8], F32, nxt(32))
    WCARRY = SBuf(nc, "wcarry", [128, 8], F32, nxt(32))
    MROW = SBuf(nc, "mrow", [1, CH], BF16, nxt(1024))
    SROW1 = SBuf(nc, "srow1", [1, CH], BF16, nxt(1024))
    SROW2 = SBuf(nc, "srow2", [1, CH], BF16, nxt(1024))
    SROW3 = SBuf(nc, "srow3", [1, CH], BF16, nxt(1024))
    MSQR = SBuf(nc, "msqr", [1, CH], F32, nxt(2048))
    VROW = SBuf(nc, "vrow", [1, CH], F32, nxt(2048))
    LROW = SBuf(nc, "lrow", [1, CH], F32, MSQR.off)

    AR = 16384
    o_XW = nxt(AR)
    o_VALSQ = nxt(AR)
    o_R1 = nxt(AR)
    o_R2 = nxt(AR)
    o_R3 = nxt(AR)
    o_R4 = nxt(AR)
    o_X1 = nxt(AR)
    o_WAR = nxt(AR)

    XW = SBuf(nc, "xw", [128, 4096], F32, o_XW)
    LROW2 = SBuf(nc, "lrow2", [1, CH], F32, o_VALSQ + 8192)
    VAL = SBuf(nc, "val", [128, 8192], BF16, o_VALSQ)
    SQB = SBuf(nc, "sqb", [128, 4096], BF16, o_VALSQ)
    SQS = SBuf(nc, "sqs", [128, 4096], BF16, o_VALSQ + 8192)
    TF = SBuf(nc, "tf", [128, 4096], F32, o_VALSQ)
    SILU = SBuf(nc, "silu", [128, 8192], BF16, o_R1)
    RS = SBuf(nc, "rs", [128, 4096], BF16, o_R1 + 8192)
    UA = SBuf(nc, "ua", [128, 4096], F32, o_R1)
    FFO = SBuf(nc, "ffo", [128, 4096], F32, o_R1)
    YG = SBuf(nc, "yg", [128, 8192], BF16, o_R2)
    KRELU = SBuf(nc, "krelu", [128, 4096], BF16, o_R2)
    KACT = SBuf(nc, "kact", [128, 4096], BF16, o_R2 + 8192)
    GATE = SBuf(nc, "gate", [128, 8192], BF16, o_R3)
    XR = SBuf(nc, "xr", [128, 4096], BF16, o_R3)
    XK = SBuf(nc, "xk", [128, 4096], BF16, o_R3 + 8192)
    U4 = SBuf(nc, "u4", [128, 4096], F32, o_R3)
    ATTN = SBuf(nc, "attn", [128, 4096], F32, o_R4)
    LN1X = SBuf(nc, "ln1x", [128, 4096], BF16, o_R4 + 8192)
    X2 = SBuf(nc, "x2", [128, 4096], F32, o_WAR)
    X1 = SBuf(nc, "x1", [128, 4096], F32, o_X1)
    WARN = SBuf(nc, "war", [128, 4096], F32, o_WAR)

    INWB = SBuf(nc, "inwb", [128, 2 * 2048], BF16, nxt(8192))
    INWB3 = SBuf(nc, "inwb3", [128, 2048], BF16, o_R2)
    MB = SBuf(nc, "mbuf", [128, 3 * 2048], BF16, nxt(12288))
    OWB = SBuf(nc, "owb", [128, 2 * 2048], BF16, nxt(8192))
    FBB = SBuf(nc, "fbb", [128, 3 * 2048], BF16, nxt(12288))
    DGB = SBuf(nc, "dgb", [128, 3 * 1152], BF16, nxt(6912))
    TMPS = SBuf(nc, "tmps", [128, 512], BF16, nxt(1024))
    SQB2 = SBuf(nc, "sqb2", [128, 4096], BF16, o_R4)
    SQS2 = SBuf(nc, "sqs2", [128, 4096], BF16, MB.off)
    T4P = [SBuf(nc, f"t4p{i}", [128, 512], F32, OWB.off + 2048 * i)
           for i in range(2)]
    ZW = SBuf(nc, "zw2", [128, 4096], BF16, nxt(8192))
    tile_top = off
    assert tile_top <= 229344, f"SBUF over budget: {tile_top}"

    o_p = o_XW
    FUSEB = SBuf(nc, "fuseb", [128, 16 * 2048], BF16, o_p)
    HBWB = SBuf(nc, "hbwb", [128, 8 * 2048], BF16, o_p + 65536)
    HAB_ = SBuf(nc, "habuf", [128, 2 * 1024], BF16, o_p + 98304)
    T1 = SBuf(nc, "t1", [16, 1024], BF16, o_p + 102400)
    T1T = SBuf(nc, "t1t", [128, 128], BF16, o_p + 104448)
    BMTB = SBuf(nc, "bmtb", [128, 256], BF16, o_p + 104704)
    FBROW = SBuf(nc, "fbrowsb", [1, 2048], BF16, o_p + 105216)
    PWWB = SBuf(nc, "pwwb", [128, 16 * 2048], BF16, o_p)
    T2B = SBuf(nc, "t2b", [128, 16 * 2048], BF16, o_WAR)
    MPB = [SBuf(nc, f"mpb_{i}", [128, 2048], BF16, o_p + 65536 + i * 4096)
           for i in range(2)]

    P = [nc.alloc_psum_tensor(f"ps{i}", [128, 512], F32) for i in range(8)]
    PBF16 = P[7].bitcast(BF16)

    def piv(b, a=0, e=512):
        return ("psum", b * 2048 + a * 4, b * 2048 + e * 4)

    nc.const_aps.register(F32, EPS_LN, CONSTS[:, 0:1])
    nc.const_aps.register(F32, -0.5, CONSTS[:, 1:2])
    nc.const_aps.register(F32, 1.000001, CONSTS[:, 2:3])
    nc.const_aps.register(F32, 1e-11, CONSTS[:, 3:4])
    nc.const_aps.register(F32, 0.5, CONSTS[:, 4:5])
    civ = CONSTS.iv()

    pl = Plan()
    rot_state = [0]

    def rot():
        rot_state[0] = (rot_state[0] + 1) % 4
        return 3 + rot_state[0]

    def DRAM1(name):
        return (name, 0, 1)

    def dma(queue, sem, out_ap, in_ap, r, w, tag=""):
        return pl.emit(queue, lambda e, o=out_ap, i=in_ap: e.dma_start(out=o, in_=i),
                       r=r, w=w, dma=sem, tag=tag)

    for col, val in [(0, EPS_LN), (1, -0.5), (2, 1.000001), (3, 1e-11),
                     (4, 0.5)]:
        pl.emit("pool", lambda e, c=col, v=val: e.memset(CONSTS[:, c:c + 1], v),
                w=[CONSTS.iv(col, col + 1)])
    pl.emit("pool", lambda e: e.memset(ONESC[:, :], 1.0 / 1024.0), w=[ONESC.iv()])
    pl.emit("pool", lambda e: e.memset(ONES128[:, :], 1.0), w=[ONES128.iv()])
    pl.emit("pool", lambda e: e.memset(ONESROW[:, :], 1.0), w=[ONESROW.iv()])

    for dst, src in [(IDENT, ident_dr), (NORMW, normw), (INB, inb), (OUTB, outb),
                     (MUR, mur), (MUK, muk), (DWAT, dwataps)]:
        dma("sync", "d_ld", dst[:, :], src[:, :], r=[], w=[dst.iv()])
    dma("pool", "d_ldp", BIAS[32:33, :], symbrow[:, :], r=[], w=[BIAS.iv()])
    for dst, src in [(SYMT, symtaps),
                     (DWABBF, dwab), (HABBF, hab), (HBBBF, hbb), (PWBBF, pwb),
                     (FBROW, fbrow)]:
        dma("pool", "d_ldp", dst[:, :], src[:, :], r=[], w=[dst.iv()])
    dma("pool", "d_ldp", ABF.rearrange("p (c s) -> p c s", s=16),
        abuf.rearrange("(c p) s -> p c s", p=128), r=[], w=[ABF.iv()])
    dma("pool", "d_ldp", BMTB.rearrange("p (c s) -> p c s", s=16),
        bmT.rearrange("(c p) s -> p c s", p=128), r=[], w=[BMTB.iv()])

    pl.emit("act", lambda e: e.activation(
        out=CONST1[:, :], in_=NORMW[:, :], func=AF.Ln),
        r=[NORMW.iv(), civ], w=[CONST1.iv()])
    pl.emit("act", lambda e: e.activation(
        out=INVNW[:, :], in_=CONST1[:, :], func=AF.Exp, scale=-1.0),
        r=[CONST1.iv(), civ], w=[INVNW.iv()])

    for k in range(16):
        p = k % 2
        dma("pool", f"d_ha{p}", HAB_[:, p * 1024:(p + 1) * 1024],
            hawT[128 * k:128 * (k + 1), :],
            r=[], w=[HAB_.iv(p * 1024, (p + 1) * 1024)])
        for n in range(2):
            pl.emit("pe", lambda e, k=k, p=p, n=n: e.matmul(
                P[n][0:16, 0:512], BMTB[:, 16 * k:16 * (k + 1)],
                HAB_[:, p * 1024 + 512 * n: p * 1024 + 512 * (n + 1)],
                start=(k == 0), stop=(k == 15)),
                r=[HAB_.iv(p * 1024, (p + 1) * 1024), BMTB.iv()], w=[piv(n)])
    dma("pool", "d_fuse", FUSEB.rearrange("p (c q) -> p c q", q=2048),
        fuseT.rearrange("(c p) q -> p c q", p=128), r=[], w=[FUSEB.iv()])
    dma("pool", "d_hbw", HBWB.rearrange("p (c q) -> p c q", q=2048),
        hbw.rearrange("(c p) q -> p c q", p=128), r=[], w=[HBWB.iv()])

    for n in range(2):
        pl.emit("act", lambda e, n=n: e.copy(
            out=T1[0:16, 512 * n:512 * (n + 1)], in_=P[n][0:16, 0:512]),
            r=[piv(n)], w=[T1.iv(512 * n, 512 * (n + 1))])
    for g in range(8):
        pl.emit("pe", lambda e, g=g: e.transpose(
            PBF16[0:128, 0:16], T1[0:16, 128 * g:128 * (g + 1)],
            IDENT[0:16, 0:16]),
            r=[T1.iv(), IDENT.iv()], w=[piv(7, 0, 64)])
        pl.emit("dve", lambda e, g=g: e.tensor_copy(
            out=T1T[:, 16 * g:16 * (g + 1)], in_=PBF16[0:128, 0:16]),
            r=[piv(7, 0, 64)], w=[T1T.iv(16 * g, 16 * (g + 1))])
    for n in range(4):
        for k in range(8):
            pl.emit("pe", lambda e, n=n, k=k: e.matmul(
                P[n][0:16, 0:512], T1T[:, 16 * k:16 * (k + 1)],
                FUSEB[:, 2048 * k + 512 * n: 2048 * k + 512 * (n + 1)],
                start=(k == 0), stop=(k == 7)),
                r=[T1T.iv(), FUSEB.iv()], w=[piv(n)])
        pl.emit("dve", lambda e, n=n: e.tensor_copy(
            out=G[0:16, 512 * n:512 * (n + 1)], in_=P[n][0:16, 0:512]),
            r=[piv(n)], w=[G.iv(512 * n, 512 * (n + 1))])

    for k in range(8):
        for n in range(4):
            pl.emit("pe", lambda e, n=n, k=k: e.matmul(
                P[4 + n][0:1, 0:512], HABBF[:, k:k + 1],
                FUSEB[:, 2048 * k + 512 * n: 2048 * k + 512 * (n + 1)],
                start=(k == 0), stop=False),
                r=[HABBF.iv(), FUSEB.iv()], w=[piv(4 + n, 0, 512)])
    for k in range(8):
        for n in range(4):
            pl.emit("pe", lambda e, n=n, k=k: e.matmul(
                P[4 + n][0:1, 0:512], HBBBF[:, k:k + 1],
                FUSEB[:, 2048 * (8 + k) + 512 * n: 2048 * (8 + k) + 512 * (n + 1)],
                start=False, stop=False),
                r=[HBBBF.iv(), FUSEB.iv()], w=[piv(4 + n, 0, 512)])
    for n in range(4):
        pl.emit("pe", lambda e, n=n: e.matmul(
            P[4 + n][0:1, 0:512], ONES128[0:1, 0:1],
            FBROW[0:1, 512 * n:512 * (n + 1)], start=False, stop=False),
            r=[ONES128.iv(), FBROW.iv()], w=[piv(4 + n, 0, 512)])

    for c in range(16):
        for n in range(4):
            bank = (c * 4 + n) % 2
            for d in range(8):
                pl.emit("pe", lambda e, c=c, n=n, d=d, bank=bank: e.matmul(
                    P[bank][0:128, 0:512],
                    HBWB[:, 2048 * d + 128 * c: 2048 * d + 128 * (c + 1)],
                    FUSEB[:, 2048 * (8 + d) + 512 * n: 2048 * (8 + d) + 512 * (n + 1)],
                    start=(d == 0), stop=(d == 7)),
                    r=[HBWB.iv(), FUSEB.iv()], w=[piv(bank)])
            pl.emit("act", lambda e, c=c, n=n, bank=bank: e.copy(
                out=T2B[:, 2048 * c + 512 * n: 2048 * c + 512 * (n + 1)],
                in_=P[bank][0:128, 0:512]),
                r=[piv(bank)],
                w=[T2B.iv(2048 * c + 512 * n, 2048 * c + 512 * (n + 1))])

    if dbg:
        pl.emit("dve", lambda e: e.tensor_copy(
            out=TF[:, 0:4096], in_=T2B[0:128, 0:4096]),
            r=[T2B.iv(0, 4096)], w=[TF.iv()])
        dma("sync", "d_o", dbt["t2"][:, :], TF[:, 0:4096],
            r=[TF.iv()], w=[DRAM1("dbg_t2")])

    if dbg:
        dma("pool", "d_dbg", dbt["fuseb"][:, :], FUSEB[:, 24576:28672],
            r=[FUSEB.iv(24576, 28672)], w=[DRAM1("dbg_fuseb")])

    for cc in range(16):
        for n in range(4):
            pl.emit("pe", lambda e, cc=cc, n=n: e.matmul(
                P[4 + n][0:1, 0:512], PWBBF[:, cc:cc + 1],
                T2B[:, 2048 * cc + 512 * n: 2048 * cc + 512 * (n + 1)],
                start=False, stop=False),
                r=[PWBBF.iv(), T2B.iv()], w=[piv(4 + n, 0, 512)])

    dma("pool", "d_pww", PWWB.rearrange("p (c q) -> p c q", q=2048),
        pww.rearrange("(c p) q -> p c q", p=128), r=[], w=[PWWB.iv()])

    for pc in range(16):
        mp = pc % 2
        for n in range(4):
            bank = (pc * 4 + n) % 2
            for cc in range(16):
                pl.emit("pe", lambda e, pc=pc, n=n, cc=cc, bank=bank: e.matmul(
                    P[bank][0:128, 0:512],
                    PWWB[:, 2048 * cc + 128 * pc: 2048 * cc + 128 * (pc + 1)],
                    T2B[:, 2048 * cc + 512 * n: 2048 * cc + 512 * (n + 1)],
                    start=(cc == 0), stop=(cc == 15)),
                    r=[PWWB.iv(), T2B.iv()], w=[piv(bank)])
            pl.emit("act", lambda e, n=n, mp=mp, bank=bank: e.copy(
                out=MPB[mp][:, 512 * n:512 * (n + 1)], in_=P[bank][0:128, 0:512]),
                r=[piv(bank)], w=[MPB[mp].iv(512 * n, 512 * (n + 1))])
        dma("act", f"d_sm{mp}",
            m_scr.rearrange("e p (c q) -> p e c q", q=128)[:, :, pc, :],
            MPB[mp].rearrange("p (e q) -> p e q", q=128),
            r=[MPB[mp].iv()], w=[DRAM1("m_scr")])

    for k in range(16):
        pl.emit("pe", lambda e, k=k: e.matmul(
            P[2][0:16, 0:1], ABF[:, 16 * k:16 * (k + 1)], DWABBF[:, k:k + 1],
            start=(k == 0), stop=(k == 15)),
            r=[ABF.iv(), DWABBF.iv()], w=[piv(2, 0, 1)])
    pl.emit("dve", lambda e: e.tensor_copy(out=VA[0:16, 0:1], in_=P[2][0:16, 0:1]),
            r=[piv(2, 0, 1)], w=[VA.iv()])
    for n in range(4):
        pl.emit("pe", lambda e, n=n: e.matmul(
            P[4 + n][0:1, 0:512], VA[0:16, 0:1],
            G[0:16, 512 * n:512 * (n + 1)], start=False, stop=True),
            r=[VA.iv(), G.iv()], w=[piv(4 + n, 0, 512)])
    for n in range(4):
        pl.emit("act", lambda e, n=n: e.copy(
            out=BIAS[0:1, 512 * n:512 * (n + 1)], in_=P[4 + n][0:1, 0:512]),
            r=[piv(4 + n, 0, 512)], w=[BIAS.iv(512 * n, 512 * (n + 1))])

    for c in range(16):
        for j in range(4):
            pl.emit("dve", lambda e, c=c, j=j: e.tensor_scalar(
                out=AJ[:, (c * 4 + j) * 16:(c * 4 + j + 1) * 16],
                in0=ABF[:, 16 * c:16 * (c + 1)],
                scalar1=DWAT[:, c * 4 + j:c * 4 + j + 1], scalar2=None,
                op0=ALU.mult),
                r=[ABF.iv(), DWAT.iv()],
                w=[AJ.iv((c * 4 + j) * 16, (c * 4 + j + 1) * 16)])

    for c in range(16):
        dp = c % 2
        for j in range(9):
            pl.emit("pool", lambda e, c=c, j=j, dp=dp: e.tensor_tensor(
                out=DGB[:, dp * 1152 + 128 * j: dp * 1152 + 128 * (j + 1)],
                in0=IDENT[:, :],
                in1=SYMT[:, c * 9 + j:c * 9 + j + 1].broadcast_to([128, 128]),
                op=ALU.mult),
                r=[IDENT.iv(), SYMT.iv()],
                w=[DGB.iv(dp * 1152 + 128 * j, dp * 1152 + 128 * (j + 1))])
        dma("sync", f"d_sd{dp}", dg_scr[c],
            DGB[:, dp * 1152:(dp + 1) * 1152],
            r=[DGB.iv(dp * 1152, (dp + 1) * 1152)], w=[DRAM1("dg_scr")])

    dma("pool", "d_siw", inw_scr.rearrange("o p q -> (o p) q"), inwT[:, :],
        r=[], w=[DRAM1("inw_scr")])
    dma("pool", "d_sf", wk_scr[:, :], wkT[:, :], r=[], w=[DRAM1("wk_scr")])
    dma("pool", "d_sf", wr_scr[:, :], wrT[:, :], r=[], w=[DRAM1("wr_scr")])
    dma("pool", "d_siw", ow_scr.rearrange("d p q -> (d p) q"), outwT[:, :],
        r=[], w=[DRAM1("ow_scr")])
    dma("pool", "d_sf", wv_scr[:, :], wvT[:, :], r=[], w=[DRAM1("wv_scr")])

    pl.emit("pool", lambda e: e.memset(
        WARN.rearrange("p (c t) -> p c t", t=CH)[:, :, 0:1], 0.0),
        w=[WARN.iv()])

    slab = {"inw": 0, "m": 0, "ow": 0, "fb": 0, "dg": 0, "tmps": 0}
    dg_base = [0]

    def norm_stats(src, width, do_ln1, qb=None, qs=None, mb_=2, sb_=7,
                   stage=0):
        qb = qb if qb is not None else SQB
        qs = qs if qs is not None else SQS
        w8 = width
        for h in (() if stage == 2 else range(2)):
            pl.emit("dve", lambda e, h=h: e.tensor_copy(
                out=qb.rearrange("p (c t) -> p c t", t=CH)[:, 4 * h:4 * h + 4, 0:w8],
                in_=src.rearrange("p (c t) -> p c t", t=CH)[:, 4 * h:4 * h + 4, 0:w8]),
                r=[src.iv(4 * h * CH, (4 * h + 4) * CH)],
                w=[qb.iv(4 * h * CH, (4 * h + 4) * CH)])
            pl.emit("act", lambda e, h=h: e.activation(
                out=qs.rearrange("p (c t) -> p c t", t=CH)[:, 4 * h:4 * h + 4, 0:w8],
                in_=src.rearrange("p (c t) -> p c t", t=CH)[:, 4 * h:4 * h + 4, 0:w8],
                func=AF.Square),
                r=[src.iv(4 * h * CH, (4 * h + 4) * CH)],
                w=[qs.iv(4 * h * CH, (4 * h + 4) * CH)])
        if stage == 1:
            return
        for half in range(2):
            for c in range(4 * half, 4 * half + 4):
                pl.emit("pe", lambda e, c=c: e.matmul(
                    P[0][0:1, 0:w8], ONESC[:, :], qb[:, c * CH:c * CH + w8],
                    start=(c == 0), stop=(c == NCH - 1)),
                    r=[ONESC.iv(), qb.iv(c * CH, c * CH + w8)], w=[piv(0, 0, w8)])
            for c in range(4 * half, 4 * half + 4):
                pl.emit("pe", lambda e, c=c: e.matmul(
                    P[1][0:1, 0:w8], ONESC[:, :], qs[:, c * CH:c * CH + w8],
                    start=(c == 0), stop=(c == NCH - 1)),
                    r=[ONESC.iv(), qs.iv(c * CH, c * CH + w8)], w=[piv(1, 0, w8)])
        pl.emit("dve", lambda e: e.tensor_copy(out=MROW[0:1, 0:w8],
                                               in_=P[0][0:1, 0:w8]),
                r=[piv(0, 0, w8)], w=[MROW.iv(0, w8)])
        pl.emit("dve", lambda e: e.tensor_tensor(
            out=MSQR[0:1, 0:w8], in0=MROW[0:1, 0:w8], in1=MROW[0:1, 0:w8],
            op=ALU.mult),
            r=[MROW.iv(0, w8)], w=[MSQR.iv(0, w8)])
        pl.emit("dve", lambda e: e.tensor_tensor(
            out=VROW[0:1, 0:w8], in0=P[1][0:1, 0:w8], in1=MSQR[0:1, 0:w8],
            op=ALU.subtract), r=[piv(1, 0, w8), MSQR.iv(0, w8)],
            w=[VROW.iv(0, w8)])
        pl.emit("act", lambda e: e.activation(
            out=LROW[0:1, 0:w8], in_=VROW[0:1, 0:w8], func=AF.Ln, bias=EPS_LN),
            r=[VROW.iv(0, w8), civ], w=[LROW.iv(0, w8)])
        pl.emit("act", lambda e: e.activation(
            out=SROW1[0:1, 0:w8], in_=LROW[0:1, 0:w8], func=AF.Exp, scale=-0.5),
            r=[LROW.iv(0, w8), civ], w=[SROW1.iv(0, w8)])
        if do_ln1:
            pl.emit("act", lambda e: e.activation(
                out=LROW[0:1, 0:w8], in_=VROW[0:1, 0:w8], func=AF.Ln,
                bias=1e-11, scale=1.000001),
                r=[VROW.iv(0, w8), civ], w=[LROW.iv(0, w8)])
            pl.emit("act", lambda e: e.activation(
                out=SROW2[0:1, 0:w8], in_=LROW[0:1, 0:w8], func=AF.Exp,
                scale=-0.5),
                r=[LROW.iv(0, w8), civ], w=[SROW2.iv(0, w8)])
        pl.emit("pe", lambda e: e.matmul(
            P[mb_][0:128, 0:w8], ONES128[:, :], MROW[0:1, 0:w8],
            start=True, stop=True),
            r=[ONES128.iv(), MROW.iv(0, w8)], w=[piv(mb_, 0, w8)])
        pl.emit("pe", lambda e: e.matmul(
            P[sb_][0:128, 0:w8], ONES128[:, :], SROW1[0:1, 0:w8],
            start=True, stop=True),
            r=[ONES128.iv(), SROW1.iv(0, w8)], w=[piv(sb_, 0, w8)])
        if do_ln1:
            pl.emit("pe", lambda e: e.matmul(
                P[0][0:128, 0:w8], ONES128[:, :], SROW2[0:1, 0:w8],
                start=True, stop=True),
                r=[ONES128.iv(), SROW2.iv(0, w8)], w=[piv(0, 0, w8)])

    def tile_geom(ti):
        s = ti * WC
        Wc = min(WC, L - s)
        Ww = Wc + 8
        w0 = s - 4
        vlo = max(0, w0) - w0
        vhi = min(L, w0 + Ww) - w0
        return s, Wc, Ww, w0, vlo, vhi

    def emit_xload(ti):
        s, Wc, Ww, w0, vlo, vhi = tile_geom(ti)
        dma("sync", "d_xw",
            XW.rearrange("p (c t) -> p c t", t=CH)[:, :, vlo:vhi],
            xdr.rearrange("(c p) t -> p c t", p=128)[:, :, w0 + vlo:w0 + vhi],
            r=[], w=[XW.iv()], tag=f"x{ti}")
        if vlo > 0:
            pl.emit("pool", lambda e, vlo=vlo: e.memset(
                XW.rearrange("p (c t) -> p c t", t=CH)[:, :, 0:vlo], 0.0),
                w=[XW.iv()])
        if vhi < Ww:
            pl.emit("pool", lambda e, vhi=vhi, Ww=Ww: e.memset(
                XW.rearrange("p (c t) -> p c t", t=CH)[:, :, vhi:Ww], 0.0),
                w=[XW.iv()])

    def emit_early_cast(tj, qb=None, qs=None):
        qb = qb if qb is not None else SQB
        qs = qs if qs is not None else SQS
        s, Wc, Ww, w0, vlo, vhi = tile_geom(tj)
        for h in range(2):
            pl.emit("dve", lambda e, h=h, Ww=Ww: e.tensor_copy(
                out=qb.rearrange("p (c t) -> p c t", t=CH)[:, 4 * h:4 * h + 4, 0:Ww],
                in_=XW.rearrange("p (c t) -> p c t", t=CH)[:, 4 * h:4 * h + 4, 0:Ww]),
                r=[XW.iv(4 * h * CH, (4 * h + 4) * CH)],
                w=[qb.iv(4 * h * CH, (4 * h + 4) * CH)])
            pl.emit("act", lambda e, h=h, Ww=Ww: e.activation(
                out=qs.rearrange("p (c t) -> p c t", t=CH)[:, 4 * h:4 * h + 4, 0:Ww],
                in_=XW.rearrange("p (c t) -> p c t", t=CH)[:, 4 * h:4 * h + 4, 0:Ww],
                func=AF.Square),
                r=[XW.iv(4 * h * CH, (4 * h + 4) * CH)],
                w=[qs.iv(4 * h * CH, (4 * h + 4) * CH)])

    def emit_early(tj, qb=None, qs=None):
        qb = qb if qb is not None else SQB
        qs = qs if qs is not None else SQS
        s, Wc, Ww, w0, vlo, vhi = tile_geom(tj)
        for c in range(NCH):
            pl.emit("pe", lambda e, c=c, Ww=Ww: e.matmul(
                P[0][0:1, 0:Ww], ONESC[:, :], qb[:, c * CH:c * CH + Ww],
                start=(c == 0), stop=(c == NCH - 1)),
                r=[ONESC.iv(), qb.iv(c * CH, c * CH + Ww)], w=[piv(0, 0, Ww)])
        for c in range(NCH):
            pl.emit("pe", lambda e, c=c, Ww=Ww: e.matmul(
                P[1][0:1, 0:Ww], ONESC[:, :], qs[:, c * CH:c * CH + Ww],
                start=(c == 0), stop=(c == NCH - 1)),
                r=[ONESC.iv(), qs.iv(c * CH, c * CH + Ww)], w=[piv(1, 0, Ww)])
        pl.emit("act", lambda e, Ww=Ww: e.copy(out=MROW[0:1, 0:Ww],
                                               in_=P[0][0:1, 0:Ww]),
                r=[piv(0, 0, Ww)], w=[MROW.iv(0, Ww)])
        pl.emit("act", lambda e, Ww=Ww: e.activation(
            out=MSQR[0:1, 0:Ww], in_=MROW[0:1, 0:Ww], func=AF.Square),
            r=[MROW.iv(0, Ww)], w=[MSQR.iv(0, Ww)])
        pl.emit("dve", lambda e, Ww=Ww: e.tensor_tensor(
            out=VROW[0:1, 0:Ww], in0=P[1][0:1, 0:Ww], in1=MSQR[0:1, 0:Ww],
            op=ALU.subtract), r=[piv(1, 0, Ww), MSQR.iv(0, Ww)],
            w=[VROW.iv(0, Ww)])
        pl.emit("act", lambda e, Ww=Ww: e.activation(
            out=LROW[0:1, 0:Ww], in_=VROW[0:1, 0:Ww], func=AF.Ln, bias=EPS_LN),
            r=[VROW.iv(0, Ww), civ], w=[LROW.iv(0, Ww)])
        pl.emit("act", lambda e, Ww=Ww: e.activation(
            out=LROW2[0:1, 0:Ww], in_=VROW[0:1, 0:Ww], func=AF.Ln,
            bias=1e-11, scale=1.000001),
            r=[VROW.iv(0, Ww), civ], w=[LROW2.iv(0, Ww)])
        pl.emit("act", lambda e, Ww=Ww: e.activation(
            out=SROW2[0:1, 0:Ww], in_=LROW2[0:1, 0:Ww], func=AF.Exp,
            scale=-0.5),
            r=[LROW2.iv(0, Ww), civ], w=[SROW2.iv(0, Ww)])
        pl.emit("dve", lambda e, Ww=Ww: e.tensor_tensor(
            out=VROW[0:1, 0:Ww], in0=LROW2[0:1, 0:Ww], in1=LROW[0:1, 0:Ww],
            op=ALU.subtract),
            r=[LROW2.iv(0, Ww), LROW.iv(0, Ww)], w=[VROW.iv(0, Ww)])
        pl.emit("act", lambda e, Ww=Ww: e.activation(
            out=SROW3[0:1, 0:Ww], in_=VROW[0:1, 0:Ww], func=AF.Exp, scale=0.5),
            r=[VROW.iv(0, Ww), civ], w=[SROW3.iv(0, Ww)])
        pl.emit("pe", lambda e, Ww=Ww: e.matmul(
            P[2][0:128, 0:Ww], ONES128[:, :], MROW[0:1, 0:Ww],
            start=True, stop=True),
            r=[ONES128.iv(), MROW.iv(0, Ww)], w=[piv(2, 0, Ww)])
        pl.emit("pe", lambda e, Ww=Ww: e.matmul(
            P[0][0:128, 0:Ww], ONES128[:, :], SROW2[0:1, 0:Ww],
            start=True, stop=True),
            r=[ONES128.iv(), SROW2.iv(0, Ww)], w=[piv(0, 0, Ww)])
        for c in range(NCH):
            pl.emit("dve", lambda e, c=c, Ww=Ww: e.tensor_tensor(
                out=TF[:, c * CH:c * CH + Ww], in0=XW[:, c * CH:c * CH + Ww],
                in1=P[2][0:128, 0:Ww], op=ALU.subtract),
                r=[XW.iv(c * CH, c * CH + Ww), piv(2, 0, Ww)],
                w=[TF.iv(c * CH, c * CH + Ww)])
        for c in range(NCH):
            pl.emit("dve", lambda e, c=c, Ww=Ww: e.scalar_tensor_tensor(
                out=ZW[:, c * CH:c * CH + Ww], in0=TF[:, c * CH:c * CH + Ww],
                scalar=NORMW[:, c:c + 1], in1=P[0][0:128, 0:Ww],
                op0=ALU.mult, op1=ALU.mult),
                r=[TF.iv(c * CH, c * CH + Ww), NORMW.iv(), piv(0, 0, Ww)],
                w=[ZW.iv(c * CH, c * CH + Ww)])
        if dbg and tj == DBG_TI:
            dma("pool", "d_dbg", dbt["erows"][0:1, :], MROW[0:1, :],
                r=[MROW.iv()], w=[DRAM1("dbg_erows")])
            dma("pool", "d_dbg", dbt["erows"][1:2, :], SROW2[0:1, :],
                r=[SROW2.iv()], w=[DRAM1("dbg_erows")])
            dma("pool", "d_dbg", dbt["erows"][2:3, :], SROW3[0:1, :],
                r=[SROW3.iv()], w=[DRAM1("dbg_erows")])
            dma("pool", "d_dbg", dbt["ezw"][:, :], ZW[:, :],
                r=[ZW.iv()], w=[DRAM1("dbg_ezw")])

    def emit_ln1x(ti):
        s, Wc, Ww, w0, vlo, vhi = tile_geom(ti)
        pl.emit("pe", lambda e, Ww=Ww: e.matmul(
            P[1][0:128, 0:Ww], ONES128[:, :], SROW3[0:1, 0:Ww],
            start=True, stop=True),
            r=[ONES128.iv(), SROW3.iv(0, Ww)], w=[piv(1, 0, Ww)])
        for c in range(NCH):
            pl.emit("dve", lambda e, c=c, Wc=Wc: e.scalar_tensor_tensor(
                out=LN1X[:, c * CH:c * CH + Wc],
                in0=ZW[:, c * CH + 4:c * CH + 4 + Wc],
                scalar=INVNW[:, c:c + 1], in1=P[1][0:128, 4:4 + Wc],
                op0=ALU.mult, op1=ALU.mult),
                r=[ZW.iv(c * CH, c * CH + CH), INVNW.iv(), piv(1)],
                w=[LN1X.iv(c * CH, c * CH + Wc)])

    def emit_s3_part(ti, part):
        s, Wc, Ww, w0, vlo, vhi = tile_geom(ti)
        if part == 0:
            emit_ln1x(ti)

        oc_order = list(range(16, 32)) + list(range(0, 16))
        for pi in (range(8) if part == 0 else range(8, 16)):
            oc0 = oc_order[2 * pi]
            par = slab["inw"] % 3
            slab["inw"] += 1
            ibuf = INWB if par < 2 else INWB3
            ibase = par * 2048 if par < 2 else 0
            dma("sync", f"d_inw{par}",
                ibuf[:, ibase:ibase + 2048],
                inw_scr[oc0 // 2],
                r=[DRAM1("inw_scr")], w=[ibuf.iv(ibase, ibase + 2048)])
            for sub in range(2):
                oc = oc0 + sub
                bank = rot()
                for k in range(NCH):
                    pl.emit("pe", lambda e, ibuf=ibuf, ibase=ibase, sub=sub,
                            k=k, bank=bank, Ww=Ww: e.matmul(
                                P[bank][0:128, 0:Ww],
                                ibuf[:, ibase + k * 256 + sub * 128:
                                     ibase + k * 256 + sub * 128 + 128],
                                ZW[:, k * CH:k * CH + Ww],
                                start=(k == 0), stop=(k == NCH - 1)),
                            r=[ibuf.iv(ibase, ibase + 2048),
                               ZW.iv(k * CH, k * CH + Ww)],
                            w=[piv(bank, 0, Ww)])
                if oc >= 16:
                    cc = oc - 16
                    pl.emit("act", lambda e, cc=cc, oc=oc, bank=bank, Ww=Ww:
                            e.activation(out=VAL[:, cc * CH:cc * CH + Ww],
                                         in_=P[bank][0:128, 0:Ww], func=AF.Identity,
                                         bias=INB[:, oc:oc + 1]),
                            r=[piv(bank, 0, Ww), INB.iv()],
                            w=[VAL.iv(cc * CH, cc * CH + Ww)])
                else:
                    pl.emit("act", lambda e, oc=oc, bank=bank, Wc=Wc:
                            e.activation(out=GATE[:, oc * CH:oc * CH + Wc],
                                         in_=P[bank][0:128, 4:4 + Wc],
                                         func=AF.Sigmoid, bias=INB[:, oc:oc + 1]),
                            r=[piv(bank, 0, CH), INB.iv()],
                            w=[GATE.iv(oc * CH, oc * CH + Wc)])
        if part == 0:
            if vlo > 0:
                pl.emit("pool", lambda e, vlo=vlo: e.memset(
                    VAL.rearrange("p (c t) -> p c t", t=CH)[:, :, 0:vlo], 0.0),
                    w=[VAL.iv()])
            if vhi < Ww:
                pl.emit("pool", lambda e, vhi=vhi, Ww=Ww: e.memset(
                    VAL.rearrange("p (c t) -> p c t", t=CH)[:, :, vhi:Ww], 0.0),
                    w=[VAL.iv()])
        else:
            dg_base[0] = slab["dg"]
            for c in range(3):
                par = slab["dg"] % 3
                slab["dg"] += 1
                dma("sync", f"d_dg{par}",
                    DGB[:, par * 1152:(par + 1) * 1152], dg_scr[c],
                    r=[DRAM1("dg_scr")], w=[DGB.iv(par * 1152, (par + 1) * 1152)])

    emit_xload(0)
    emit_early_cast(0, SQB2, SQS2)
    emit_early(0, SQB2, SQS2)
    emit_s3_part(0, 0)
    emit_s3_part(0, 1)
    prev_wc = WC
    for ti in range(n_tiles):
        s, Wc, Ww, w0, vlo, vhi = tile_geom(ti)

        namm = [0]
        for c in range(16):
            for j in range(4):
                namm[0] += 1
                first = namm[0] == 1
                last = namm[0] == 64
                d = j - 1
                pl.emit("pe", lambda e, c=c, j=j, d=d, first=first, last=last,
                        Wc=Wc: e.matmul(
                            P[7][0:16, 0:Wc],
                            AJ[:, (c * 4 + j) * 16:(c * 4 + j + 1) * 16],
                            VAL[:, c * CH + 4 + d:c * CH + 4 + d + Wc],
                            start=first, stop=last),
                        r=[AJ.iv(), VAL.iv(c * CH, c * CH + CH)],
                        w=[piv(7, 0, Wc)])
        pl.emit("dve", lambda e, Wc=Wc: e.tensor_copy(
            out=ALOW[0:16, 0:Wc], in_=P[7][0:16, 0:Wc]),
            r=[piv(7, 0, Wc)], w=[ALOW.iv(0, Wc)])

        for c in range(16):
            par = (dg_base[0] + c) % 3
            if c >= 3:
                slab["dg"] += 1
                dma("sync", f"d_dg{par}",
                    DGB[:, par * 1152:(par + 1) * 1152], dg_scr[c],
                    r=[DRAM1("dg_scr")], w=[DGB.iv(par * 1152, (par + 1) * 1152)])
            bank = rot()
            for j in range(9):
                d = j - 4
                pl.emit("pe", lambda e, c=c, j=j, d=d, par=par, bank=bank,
                        Wc=Wc: e.matmul(
                            P[bank][0:128, 0:Wc],
                            DGB[:, par * 1152 + 128 * j:par * 1152 + 128 * (j + 1)],
                            VAL[:, c * CH + 4 + d:c * CH + 4 + d + Wc],
                            start=(j == 0), stop=False),
                        r=[DGB.iv(par * 1152, (par + 1) * 1152),
                           VAL.iv(c * CH, c * CH + CH)],
                        w=[piv(bank, 0, Wc)])
            pl.emit("pe", lambda e, c=c, bank=bank, Wc=Wc: e.matmul(
                P[bank][0:128, 0:Wc], BIAS[32:33, 128 * c:128 * (c + 1)],
                ONESROW[32:33, 0:Wc], start=False, stop=True),
                r=[BIAS.iv(), ONESROW.iv()], w=[piv(bank, 0, Wc)])
            pl.emit("act", lambda e, bank=bank, Wc=Wc: e.activation(
                out=TMPS[:, 0:Wc], in_=P[bank][0:128, 0:Wc],
                func=AF.Sigmoid),
                r=[piv(bank, 0, Wc)], w=[TMPS.iv(0, Wc)])
            pl.emit("dve", lambda e, c=c, bank=bank, Wc=Wc: e.tensor_tensor(
                out=SILU[:, c * CH:c * CH + Wc], in0=P[bank][0:128, 0:Wc],
                in1=TMPS[:, 0:Wc], op=ALU.mult),
                r=[piv(bank, 0, Wc), TMPS.iv(0, Wc)],
                w=[SILU.iv(c * CH, c * CH + Wc)])

        if dbg and ti == DBG_TI:
            for nm, buf in [("val", VAL), ("gate", GATE), ("silu", SILU),
                            ("alow", ALOW), ("zw", ZW), ("ln1x", LN1X),
                            ("g", G), ("bias", BIAS)]:
                rr = buf.t.shape[0]
                if nm == "bias":
                    rr = 1
                cc = min(buf.ncols, 4096)
                pl.emit("dve", lambda e, buf=buf, rr=rr, cc=cc: e.tensor_copy(
                    out=TF[0:rr, 0:cc], in_=buf[0:rr, 0:cc]),
                    r=[buf.iv()], w=[TF.iv()], tag="dbgc" + nm)
                dma("sync", "d_o", dbt[nm][:, :], TF[0:rr, 0:cc],
                    r=[TF.iv()], w=[DRAM1("dbg_" + nm)])
        for ec in range(16):
            par = slab["m"] % 3
            slab["m"] += 1
            dma("sync", f"d_mb{par}",
                MB[:, par * 2048:(par + 1) * 2048],
                m_scr[ec],
                r=[DRAM1("m_scr")], w=[MB.iv(par * 2048, (par + 1) * 2048)])
            bank = rot()
            pl.emit("pe", lambda e, ec=ec, bank=bank, Wc=Wc: e.matmul(
                P[bank][0:128, 0:Wc], G[0:16, 128 * ec:128 * (ec + 1)],
                ALOW[0:16, 0:Wc], start=True, stop=False),
                r=[G.iv(), ALOW.iv(0, Wc)], w=[piv(bank, 0, Wc)])
            for pc in range(16):
                pl.emit("pe", lambda e, pc=pc, par=par, bank=bank, Wc=Wc:
                        e.matmul(
                            P[bank][0:128, 0:Wc],
                            MB[:, par * 2048 + pc * 128:
                               par * 2048 + pc * 128 + 128],
                            SILU[:, pc * CH:pc * CH + Wc],
                            start=False, stop=False),
                        r=[MB.iv(par * 2048, (par + 1) * 2048),
                           SILU.iv(pc * CH, pc * CH + Wc)],
                        w=[piv(bank, 0, Wc)])
            pl.emit("pe", lambda e, ec=ec, bank=bank, Wc=Wc: e.matmul(
                P[bank][0:128, 0:Wc], BIAS[0:1, 128 * ec:128 * (ec + 1)],
                ONESROW[0:1, 0:Wc], start=False, stop=True),
                r=[BIAS.iv(), ONESROW.iv()], w=[piv(bank, 0, Wc)])
            pl.emit("dve", lambda e, ec=ec, bank=bank, Wc=Wc: e.tensor_tensor(
                out=YG[:, ec * CH:ec * CH + Wc], in0=P[bank][0:128, 0:Wc],
                in1=GATE[:, ec * CH:ec * CH + Wc], op=ALU.mult),
                r=[piv(bank, 0, Wc), GATE.iv(ec * CH, ec * CH + Wc)],
                w=[YG.iv(ec * CH, ec * CH + Wc)])

        for dc in range(8):
            par = slab["ow"] % 2
            slab["ow"] += 1
            dma("sync", f"d_ow{par}",
                OWB[:, par * 2048:(par + 1) * 2048],
                ow_scr[dc],
                r=[DRAM1("ow_scr")], w=[OWB.iv(par * 2048, (par + 1) * 2048)])
            bank = rot()
            for ec in range(16):
                pl.emit("pe", lambda e, ec=ec, par=par, bank=bank, Wc=Wc:
                        e.matmul(P[bank][0:128, 0:Wc],
                                 OWB[:, par * 2048 + 128 * ec:
                                     par * 2048 + 128 * (ec + 1)],
                                 YG[:, ec * CH:ec * CH + Wc],
                                 start=(ec == 0), stop=False),
                        r=[OWB.iv(par * 2048, (par + 1) * 2048),
                           YG.iv(ec * CH, ec * CH + Wc)],
                        w=[piv(bank, 0, Wc)])
            pl.emit("pe", lambda e, dc=dc, bank=bank, Wc=Wc: e.matmul(
                P[bank][0:128, 0:Wc], IDENT[:, :],
                LN1X[:, dc * CH:dc * CH + Wc], start=False, stop=True),
                r=[IDENT.iv(), LN1X.iv(dc * CH, dc * CH + Wc)],
                w=[piv(bank, 0, Wc)])
            pl.emit("act", lambda e, dc=dc, bank=bank, Wc=Wc: e.activation(
                out=ATTN[:, dc * CH:dc * CH + Wc], in_=P[bank][0:128, 0:Wc],
                func=AF.Identity, bias=OUTB[:, dc:dc + 1]),
                r=[piv(bank, 0, Wc), OUTB.iv()],
                w=[ATTN.iv(dc * CH, dc * CH + Wc)])

        if dbg and ti == DBG_TI:
            dma("sync", "d_o", dbt["attn"][:, :], ATTN[:, :],
                r=[ATTN.iv()], w=[DRAM1("dbg_attn")])
        norm_stats(ATTN, Wc, do_ln1=False)
        for c in range(NCH):
            pl.emit("dve", lambda e, c=c, Wc=Wc: e.tensor_tensor(
                out=TF[:, c * CH:c * CH + Wc], in0=ATTN[:, c * CH:c * CH + Wc],
                in1=P[2][0:128, 0:Wc], op=ALU.subtract),
                r=[ATTN.iv(c * CH, c * CH + Wc), piv(2, 0, Wc)],
                w=[TF.iv(c * CH, c * CH + Wc)])
            pl.emit("dve", lambda e, c=c, Wc=Wc: e.tensor_tensor(
                out=UA[:, c * CH:c * CH + Wc], in0=TF[:, c * CH:c * CH + Wc],
                in1=P[7][0:128, 0:Wc], op=ALU.mult),
                r=[TF.iv(c * CH, c * CH + Wc), piv(7, 0, Wc)],
                w=[UA.iv(c * CH, c * CH + Wc)])
            pl.emit("pool", lambda e, c=c, Wc=Wc: e.tensor_tensor(
                out=X1[:, c * CH:c * CH + Wc], in0=UA[:, c * CH:c * CH + Wc],
                in1=XW[:, c * CH + 4:c * CH + 4 + Wc], op=ALU.add),
                r=[UA.iv(c * CH, c * CH + Wc), XW.iv(c * CH, c * CH + CH)],
                w=[X1.iv(c * CH, c * CH + Wc)])

        if ti + 1 < n_tiles:
            emit_xload(ti + 1)
            emit_early_cast(ti + 1, SQB2, SQS2)

        if ti > 0:
            pl.emit("pool", lambda e: e.tensor_copy(
                out=WARN.rearrange("p (c t) -> p c t", t=CH)[:, :, 0:1],
                in_=WCARRY.rearrange("p (c o) -> p c o", o=1)),
                r=[WCARRY.iv()], w=[WARN.iv()])
        norm_stats(X1, Wc, do_ln1=False)

        def emit_mix(c, Wc):
            pl.emit("pool", lambda e, c=c, Wc=Wc: e.tensor_tensor(
                out=UA[:, c * CH:c * CH + Wc],
                in0=WARN[:, c * CH + 1:c * CH + 1 + Wc],
                in1=WARN[:, c * CH:c * CH + Wc], op=ALU.subtract),
                r=[WARN.iv(c * CH, c * CH + 1 + Wc)],
                w=[UA.iv(c * CH, c * CH + Wc)])

        def emit_xk(c, Wc):
            pl.emit("dve", lambda e, c=c, Wc=Wc: e.scalar_tensor_tensor(
                out=XK[:, c * CH:c * CH + Wc], in0=UA[:, c * CH:c * CH + Wc],
                scalar=MUK[:, c:c + 1], in1=WARN[:, c * CH:c * CH + Wc],
                op0=ALU.mult, op1=ALU.add),
                r=[UA.iv(c * CH, c * CH + Wc), MUK.iv(),
                   WARN.iv(c * CH, c * CH + Wc)],
                w=[XK.iv(c * CH, c * CH + Wc)])

        def emit_xr(c, Wc):
            pl.emit("dve", lambda e, c=c, Wc=Wc: e.scalar_tensor_tensor(
                out=XR[:, c * CH:c * CH + Wc], in0=UA[:, c * CH:c * CH + Wc],
                scalar=MUR[:, c:c + 1], in1=WARN[:, c * CH:c * CH + Wc],
                op0=ALU.mult, op1=ALU.add),
                r=[UA.iv(c * CH, c * CH + Wc), MUR.iv(),
                   WARN.iv(c * CH, c * CH + Wc)],
                w=[XR.iv(c * CH, c * CH + Wc)])

        for c in range(NCH):
            pl.emit("dve", lambda e, c=c, Wc=Wc: e.tensor_tensor(
                out=TF[:, c * CH:c * CH + Wc], in0=X1[:, c * CH:c * CH + Wc],
                in1=P[2][0:128, 0:Wc], op=ALU.subtract),
                r=[X1.iv(c * CH, c * CH + Wc), piv(2, 0, Wc)],
                w=[TF.iv(c * CH, c * CH + Wc)])
            pl.emit("dve", lambda e, c=c, Wc=Wc: e.tensor_tensor(
                out=WARN[:, c * CH + 1:c * CH + 1 + Wc],
                in0=TF[:, c * CH:c * CH + Wc],
                in1=P[7][0:128, 0:Wc], op=ALU.mult),
                r=[TF.iv(c * CH, c * CH + Wc), piv(7, 0, Wc)],
                w=[WARN.iv(c * CH + 1, c * CH + 1 + Wc)])
            emit_mix(c, Wc)
            if c >= 1:
                emit_xk(c - 1, Wc)
        emit_xk(NCH - 1, Wc)
        for c in range(NCH):
            emit_xr(c, Wc)
        prev_wc = Wc
        if ti + 1 < n_tiles:
            emit_early(ti + 1, SQB2, SQS2)
            emit_s3_part(ti + 1, 0)

        pl.emit("pool", lambda e, Wc=Wc: e.tensor_copy(
            out=WCARRY.rearrange("p (c o) -> p c o", o=1),
            in_=WARN.rearrange("p (c t) -> p c t", t=CH)[:, :, Wc:Wc + 1]),
            r=[WARN.iv()], w=[WCARRY.iv()])

        def ffn_pass(scr_name, rhs_buf, evac):
            for q in range(4):
                par = slab["fb"] % 3
                slab["fb"] += 1
                dma("sync", f"d_fb{par}",
                    FBB[:, par * 2048:(par + 1) * 2048],
                    ffn_scr[scr_name][128 * q:128 * (q + 1), :],
                    r=[DRAM1(scr_name + "_scr")],
                    w=[FBB.iv(par * 2048, (par + 1) * 2048)])
                for ds in range(2):
                    dch = 2 * q + ds
                    bank = rot()
                    for c in range(NCH):
                        pl.emit("pe", lambda e, par=par, ds=ds, c=c, bank=bank,
                                Wc=Wc, rhs_buf=rhs_buf: e.matmul(
                                    P[bank][0:128, 0:Wc],
                                    FBB[:, par * 2048 + c * 256 + ds * 128:
                                        par * 2048 + c * 256 + ds * 128 + 128],
                                    rhs_buf[:, c * CH:c * CH + Wc],
                                    start=(c == 0), stop=(c == NCH - 1)),
                                r=[FBB.iv(par * 2048, (par + 1) * 2048),
                                   rhs_buf.iv(c * CH, c * CH + Wc)],
                                w=[piv(bank, 0, Wc)])
                    evac(dch, bank)

        def r_evac(dch, bank):
            pl.emit("act", lambda e, dch=dch, bank=bank, Wc=Wc: e.activation(
                out=RS[:, dch * CH:dch * CH + Wc], in_=P[bank][0:128, 0:Wc],
                func=AF.Sigmoid),
                r=[piv(bank, 0, Wc)], w=[RS.iv(dch * CH, dch * CH + Wc)])

        def k_evac(dch, bank):
            pl.emit("act", lambda e, dch=dch, bank=bank, Wc=Wc: e.activation(
                out=KRELU[:, dch * CH:dch * CH + Wc], in_=P[bank][0:128, 0:Wc],
                func=AF.Relu),
                r=[piv(bank, 0, Wc)], w=[KRELU.iv(dch * CH, dch * CH + Wc)])
            pl.emit("dve", lambda e, dch=dch, Wc=Wc: e.tensor_tensor(
                out=KACT[:, dch * CH:dch * CH + Wc],
                in0=KRELU[:, dch * CH:dch * CH + Wc],
                in1=KRELU[:, dch * CH:dch * CH + Wc], op=ALU.mult),
                r=[KRELU.iv(dch * CH, dch * CH + Wc)],
                w=[KACT.iv(dch * CH, dch * CH + Wc)])

        def v_evac(dch, bank):
            pl.emit("dve", lambda e, dch=dch, bank=bank, Wc=Wc: e.tensor_tensor(
                out=FFO[:, dch * CH:dch * CH + Wc], in0=P[bank][0:128, 0:Wc],
                in1=RS[:, dch * CH:dch * CH + Wc], op=ALU.mult),
                r=[piv(bank, 0, Wc), RS.iv(dch * CH, dch * CH + Wc)],
                w=[FFO.iv(dch * CH, dch * CH + Wc)])

        ffn_pass("wk", XK, k_evac)
        ffn_pass("wr", XR, r_evac)
        ffn_pass("wv", KACT, v_evac)
        norm_stats(FFO, Wc, do_ln1=False, qb=SQB2, qs=SQS2, stage=1)

        if dbg and ti == DBG_TI:
            pl.emit("dve", lambda e: e.tensor_copy(
                out=TF[:, 0:4096], in_=XR[:, :]), r=[XR.iv()], w=[TF.iv()])
            dma("sync", "d_o", dbt["xr"][:, :], TF[:, 0:4096],
                r=[TF.iv()], w=[DRAM1("dbg_xr")])
            dma("sync", "d_o", dbt["ffo"][:, :], FFO[:, :],
                r=[FFO.iv()], w=[DRAM1("dbg_ffo")])
        if ti + 1 < n_tiles:
            emit_s3_part(ti + 1, 1)

        norm_stats(FFO, Wc, do_ln1=False, qb=SQB2, qs=SQS2, mb_=0, sb_=1,
                   stage=2)
        for c in range(NCH):
            pl.emit("dve", lambda e, c=c, Wc=Wc: e.tensor_tensor(
                out=T4P[c % 2][:, 0:Wc], in0=FFO[:, c * CH:c * CH + Wc],
                in1=P[0][0:128, 0:Wc], op=ALU.subtract),
                r=[FFO.iv(c * CH, c * CH + Wc), piv(0, 0, Wc)],
                w=[T4P[c % 2].iv()])
            pl.emit("dve", lambda e, c=c, Wc=Wc: e.tensor_tensor(
                out=X2[:, c * CH:c * CH + Wc], in0=T4P[c % 2][:, 0:Wc],
                in1=P[1][0:128, 0:Wc], op=ALU.mult),
                r=[T4P[c % 2].iv(), piv(1, 0, Wc)],
                w=[X2.iv(c * CH, c * CH + Wc)])
            pl.emit("pool", lambda e, c=c, Wc=Wc: e.tensor_tensor(
                out=X2[:, c * CH:c * CH + Wc], in0=X2[:, c * CH:c * CH + Wc],
                in1=X1[:, c * CH:c * CH + Wc], op=ALU.add),
                r=[X2.iv(c * CH, c * CH + Wc), X1.iv(c * CH, c * CH + Wc)],
                w=[X2.iv(c * CH, c * CH + Wc)])

        dma("act", "d_o",
            outdr.rearrange("(c p) t -> p c t", p=128)[:, :, s:s + Wc],
            X2.rearrange("p (c t) -> p c t", t=CH)[:, :, 0:Wc],
            r=[X2.iv()], w=[DRAM1("out")], tag=f"out{ti}")

    pl.emit("sync", lambda e: e.nop(), r=[DRAM1("out")])
    pl.finalize()

    with contextlib.ExitStack() as st:
        sems = {}
        for name in ["d_ld", "d_ldp", "d_dbg", "d_sf", "d_siw",
                     "d_sm0", "d_sm1", "d_sd0", "d_sd1", "d_fuse",
                     "d_hbw", "d_pww", "d_ha0", "d_ha1",
                     "d_inw0", "d_inw1", "d_inw2", "d_mb0", "d_mb1", "d_mb2", "d_ow0", "d_ow1",
                     "d_fb0", "d_fb1", "d_fb2", "d_dg0", "d_dg1", "d_dg2",
                     "d_xw", "d_o",
                     "e_pe", "e_act", "e_dve", "e_pool", "e_sync"]:
            sems[name] = st.enter_context(nc.semaphore(name))
        block = st.enter_context(nc.Block())
        block.tensor(lambda e: pl.run("pe", e, sems))
        block.scalar(lambda e: pl.run("act", e, sems))
        block.vector(lambda e: pl.run("dve", e, sems))
        block.gpsimd(lambda e: pl.run("pool", e, sems))
        block.sync(lambda e: pl.run("sync", e, sems))

    return nc


def _host_inputs(inp):
    asc = np.ascontiguousarray
    f = np.float32

    def colchunks(v, nch):
        return asc(np.asarray(v, f).reshape(nch, 128).T)

    w = {}
    w["inwT"] = asc(np.asarray(inp["L0_attn_in_w"], f).T
                    .reshape(8, 128, 16, 256).transpose(2, 1, 0, 3)
                    .reshape(2048, 2048))
    w["normw"] = colchunks(inp["L0_attn_norm_w"], NCH)
    w["inb"] = colchunks(inp["L0_attn_in_b"], 32)
    dk = np.asarray(inp["L0_attn_dwa_k"], f)[:, 0, :]
    w["dwataps"] = asc(dk.reshape(16, 128, 4).transpose(1, 0, 2).reshape(128, 64))
    w["dwab"] = colchunks(inp["L0_attn_dwa_b"], 16)
    w["abuf"] = asc(np.asarray(inp["L0_attn_A"], f))
    w["bmT"] = asc(np.asarray(inp["L0_attn_Bm"], f).T)
    sk = np.asarray(inp["L0_attn_sym_k"], f)[:, 0, :]
    w["symtaps"] = asc(sk.reshape(16, 128, 9).transpose(1, 0, 2).reshape(128, 144))
    w["symbrow"] = asc(np.asarray(inp["L0_attn_sym_b"], f).reshape(1, DI))
    w["pww"] = asc(np.asarray(inp["L0_attn_pw_w"], f))
    w["pwb"] = colchunks(inp["L0_attn_pw_b"], 16)
    w["hawT"] = asc(np.asarray(inp["L0_attn_ha_w"], f).T)
    w["hab"] = colchunks(inp["L0_attn_ha_b"], 8)
    w["hbw"] = asc(np.asarray(inp["L0_attn_hb_w"], f))
    w["hbb"] = colchunks(inp["L0_attn_hb_b"], 8)
    w["fuseT"] = asc(np.asarray(inp["L0_attn_fuse_w"], f).T)
    w["fbrow"] = asc(np.asarray(inp["L0_attn_fuse_b"], f).reshape(1, DI))
    w["outwT"] = asc(np.asarray(inp["L0_attn_out_w"], f).T
                     .reshape(16, 128, 8, 128).transpose(2, 1, 0, 3)
                     .reshape(1024, 2048))
    w["outb"] = colchunks(inp["L0_attn_out_b"], 8)
    def _ffnlay(a):
        return asc(np.asarray(a, f).T.reshape(8, 128, 4, 256)
                   .transpose(2, 1, 0, 3).reshape(512, 2048))
    w["wrT"] = _ffnlay(inp["L0_ffn_W_r"])
    w["wkT"] = _ffnlay(inp["L0_ffn_W_k"])
    w["wvT"] = _ffnlay(inp["L0_ffn_W_v"])
    w["mur"] = colchunks(inp["L0_ffn_mu_r"], 8)
    w["muk"] = colchunks(inp["L0_ffn_mu_k"], 8)
    w["ident"] = np.eye(128, dtype=ml_dtypes.bfloat16)
    return w


def kernel(**inp):
    x = np.asarray(inp["L0_attn_x"], np.float32)
    B, L, _ = x.shape
    w = _host_inputs(inp)
    nc = build_nc(L)
    in_maps = []
    for b in range(B):
        m = dict(w)
        m["x"] = np.ascontiguousarray(x[b].T)
        in_maps.append(m)
    res = run_bass_kernel_spmd(nc, in_maps, core_ids=list(range(B)))
    out = np.stack([np.ascontiguousarray(r["out"].T) for r in res.results])
    return out.astype(np.float32)


# version 23
# speedup vs baseline: 1.0197x; 1.0197x over previous
import contextlib
import numpy as np
import ml_dtypes
import concourse.bass as bass
import concourse.mybir as mybir
from concourse.bass_utils import run_bass_kernel_spmd

F32 = mybir.dt.float32
BF16 = mybir.dt.bfloat16
AF = mybir.ActivationFunctionType
ALU = mybir.AluOpType

D = 1024
DI = 2048
NCH = 8
WC = 504
CH = 512
EPS_LN = 1e-5
DBG_TI = 0

ENG_SEM = {"pe": "e_pe", "act": "e_act", "dve": "e_dve", "pool": "e_pool",
           "sync": "e_sync"}


class Op:
    __slots__ = ("idx", "eng", "fn", "deps", "inc", "waits", "is_dma",
                 "dma_sem", "marker", "tag")


class Plan:

    ALWAYS_FINAL = {"d_ld", "d_ldp", "d_sf", "d_siw", "d_fuse", "d_hbw",
                    "d_pww"}
    AFTER_LAST = {"d_sm0", "d_sm1",
                  "d_sd0", "d_sd1"}

    def __init__(self):
        self.ops = []
        self.spaces = {}

    def emit(self, eng, fn, r=(), w=(), dma=None, tag=""):
        idx = len(self.ops)
        deps = set()
        for iv in r:
            sp = self.spaces.setdefault(iv[0], [])
            lo, hi = iv[1], iv[2]
            for e in sp:
                if e[0] < hi and lo < e[1]:
                    if e[2] is not None:
                        deps.add(e[2])
                    e[3].append(idx)
        for iv in w:
            sp = self.spaces.setdefault(iv[0], [])
            lo, hi = iv[1], iv[2]
            keep = []
            for e in sp:
                if e[0] < hi and lo < e[1]:
                    if e[2] is not None:
                        deps.add(e[2])
                    deps.update(e[3])
                    if e[0] < lo:
                        keep.append([e[0], lo, e[2], list(e[3])])
                    if hi < e[1]:
                        keep.append([hi, e[1], e[2], list(e[3])])
                else:
                    keep.append(e)
            keep.append([lo, hi, idx, []])
            self.spaces[iv[0]] = keep
        op = Op()
        op.idx = idx
        op.eng = eng
        op.fn = fn
        op.deps = deps
        op.is_dma = dma is not None
        op.dma_sem = dma
        op.tag = tag
        self.ops.append(op)
        return idx

    def finalize(self):
        referenced = set()
        for op in self.ops:
            pruned = set()
            for d in op.deps:
                p = self.ops[d]
                if (not p.is_dma) and p.eng == op.eng:
                    continue
                pruned.add(d)
            op.deps = pruned
            referenced.update(pruned)
        ctr = {}
        for op in self.ops:
            if op.is_dma:
                ctr[op.dma_sem] = ctr.get(op.dma_sem, 0) + 16
                op.marker = (op.dma_sem, ctr[op.dma_sem])
                op.inc = (op.dma_sem, 16)
            elif op.idx in referenced:
                s = ENG_SEM[op.eng]
                ctr[s] = ctr.get(s, 0) + 1
                op.marker = (s, ctr[s])
                op.inc = (s, 1)
            else:
                op.marker = None
                op.inc = None
        self.eng_ops = {}
        for op in self.ops:
            self.eng_ops.setdefault(op.eng, []).append(op)
        totals = dict(ctr)
        last_dma = {}
        for op in self.ops:
            if op.is_dma:
                last_dma[op.dma_sem] = op.idx
        for eng, ops in self.eng_ops.items():
            observed = {}
            for op in ops:
                ws = {}
                for d in op.deps:
                    s, v = self.ops[d].marker
                    if s in self.ALWAYS_FINAL or (
                            s in self.AFTER_LAST and op.idx > last_dma[s]):
                        v = totals[s]
                    if observed.get(s, 0) < v:
                        ws[s] = max(ws.get(s, 0), v)
                for s, v in ws.items():
                    observed[s] = v
                op.waits = sorted(ws.items())

    def run(self, eng_name, eng, sems):
        for op in self.eng_ops.get(eng_name, []):
            for s, v in op.waits:
                eng.wait_ge(sems[s], v)
            inst = op.fn(eng)
            if op.inc is not None:
                inst.then_inc(sems[op.inc[0]], op.inc[1])


class SBuf:
    def __init__(self, nc, name, shape, dtype, off):
        self.t = nc.alloc_sbuf_tensor_at(name, shape, dtype, offset=off)
        self.off = off
        self.esz = 2 if dtype == BF16 else 4
        n = 1
        for s in shape[1:]:
            n *= s
        self.ncols = n

    def __getitem__(self, key):
        return self.t[key]

    def rearrange(self, *a, **k):
        return self.t.rearrange(*a, **k)

    def iv(self, a=0, b=None):
        if b is None:
            b = self.ncols
        return ("sbuf", self.off + a * self.esz, self.off + b * self.esz)


def build_nc(L, dbg=False):
    n_tiles = (L + WC - 1) // WC
    nc = bass.Bass("TRN2", target_bir_lowering=False)

    def din(name, shape, dt=F32):
        return nc.dram_tensor(name, shape, dt, kind="ExternalInput")

    xdr = din("x", [D, L])
    inwT = din("inwT", [2048, 2048])
    normw = din("normw", [128, NCH])
    inb = din("inb", [128, 32])
    dwataps = din("dwataps", [128, 64])
    dwab = din("dwab", [128, 16])
    abuf = din("abuf", [DI, 16])
    bmT = din("bmT", [DI, 16])
    symtaps = din("symtaps", [128, 144])
    symbrow = din("symbrow", [1, DI])
    pww = din("pww", [DI, DI])
    pwb = din("pwb", [128, 16])
    hawT = din("hawT", [DI, D])
    hab = din("hab", [128, 8])
    hbw = din("hbw", [D, DI])
    hbb = din("hbb", [128, 8])
    fuseT = din("fuseT", [DI, DI])
    fbrow = din("fbrow", [1, DI])
    outwT = din("outwT", [1024, 2048])
    outb = din("outb", [128, 8])
    wrT = din("wrT", [4 * 128, 2048])
    wkT = din("wkT", [4 * 128, 2048])
    wvT = din("wvT", [4 * 128, 2048])
    mur = din("mur", [128, 8])
    muk = din("muk", [128, 8])
    ident_dr = din("ident", [128, 128], BF16)

    outdr = nc.dram_tensor("out", [D, L], F32, kind="ExternalOutput")
    dbt = {}
    if dbg:
        for nm, rows, cols in [("val", 128, 4096), ("gate", 128, 4096),
                               ("silu", 128, 4096), ("alow", 16, 512),
                               ("yg", 128, 4096), ("attn", 128, 4096),
                               ("x1", 128, 4096), ("war", 128, 4096),
                               ("xr", 128, 4096), ("ffo", 128, 4096),
                               ("zw", 128, 4096), ("ln1x", 128, 4096),
                               ("g", 16, 2048), ("bias", 1, 2048),
                               ("t2", 128, 4096), ("fuseb", 128, 4096),
                               ("esqb", 128, 4096), ("erows", 3, 512),
                               ("ezw", 128, 4096), ("exw", 128, 4096)]:
            dbt[nm] = nc.dram_tensor("dbg_" + nm, [rows, cols], F32,
                                     kind="ExternalOutput")

    inw_scr = nc.dram_tensor("inw_scr", [16, 128, 2048], BF16, kind="Internal")
    m_scr = nc.dram_tensor("m_scr", [16, 128, 2048], BF16, kind="Internal")
    ow_scr = nc.dram_tensor("ow_scr", [8, 128, 2048], BF16, kind="Internal")
    wr_scr = nc.dram_tensor("wr_scr", [4 * 128, 2048], BF16, kind="Internal")
    wk_scr = nc.dram_tensor("wk_scr", [4 * 128, 2048], BF16, kind="Internal")
    wv_scr = nc.dram_tensor("wv_scr", [4 * 128, 2048], BF16, kind="Internal")
    dg_scr = nc.dram_tensor("dg_scr", [16, 128, 1152], BF16, kind="Internal")
    ffn_scr = {"wr": wr_scr, "wk": wk_scr, "wv": wv_scr}

    off = 16512

    def nxt(sz):
        nonlocal off
        o = off
        off += (sz + 31) & ~31
        return o

    CONSTS = SBuf(nc, "consts", [128, 8], F32, nxt(32))
    ONESC = SBuf(nc, "onesc", [128, 1], BF16, nxt(32))
    ONES128 = SBuf(nc, "ones128", [1, 128], BF16, nxt(256))
    ONESROW = SBuf(nc, "onesrow", [33, CH], BF16, nxt(1024))
    IDENT = SBuf(nc, "identsb", [128, 128], BF16, nxt(256))
    NORMW = SBuf(nc, "normwsb", [128, NCH], F32, nxt(32))
    INB = SBuf(nc, "inbsb", [128, 32], F32, nxt(128))
    OUTB = SBuf(nc, "outbsb", [128, 8], F32, nxt(32))
    MUR = SBuf(nc, "mursb", [128, 8], F32, nxt(32))
    MUK = SBuf(nc, "muksb", [128, 8], F32, nxt(32))
    DWAT = SBuf(nc, "dwatsb", [128, 64], F32, nxt(256))
    SYMT = SBuf(nc, "symtsb", [128, 144], BF16, nxt(288))
    ABF = SBuf(nc, "abfsb", [128, 256], BF16, nxt(512))
    AJ = SBuf(nc, "ajsb", [128, 1024], BF16, nxt(2048))
    G = SBuf(nc, "gsb", [16, DI], BF16, nxt(4096))
    BIAS = SBuf(nc, "biassb", [33, DI], BF16, nxt(4096))
    ALOW = SBuf(nc, "alowsb", [16, CH], BF16, nxt(1024))
    DWABBF = SBuf(nc, "dwabbf", [128, 16], BF16, nxt(32))
    HABBF = SBuf(nc, "habbf", [128, 8], BF16, nxt(32))
    HBBBF = SBuf(nc, "hbbbf", [128, 8], BF16, nxt(32))
    PWBBF = SBuf(nc, "pwbbf", [128, 16], BF16, nxt(32))
    VA = SBuf(nc, "vasb", [16, 1], BF16, nxt(32))
    INVNW = SBuf(nc, "invnw", [128, 8], F32, nxt(32))
    CONST1 = SBuf(nc, "const1", [128, 8], F32, nxt(32))
    WCARRY = SBuf(nc, "wcarry", [128, 8], F32, nxt(32))
    MROW = SBuf(nc, "mrow", [1, CH], BF16, nxt(1024))
    SROW1 = SBuf(nc, "srow1", [1, CH], BF16, nxt(1024))
    SROW2 = SBuf(nc, "srow2", [1, CH], BF16, nxt(1024))
    SROW3 = SBuf(nc, "srow3", [1, CH], BF16, nxt(1024))
    MSQR = SBuf(nc, "msqr", [1, CH], F32, nxt(2048))
    VROW = SBuf(nc, "vrow", [1, CH], F32, nxt(2048))
    LROW = SBuf(nc, "lrow", [1, CH], F32, MSQR.off)

    AR = 16384
    o_XW = nxt(AR)
    o_VALSQ = nxt(AR)
    o_R1 = nxt(AR)
    o_R2 = nxt(AR)
    o_R3 = nxt(AR)
    o_R4 = nxt(AR)
    o_X1 = nxt(AR)
    o_WAR = nxt(AR)

    XW = SBuf(nc, "xw", [128, 4096], F32, o_XW)
    LROW2 = SBuf(nc, "lrow2", [1, CH], F32, o_VALSQ + 8192)
    VAL = SBuf(nc, "val", [128, 8192], BF16, o_VALSQ)
    SQB = SBuf(nc, "sqb", [128, 4096], BF16, o_VALSQ)
    SQS = SBuf(nc, "sqs", [128, 4096], BF16, o_VALSQ + 8192)
    TF = SBuf(nc, "tf", [128, 4096], F32, o_VALSQ)
    SILU = SBuf(nc, "silu", [128, 8192], BF16, o_R1)
    RS = SBuf(nc, "rs", [128, 4096], BF16, o_R1 + 8192)
    UA = SBuf(nc, "ua", [128, 4096], F32, o_R1)
    FFO = SBuf(nc, "ffo", [128, 4096], F32, o_R1)
    YG = SBuf(nc, "yg", [128, 8192], BF16, o_R2)
    KRELU = SBuf(nc, "krelu", [128, 4096], BF16, o_R2)
    KACT = SBuf(nc, "kact", [128, 4096], BF16, o_R2 + 8192)
    GATE = SBuf(nc, "gate", [128, 8192], BF16, o_R3)
    XR = SBuf(nc, "xr", [128, 4096], BF16, o_R3)
    XK = SBuf(nc, "xk", [128, 4096], BF16, o_R3 + 8192)
    U4 = SBuf(nc, "u4", [128, 4096], F32, o_R3)
    ATTN = SBuf(nc, "attn", [128, 4096], F32, o_R4)
    LN1X = SBuf(nc, "ln1x", [128, 4096], BF16, o_R4 + 8192)
    X2 = SBuf(nc, "x2", [128, 4096], F32, o_WAR)
    X1 = SBuf(nc, "x1", [128, 4096], F32, o_X1)
    WARN = SBuf(nc, "war", [128, 4096], F32, o_WAR)

    INWB = SBuf(nc, "inwb", [128, 2 * 2048], BF16, nxt(8192))
    INWB3 = SBuf(nc, "inwb3", [128, 2048], BF16, o_R2)
    MB = SBuf(nc, "mbuf", [128, 3 * 2048], BF16, nxt(12288))
    OWB = SBuf(nc, "owb", [128, 2 * 2048], BF16, nxt(8192))
    FBB = SBuf(nc, "fbb", [128, 3 * 2048], BF16, nxt(12288))
    DGB = SBuf(nc, "dgb", [128, 3 * 1152], BF16, nxt(6912))
    TMPS = SBuf(nc, "tmps", [128, 512], BF16, nxt(1024))
    SQB2 = SBuf(nc, "sqb2", [128, 4096], BF16, o_R4)
    SQS2 = SBuf(nc, "sqs2", [128, 4096], BF16, MB.off)
    T4P = [SBuf(nc, f"t4p{i}", [128, 512], F32, OWB.off + 2048 * i)
           for i in range(2)]
    ZW = SBuf(nc, "zw2", [128, 4096], BF16, nxt(8192))
    tile_top = off
    assert tile_top <= 229344, f"SBUF over budget: {tile_top}"

    o_p = o_XW
    FUSEB = SBuf(nc, "fuseb", [128, 16 * 2048], BF16, o_p)
    HBWB = SBuf(nc, "hbwb", [128, 8 * 2048], BF16, o_p + 65536)
    HAB_ = SBuf(nc, "habuf", [128, 2 * 1024], BF16, o_p + 98304)
    T1 = SBuf(nc, "t1", [16, 1024], BF16, o_p + 102400)
    T1T = SBuf(nc, "t1t", [128, 128], BF16, o_p + 104448)
    BMTB = SBuf(nc, "bmtb", [128, 256], BF16, o_p + 104704)
    FBROW = SBuf(nc, "fbrowsb", [1, 2048], BF16, o_p + 105216)
    PWWB = SBuf(nc, "pwwb", [128, 16 * 2048], BF16, o_p)
    T2B = SBuf(nc, "t2b", [128, 16 * 2048], BF16, o_WAR)
    MPB = [SBuf(nc, f"mpb_{i}", [128, 2048], BF16, o_p + 65536 + i * 4096)
           for i in range(2)]

    P = [nc.alloc_psum_tensor(f"ps{i}", [128, 512], F32) for i in range(8)]
    PBF16 = P[7].bitcast(BF16)

    def piv(b, a=0, e=512):
        return ("psum", b * 2048 + a * 4, b * 2048 + e * 4)

    nc.const_aps.register(F32, EPS_LN, CONSTS[:, 0:1])
    nc.const_aps.register(F32, -0.5, CONSTS[:, 1:2])
    nc.const_aps.register(F32, 1.000001, CONSTS[:, 2:3])
    nc.const_aps.register(F32, 1e-11, CONSTS[:, 3:4])
    nc.const_aps.register(F32, 0.5, CONSTS[:, 4:5])
    civ = CONSTS.iv()

    pl = Plan()
    rot_state = [0]

    def rot():
        rot_state[0] = (rot_state[0] + 1) % 4
        return 3 + rot_state[0]

    def DRAM1(name):
        return (name, 0, 1)

    def dma(queue, sem, out_ap, in_ap, r, w, tag=""):
        return pl.emit(queue, lambda e, o=out_ap, i=in_ap: e.dma_start(out=o, in_=i),
                       r=r, w=w, dma=sem, tag=tag)

    for col, val in [(0, EPS_LN), (1, -0.5), (2, 1.000001), (3, 1e-11),
                     (4, 0.5)]:
        pl.emit("pool", lambda e, c=col, v=val: e.memset(CONSTS[:, c:c + 1], v),
                w=[CONSTS.iv(col, col + 1)])
    pl.emit("pool", lambda e: e.memset(ONESC[:, :], 1.0 / 1024.0), w=[ONESC.iv()])
    pl.emit("pool", lambda e: e.memset(ONES128[:, :], 1.0), w=[ONES128.iv()])
    pl.emit("pool", lambda e: e.memset(ONESROW[:, :], 1.0), w=[ONESROW.iv()])

    for dst, src in [(IDENT, ident_dr), (NORMW, normw), (INB, inb), (OUTB, outb),
                     (MUR, mur), (MUK, muk), (DWAT, dwataps)]:
        dma("sync", "d_ld", dst[:, :], src[:, :], r=[], w=[dst.iv()])
    dma("pool", "d_ldp", BIAS[32:33, :], symbrow[:, :], r=[], w=[BIAS.iv()])
    for dst, src in [(SYMT, symtaps),
                     (DWABBF, dwab), (HABBF, hab), (HBBBF, hbb), (PWBBF, pwb),
                     (FBROW, fbrow)]:
        dma("pool", "d_ldp", dst[:, :], src[:, :], r=[], w=[dst.iv()])
    dma("pool", "d_ldp", ABF.rearrange("p (c s) -> p c s", s=16),
        abuf.rearrange("(c p) s -> p c s", p=128), r=[], w=[ABF.iv()])
    dma("pool", "d_ldp", BMTB.rearrange("p (c s) -> p c s", s=16),
        bmT.rearrange("(c p) s -> p c s", p=128), r=[], w=[BMTB.iv()])

    pl.emit("act", lambda e: e.activation(
        out=CONST1[:, :], in_=NORMW[:, :], func=AF.Ln),
        r=[NORMW.iv(), civ], w=[CONST1.iv()])
    pl.emit("act", lambda e: e.activation(
        out=INVNW[:, :], in_=CONST1[:, :], func=AF.Exp, scale=-1.0),
        r=[CONST1.iv(), civ], w=[INVNW.iv()])

    for k in range(16):
        p = k % 2
        dma("pool", f"d_ha{p}", HAB_[:, p * 1024:(p + 1) * 1024],
            hawT[128 * k:128 * (k + 1), :],
            r=[], w=[HAB_.iv(p * 1024, (p + 1) * 1024)])
        for n in range(2):
            pl.emit("pe", lambda e, k=k, p=p, n=n: e.matmul(
                P[n][0:16, 0:512], BMTB[:, 16 * k:16 * (k + 1)],
                HAB_[:, p * 1024 + 512 * n: p * 1024 + 512 * (n + 1)],
                start=(k == 0), stop=(k == 15)),
                r=[HAB_.iv(p * 1024, (p + 1) * 1024), BMTB.iv()], w=[piv(n)])
    dma("pool", "d_fuse", FUSEB.rearrange("p (c q) -> p c q", q=2048),
        fuseT.rearrange("(c p) q -> p c q", p=128), r=[], w=[FUSEB.iv()])
    dma("pool", "d_hbw", HBWB.rearrange("p (c q) -> p c q", q=2048),
        hbw.rearrange("(c p) q -> p c q", p=128), r=[], w=[HBWB.iv()])

    for n in range(2):
        pl.emit("act", lambda e, n=n: e.copy(
            out=T1[0:16, 512 * n:512 * (n + 1)], in_=P[n][0:16, 0:512]),
            r=[piv(n)], w=[T1.iv(512 * n, 512 * (n + 1))])
    for g in range(8):
        pl.emit("pe", lambda e, g=g: e.transpose(
            PBF16[0:128, 0:16], T1[0:16, 128 * g:128 * (g + 1)],
            IDENT[0:16, 0:16]),
            r=[T1.iv(), IDENT.iv()], w=[piv(7, 0, 64)])
        pl.emit("dve", lambda e, g=g: e.tensor_copy(
            out=T1T[:, 16 * g:16 * (g + 1)], in_=PBF16[0:128, 0:16]),
            r=[piv(7, 0, 64)], w=[T1T.iv(16 * g, 16 * (g + 1))])
    for n in range(4):
        for k in range(8):
            pl.emit("pe", lambda e, n=n, k=k: e.matmul(
                P[n][0:16, 0:512], T1T[:, 16 * k:16 * (k + 1)],
                FUSEB[:, 2048 * k + 512 * n: 2048 * k + 512 * (n + 1)],
                start=(k == 0), stop=(k == 7)),
                r=[T1T.iv(), FUSEB.iv()], w=[piv(n)])
        pl.emit("dve", lambda e, n=n: e.tensor_copy(
            out=G[0:16, 512 * n:512 * (n + 1)], in_=P[n][0:16, 0:512]),
            r=[piv(n)], w=[G.iv(512 * n, 512 * (n + 1))])

    for k in range(8):
        for n in range(4):
            pl.emit("pe", lambda e, n=n, k=k: e.matmul(
                P[4 + n][0:1, 0:512], HABBF[:, k:k + 1],
                FUSEB[:, 2048 * k + 512 * n: 2048 * k + 512 * (n + 1)],
                start=(k == 0), stop=False),
                r=[HABBF.iv(), FUSEB.iv()], w=[piv(4 + n, 0, 512)])
    for k in range(8):
        for n in range(4):
            pl.emit("pe", lambda e, n=n, k=k: e.matmul(
                P[4 + n][0:1, 0:512], HBBBF[:, k:k + 1],
                FUSEB[:, 2048 * (8 + k) + 512 * n: 2048 * (8 + k) + 512 * (n + 1)],
                start=False, stop=False),
                r=[HBBBF.iv(), FUSEB.iv()], w=[piv(4 + n, 0, 512)])
    for n in range(4):
        pl.emit("pe", lambda e, n=n: e.matmul(
            P[4 + n][0:1, 0:512], ONES128[0:1, 0:1],
            FBROW[0:1, 512 * n:512 * (n + 1)], start=False, stop=False),
            r=[ONES128.iv(), FBROW.iv()], w=[piv(4 + n, 0, 512)])

    for c in range(16):
        for n in range(4):
            bank = (c * 4 + n) % 2
            for d in range(8):
                pl.emit("pe", lambda e, c=c, n=n, d=d, bank=bank: e.matmul(
                    P[bank][0:128, 0:512],
                    HBWB[:, 2048 * d + 128 * c: 2048 * d + 128 * (c + 1)],
                    FUSEB[:, 2048 * (8 + d) + 512 * n: 2048 * (8 + d) + 512 * (n + 1)],
                    start=(d == 0), stop=(d == 7)),
                    r=[HBWB.iv(), FUSEB.iv()], w=[piv(bank)])
            pl.emit("act", lambda e, c=c, n=n, bank=bank: e.copy(
                out=T2B[:, 2048 * c + 512 * n: 2048 * c + 512 * (n + 1)],
                in_=P[bank][0:128, 0:512]),
                r=[piv(bank)],
                w=[T2B.iv(2048 * c + 512 * n, 2048 * c + 512 * (n + 1))])

    if dbg:
        pl.emit("dve", lambda e: e.tensor_copy(
            out=TF[:, 0:4096], in_=T2B[0:128, 0:4096]),
            r=[T2B.iv(0, 4096)], w=[TF.iv()])
        dma("sync", "d_o", dbt["t2"][:, :], TF[:, 0:4096],
            r=[TF.iv()], w=[DRAM1("dbg_t2")])

    if dbg:
        dma("pool", "d_dbg", dbt["fuseb"][:, :], FUSEB[:, 24576:28672],
            r=[FUSEB.iv(24576, 28672)], w=[DRAM1("dbg_fuseb")])

    for cc in range(16):
        for n in range(4):
            pl.emit("pe", lambda e, cc=cc, n=n: e.matmul(
                P[4 + n][0:1, 0:512], PWBBF[:, cc:cc + 1],
                T2B[:, 2048 * cc + 512 * n: 2048 * cc + 512 * (n + 1)],
                start=False, stop=False),
                r=[PWBBF.iv(), T2B.iv()], w=[piv(4 + n, 0, 512)])

    dma("pool", "d_pww", PWWB.rearrange("p (c q) -> p c q", q=2048),
        pww.rearrange("(c p) q -> p c q", p=128), r=[], w=[PWWB.iv()])

    for pc in range(16):
        mp = pc % 2
        for n in range(4):
            bank = (pc * 4 + n) % 2
            for cc in range(16):
                pl.emit("pe", lambda e, pc=pc, n=n, cc=cc, bank=bank: e.matmul(
                    P[bank][0:128, 0:512],
                    PWWB[:, 2048 * cc + 128 * pc: 2048 * cc + 128 * (pc + 1)],
                    T2B[:, 2048 * cc + 512 * n: 2048 * cc + 512 * (n + 1)],
                    start=(cc == 0), stop=(cc == 15)),
                    r=[PWWB.iv(), T2B.iv()], w=[piv(bank)])
            pl.emit("act", lambda e, n=n, mp=mp, bank=bank: e.copy(
                out=MPB[mp][:, 512 * n:512 * (n + 1)], in_=P[bank][0:128, 0:512]),
                r=[piv(bank)], w=[MPB[mp].iv(512 * n, 512 * (n + 1))])
        dma("act", f"d_sm{mp}",
            m_scr.rearrange("e p (c q) -> p e c q", q=128)[:, :, pc, :],
            MPB[mp].rearrange("p (e q) -> p e q", q=128),
            r=[MPB[mp].iv()], w=[DRAM1("m_scr")])

    for k in range(16):
        pl.emit("pe", lambda e, k=k: e.matmul(
            P[2][0:16, 0:1], ABF[:, 16 * k:16 * (k + 1)], DWABBF[:, k:k + 1],
            start=(k == 0), stop=(k == 15)),
            r=[ABF.iv(), DWABBF.iv()], w=[piv(2, 0, 1)])
    pl.emit("dve", lambda e: e.tensor_copy(out=VA[0:16, 0:1], in_=P[2][0:16, 0:1]),
            r=[piv(2, 0, 1)], w=[VA.iv()])
    for n in range(4):
        pl.emit("pe", lambda e, n=n: e.matmul(
            P[4 + n][0:1, 0:512], VA[0:16, 0:1],
            G[0:16, 512 * n:512 * (n + 1)], start=False, stop=True),
            r=[VA.iv(), G.iv()], w=[piv(4 + n, 0, 512)])
    for n in range(4):
        pl.emit("act", lambda e, n=n: e.copy(
            out=BIAS[0:1, 512 * n:512 * (n + 1)], in_=P[4 + n][0:1, 0:512]),
            r=[piv(4 + n, 0, 512)], w=[BIAS.iv(512 * n, 512 * (n + 1))])

    for c in range(16):
        for j in range(4):
            pl.emit("dve", lambda e, c=c, j=j: e.tensor_scalar(
                out=AJ[:, (c * 4 + j) * 16:(c * 4 + j + 1) * 16],
                in0=ABF[:, 16 * c:16 * (c + 1)],
                scalar1=DWAT[:, c * 4 + j:c * 4 + j + 1], scalar2=None,
                op0=ALU.mult),
                r=[ABF.iv(), DWAT.iv()],
                w=[AJ.iv((c * 4 + j) * 16, (c * 4 + j + 1) * 16)])

    for c in range(16):
        dp = c % 2
        for j in range(9):
            pl.emit("pool", lambda e, c=c, j=j, dp=dp: e.tensor_tensor(
                out=DGB[:, dp * 1152 + 128 * j: dp * 1152 + 128 * (j + 1)],
                in0=IDENT[:, :],
                in1=SYMT[:, c * 9 + j:c * 9 + j + 1].broadcast_to([128, 128]),
                op=ALU.mult),
                r=[IDENT.iv(), SYMT.iv()],
                w=[DGB.iv(dp * 1152 + 128 * j, dp * 1152 + 128 * (j + 1))])
        dma("sync", f"d_sd{dp}", dg_scr[c],
            DGB[:, dp * 1152:(dp + 1) * 1152],
            r=[DGB.iv(dp * 1152, (dp + 1) * 1152)], w=[DRAM1("dg_scr")])

    dma("pool", "d_siw", inw_scr.rearrange("o p q -> (o p) q"), inwT[:, :],
        r=[], w=[DRAM1("inw_scr")])
    dma("pool", "d_sf", wk_scr[:, :], wkT[:, :], r=[], w=[DRAM1("wk_scr")])
    dma("pool", "d_sf", wr_scr[:, :], wrT[:, :], r=[], w=[DRAM1("wr_scr")])
    dma("pool", "d_siw", ow_scr.rearrange("d p q -> (d p) q"), outwT[:, :],
        r=[], w=[DRAM1("ow_scr")])
    dma("pool", "d_sf", wv_scr[:, :], wvT[:, :], r=[], w=[DRAM1("wv_scr")])

    pl.emit("pool", lambda e: e.memset(
        WARN.rearrange("p (c t) -> p c t", t=CH)[:, :, 0:1], 0.0),
        w=[WARN.iv()])

    slab = {"inw": 0, "m": 0, "ow": 0, "fb": 0, "dg": 0, "tmps": 0}
    dg_base = [0]

    def norm_stats(src, width, do_ln1, qb=None, qs=None, mb_=2, sb_=7,
                   stage=0, cast_eng="dve"):
        qb = qb if qb is not None else SQB
        qs = qs if qs is not None else SQS
        w8 = width
        if stage != 2 and cast_eng == "act":
            for c in range(NCH):
                pl.emit("act", lambda e, c=c: e.activation(
                    out=qb[:, c * CH:c * CH + w8],
                    in_=src[:, c * CH:c * CH + w8], func=AF.Identity),
                    r=[src.iv(c * CH, c * CH + w8)],
                    w=[qb.iv(c * CH, c * CH + w8)])
                pl.emit("act", lambda e, c=c: e.activation(
                    out=qs[:, c * CH:c * CH + w8],
                    in_=src[:, c * CH:c * CH + w8], func=AF.Square),
                    r=[src.iv(c * CH, c * CH + w8)],
                    w=[qs.iv(c * CH, c * CH + w8)])
        for h in (() if stage == 2 or cast_eng == "act" else range(2)):
            pl.emit("dve", lambda e, h=h: e.tensor_copy(
                out=qb.rearrange("p (c t) -> p c t", t=CH)[:, 4 * h:4 * h + 4, 0:w8],
                in_=src.rearrange("p (c t) -> p c t", t=CH)[:, 4 * h:4 * h + 4, 0:w8]),
                r=[src.iv(4 * h * CH, (4 * h + 4) * CH)],
                w=[qb.iv(4 * h * CH, (4 * h + 4) * CH)])
            pl.emit("act", lambda e, h=h: e.activation(
                out=qs.rearrange("p (c t) -> p c t", t=CH)[:, 4 * h:4 * h + 4, 0:w8],
                in_=src.rearrange("p (c t) -> p c t", t=CH)[:, 4 * h:4 * h + 4, 0:w8],
                func=AF.Square),
                r=[src.iv(4 * h * CH, (4 * h + 4) * CH)],
                w=[qs.iv(4 * h * CH, (4 * h + 4) * CH)])
        if stage == 1:
            return
        for half in range(2):
            for c in range(4 * half, 4 * half + 4):
                pl.emit("pe", lambda e, c=c: e.matmul(
                    P[0][0:1, 0:w8], ONESC[:, :], qb[:, c * CH:c * CH + w8],
                    start=(c == 0), stop=(c == NCH - 1)),
                    r=[ONESC.iv(), qb.iv(c * CH, c * CH + w8)], w=[piv(0, 0, w8)])
            for c in range(4 * half, 4 * half + 4):
                pl.emit("pe", lambda e, c=c: e.matmul(
                    P[1][0:1, 0:w8], ONESC[:, :], qs[:, c * CH:c * CH + w8],
                    start=(c == 0), stop=(c == NCH - 1)),
                    r=[ONESC.iv(), qs.iv(c * CH, c * CH + w8)], w=[piv(1, 0, w8)])
        pl.emit("dve", lambda e: e.tensor_copy(out=MROW[0:1, 0:w8],
                                               in_=P[0][0:1, 0:w8]),
                r=[piv(0, 0, w8)], w=[MROW.iv(0, w8)])
        pl.emit("dve", lambda e: e.tensor_tensor(
            out=MSQR[0:1, 0:w8], in0=MROW[0:1, 0:w8], in1=MROW[0:1, 0:w8],
            op=ALU.mult),
            r=[MROW.iv(0, w8)], w=[MSQR.iv(0, w8)])
        pl.emit("dve", lambda e: e.tensor_tensor(
            out=VROW[0:1, 0:w8], in0=P[1][0:1, 0:w8], in1=MSQR[0:1, 0:w8],
            op=ALU.subtract), r=[piv(1, 0, w8), MSQR.iv(0, w8)],
            w=[VROW.iv(0, w8)])
        pl.emit("act", lambda e: e.activation(
            out=LROW[0:1, 0:w8], in_=VROW[0:1, 0:w8], func=AF.Ln, bias=EPS_LN),
            r=[VROW.iv(0, w8), civ], w=[LROW.iv(0, w8)])
        pl.emit("act", lambda e: e.activation(
            out=SROW1[0:1, 0:w8], in_=LROW[0:1, 0:w8], func=AF.Exp, scale=-0.5),
            r=[LROW.iv(0, w8), civ], w=[SROW1.iv(0, w8)])
        if do_ln1:
            pl.emit("act", lambda e: e.activation(
                out=LROW[0:1, 0:w8], in_=VROW[0:1, 0:w8], func=AF.Ln,
                bias=1e-11, scale=1.000001),
                r=[VROW.iv(0, w8), civ], w=[LROW.iv(0, w8)])
            pl.emit("act", lambda e: e.activation(
                out=SROW2[0:1, 0:w8], in_=LROW[0:1, 0:w8], func=AF.Exp,
                scale=-0.5),
                r=[LROW.iv(0, w8), civ], w=[SROW2.iv(0, w8)])
        pl.emit("pe", lambda e: e.matmul(
            P[mb_][0:128, 0:w8], ONES128[:, :], MROW[0:1, 0:w8],
            start=True, stop=True),
            r=[ONES128.iv(), MROW.iv(0, w8)], w=[piv(mb_, 0, w8)])
        pl.emit("pe", lambda e: e.matmul(
            P[sb_][0:128, 0:w8], ONES128[:, :], SROW1[0:1, 0:w8],
            start=True, stop=True),
            r=[ONES128.iv(), SROW1.iv(0, w8)], w=[piv(sb_, 0, w8)])
        if do_ln1:
            pl.emit("pe", lambda e: e.matmul(
                P[0][0:128, 0:w8], ONES128[:, :], SROW2[0:1, 0:w8],
                start=True, stop=True),
                r=[ONES128.iv(), SROW2.iv(0, w8)], w=[piv(0, 0, w8)])

    def tile_geom(ti):
        s = ti * WC
        Wc = min(WC, L - s)
        Ww = Wc + 8
        w0 = s - 4
        vlo = max(0, w0) - w0
        vhi = min(L, w0 + Ww) - w0
        return s, Wc, Ww, w0, vlo, vhi

    def emit_xload(ti):
        s, Wc, Ww, w0, vlo, vhi = tile_geom(ti)
        dma("sync", "d_xw",
            XW.rearrange("p (c t) -> p c t", t=CH)[:, :, vlo:vhi],
            xdr.rearrange("(c p) t -> p c t", p=128)[:, :, w0 + vlo:w0 + vhi],
            r=[], w=[XW.iv()], tag=f"x{ti}")
        if vlo > 0:
            pl.emit("pool", lambda e, vlo=vlo: e.memset(
                XW.rearrange("p (c t) -> p c t", t=CH)[:, :, 0:vlo], 0.0),
                w=[XW.iv()])
        if vhi < Ww:
            pl.emit("pool", lambda e, vhi=vhi, Ww=Ww: e.memset(
                XW.rearrange("p (c t) -> p c t", t=CH)[:, :, vhi:Ww], 0.0),
                w=[XW.iv()])

    def emit_early_cast(tj):
        s, Wc, Ww, w0, vlo, vhi = tile_geom(tj)
        for h in range(2):
            pl.emit("dve", lambda e, h=h, Ww=Ww: e.tensor_copy(
                out=SQB.rearrange("p (c t) -> p c t", t=CH)[:, 4 * h:4 * h + 4, 0:Ww],
                in_=XW.rearrange("p (c t) -> p c t", t=CH)[:, 4 * h:4 * h + 4, 0:Ww]),
                r=[XW.iv(4 * h * CH, (4 * h + 4) * CH)],
                w=[SQB.iv(4 * h * CH, (4 * h + 4) * CH)])
            pl.emit("act", lambda e, h=h, Ww=Ww: e.activation(
                out=SQS.rearrange("p (c t) -> p c t", t=CH)[:, 4 * h:4 * h + 4, 0:Ww],
                in_=XW.rearrange("p (c t) -> p c t", t=CH)[:, 4 * h:4 * h + 4, 0:Ww],
                func=AF.Square),
                r=[XW.iv(4 * h * CH, (4 * h + 4) * CH)],
                w=[SQS.iv(4 * h * CH, (4 * h + 4) * CH)])

    def emit_early(tj):
        s, Wc, Ww, w0, vlo, vhi = tile_geom(tj)
        for c in range(NCH):
            pl.emit("pe", lambda e, c=c, Ww=Ww: e.matmul(
                P[0][0:1, 0:Ww], ONESC[:, :], SQB[:, c * CH:c * CH + Ww],
                start=(c == 0), stop=(c == NCH - 1)),
                r=[ONESC.iv(), SQB.iv(c * CH, c * CH + Ww)], w=[piv(0, 0, Ww)])
        for c in range(NCH):
            pl.emit("pe", lambda e, c=c, Ww=Ww: e.matmul(
                P[1][0:1, 0:Ww], ONESC[:, :], SQS[:, c * CH:c * CH + Ww],
                start=(c == 0), stop=(c == NCH - 1)),
                r=[ONESC.iv(), SQS.iv(c * CH, c * CH + Ww)], w=[piv(1, 0, Ww)])
        pl.emit("act", lambda e, Ww=Ww: e.copy(out=MROW[0:1, 0:Ww],
                                               in_=P[0][0:1, 0:Ww]),
                r=[piv(0, 0, Ww)], w=[MROW.iv(0, Ww)])
        pl.emit("act", lambda e, Ww=Ww: e.activation(
            out=MSQR[0:1, 0:Ww], in_=MROW[0:1, 0:Ww], func=AF.Square),
            r=[MROW.iv(0, Ww)], w=[MSQR.iv(0, Ww)])
        pl.emit("dve", lambda e, Ww=Ww: e.tensor_tensor(
            out=VROW[0:1, 0:Ww], in0=P[1][0:1, 0:Ww], in1=MSQR[0:1, 0:Ww],
            op=ALU.subtract), r=[piv(1, 0, Ww), MSQR.iv(0, Ww)],
            w=[VROW.iv(0, Ww)])
        pl.emit("act", lambda e, Ww=Ww: e.activation(
            out=LROW[0:1, 0:Ww], in_=VROW[0:1, 0:Ww], func=AF.Ln, bias=EPS_LN),
            r=[VROW.iv(0, Ww), civ], w=[LROW.iv(0, Ww)])
        pl.emit("act", lambda e, Ww=Ww: e.activation(
            out=LROW2[0:1, 0:Ww], in_=VROW[0:1, 0:Ww], func=AF.Ln,
            bias=1e-11, scale=1.000001),
            r=[VROW.iv(0, Ww), civ], w=[LROW2.iv(0, Ww)])
        pl.emit("act", lambda e, Ww=Ww: e.activation(
            out=SROW2[0:1, 0:Ww], in_=LROW2[0:1, 0:Ww], func=AF.Exp,
            scale=-0.5),
            r=[LROW2.iv(0, Ww), civ], w=[SROW2.iv(0, Ww)])
        pl.emit("dve", lambda e, Ww=Ww: e.tensor_tensor(
            out=VROW[0:1, 0:Ww], in0=LROW2[0:1, 0:Ww], in1=LROW[0:1, 0:Ww],
            op=ALU.subtract),
            r=[LROW2.iv(0, Ww), LROW.iv(0, Ww)], w=[VROW.iv(0, Ww)])
        pl.emit("act", lambda e, Ww=Ww: e.activation(
            out=SROW3[0:1, 0:Ww], in_=VROW[0:1, 0:Ww], func=AF.Exp, scale=0.5),
            r=[VROW.iv(0, Ww), civ], w=[SROW3.iv(0, Ww)])
        pl.emit("pe", lambda e, Ww=Ww: e.matmul(
            P[2][0:128, 0:Ww], ONES128[:, :], MROW[0:1, 0:Ww],
            start=True, stop=True),
            r=[ONES128.iv(), MROW.iv(0, Ww)], w=[piv(2, 0, Ww)])
        pl.emit("pe", lambda e, Ww=Ww: e.matmul(
            P[0][0:128, 0:Ww], ONES128[:, :], SROW2[0:1, 0:Ww],
            start=True, stop=True),
            r=[ONES128.iv(), SROW2.iv(0, Ww)], w=[piv(0, 0, Ww)])
        for c in range(NCH):
            pl.emit("dve", lambda e, c=c, Ww=Ww: e.tensor_tensor(
                out=TF[:, c * CH:c * CH + Ww], in0=XW[:, c * CH:c * CH + Ww],
                in1=P[2][0:128, 0:Ww], op=ALU.subtract),
                r=[XW.iv(c * CH, c * CH + Ww), piv(2, 0, Ww)],
                w=[TF.iv(c * CH, c * CH + Ww)])
        for c in range(NCH):
            pl.emit("dve", lambda e, c=c, Ww=Ww: e.scalar_tensor_tensor(
                out=ZW[:, c * CH:c * CH + Ww], in0=TF[:, c * CH:c * CH + Ww],
                scalar=NORMW[:, c:c + 1], in1=P[0][0:128, 0:Ww],
                op0=ALU.mult, op1=ALU.mult),
                r=[TF.iv(c * CH, c * CH + Ww), NORMW.iv(), piv(0, 0, Ww)],
                w=[ZW.iv(c * CH, c * CH + Ww)])
        if dbg and tj == DBG_TI:
            dma("pool", "d_dbg", dbt["erows"][0:1, :], MROW[0:1, :],
                r=[MROW.iv()], w=[DRAM1("dbg_erows")])
            dma("pool", "d_dbg", dbt["erows"][1:2, :], SROW2[0:1, :],
                r=[SROW2.iv()], w=[DRAM1("dbg_erows")])
            dma("pool", "d_dbg", dbt["erows"][2:3, :], SROW3[0:1, :],
                r=[SROW3.iv()], w=[DRAM1("dbg_erows")])
            dma("pool", "d_dbg", dbt["ezw"][:, :], ZW[:, :],
                r=[ZW.iv()], w=[DRAM1("dbg_ezw")])

    def emit_ln1x(ti):
        s, Wc, Ww, w0, vlo, vhi = tile_geom(ti)
        pl.emit("pe", lambda e, Ww=Ww: e.matmul(
            P[1][0:128, 0:Ww], ONES128[:, :], SROW3[0:1, 0:Ww],
            start=True, stop=True),
            r=[ONES128.iv(), SROW3.iv(0, Ww)], w=[piv(1, 0, Ww)])
        for c in range(NCH):
            pl.emit("dve", lambda e, c=c, Wc=Wc: e.scalar_tensor_tensor(
                out=LN1X[:, c * CH:c * CH + Wc],
                in0=ZW[:, c * CH + 4:c * CH + 4 + Wc],
                scalar=INVNW[:, c:c + 1], in1=P[1][0:128, 4:4 + Wc],
                op0=ALU.mult, op1=ALU.mult),
                r=[ZW.iv(c * CH, c * CH + CH), INVNW.iv(), piv(1)],
                w=[LN1X.iv(c * CH, c * CH + Wc)])

    def emit_s3(ti):
        s, Wc, Ww, w0, vlo, vhi = tile_geom(ti)
        emit_ln1x(ti)

        oc_order = list(range(16, 32)) + list(range(0, 16))
        for pi in range(16):
            oc0 = oc_order[2 * pi]
            par = slab["inw"] % 3
            slab["inw"] += 1
            ibuf = INWB if par < 2 else INWB3
            ibase = par * 2048 if par < 2 else 0
            dma("sync", f"d_inw{par}",
                ibuf[:, ibase:ibase + 2048],
                inw_scr[oc0 // 2],
                r=[DRAM1("inw_scr")], w=[ibuf.iv(ibase, ibase + 2048)])
            for sub in range(2):
                oc = oc0 + sub
                bank = rot()
                for k in range(NCH):
                    pl.emit("pe", lambda e, ibuf=ibuf, ibase=ibase, sub=sub,
                            k=k, bank=bank, Ww=Ww: e.matmul(
                                P[bank][0:128, 0:Ww],
                                ibuf[:, ibase + k * 256 + sub * 128:
                                     ibase + k * 256 + sub * 128 + 128],
                                ZW[:, k * CH:k * CH + Ww],
                                start=(k == 0), stop=(k == NCH - 1)),
                            r=[ibuf.iv(ibase, ibase + 2048),
                               ZW.iv(k * CH, k * CH + Ww)],
                            w=[piv(bank, 0, Ww)])
                if oc >= 16:
                    cc = oc - 16
                    pl.emit("act", lambda e, cc=cc, oc=oc, bank=bank, Ww=Ww:
                            e.activation(out=VAL[:, cc * CH:cc * CH + Ww],
                                         in_=P[bank][0:128, 0:Ww], func=AF.Identity,
                                         bias=INB[:, oc:oc + 1]),
                            r=[piv(bank, 0, Ww), INB.iv()],
                            w=[VAL.iv(cc * CH, cc * CH + Ww)])
                else:
                    pl.emit("act", lambda e, oc=oc, bank=bank, Wc=Wc:
                            e.activation(out=GATE[:, oc * CH:oc * CH + Wc],
                                         in_=P[bank][0:128, 4:4 + Wc],
                                         func=AF.Sigmoid, bias=INB[:, oc:oc + 1]),
                            r=[piv(bank, 0, CH), INB.iv()],
                            w=[GATE.iv(oc * CH, oc * CH + Wc)])
        if vlo > 0:
            pl.emit("pool", lambda e, vlo=vlo: e.memset(
                VAL.rearrange("p (c t) -> p c t", t=CH)[:, :, 0:vlo], 0.0),
                w=[VAL.iv()])
        if vhi < Ww:
            pl.emit("pool", lambda e, vhi=vhi, Ww=Ww: e.memset(
                VAL.rearrange("p (c t) -> p c t", t=CH)[:, :, vhi:Ww], 0.0),
                w=[VAL.iv()])
        dg_base[0] = slab["dg"]
        for c in range(3):
            par = slab["dg"] % 3
            slab["dg"] += 1
            dma("sync", f"d_dg{par}",
                DGB[:, par * 1152:(par + 1) * 1152], dg_scr[c],
                r=[DRAM1("dg_scr")], w=[DGB.iv(par * 1152, (par + 1) * 1152)])

    emit_xload(0)
    emit_early_cast(0)
    emit_early(0)
    emit_s3(0)
    prev_wc = WC
    for ti in range(n_tiles):
        s, Wc, Ww, w0, vlo, vhi = tile_geom(ti)

        namm = [0]
        for c in range(16):
            for j in range(4):
                namm[0] += 1
                first = namm[0] == 1
                last = namm[0] == 64
                d = j - 1
                pl.emit("pe", lambda e, c=c, j=j, d=d, first=first, last=last,
                        Wc=Wc: e.matmul(
                            P[7][0:16, 0:Wc],
                            AJ[:, (c * 4 + j) * 16:(c * 4 + j + 1) * 16],
                            VAL[:, c * CH + 4 + d:c * CH + 4 + d + Wc],
                            start=first, stop=last),
                        r=[AJ.iv(), VAL.iv(c * CH, c * CH + CH)],
                        w=[piv(7, 0, Wc)])
        pl.emit("dve", lambda e, Wc=Wc: e.tensor_copy(
            out=ALOW[0:16, 0:Wc], in_=P[7][0:16, 0:Wc]),
            r=[piv(7, 0, Wc)], w=[ALOW.iv(0, Wc)])

        for c in range(16):
            par = (dg_base[0] + c) % 3
            if c >= 3:
                slab["dg"] += 1
                dma("sync", f"d_dg{par}",
                    DGB[:, par * 1152:(par + 1) * 1152], dg_scr[c],
                    r=[DRAM1("dg_scr")], w=[DGB.iv(par * 1152, (par + 1) * 1152)])
            bank = rot()
            for j in range(9):
                d = j - 4
                pl.emit("pe", lambda e, c=c, j=j, d=d, par=par, bank=bank,
                        Wc=Wc: e.matmul(
                            P[bank][0:128, 0:Wc],
                            DGB[:, par * 1152 + 128 * j:par * 1152 + 128 * (j + 1)],
                            VAL[:, c * CH + 4 + d:c * CH + 4 + d + Wc],
                            start=(j == 0), stop=False),
                        r=[DGB.iv(par * 1152, (par + 1) * 1152),
                           VAL.iv(c * CH, c * CH + CH)],
                        w=[piv(bank, 0, Wc)])
            pl.emit("pe", lambda e, c=c, bank=bank, Wc=Wc: e.matmul(
                P[bank][0:128, 0:Wc], BIAS[32:33, 128 * c:128 * (c + 1)],
                ONESROW[32:33, 0:Wc], start=False, stop=True),
                r=[BIAS.iv(), ONESROW.iv()], w=[piv(bank, 0, Wc)])
            pl.emit("act", lambda e, bank=bank, Wc=Wc: e.activation(
                out=TMPS[:, 0:Wc], in_=P[bank][0:128, 0:Wc],
                func=AF.Sigmoid),
                r=[piv(bank, 0, Wc)], w=[TMPS.iv(0, Wc)])
            pl.emit("dve", lambda e, c=c, bank=bank, Wc=Wc: e.tensor_tensor(
                out=SILU[:, c * CH:c * CH + Wc], in0=P[bank][0:128, 0:Wc],
                in1=TMPS[:, 0:Wc], op=ALU.mult),
                r=[piv(bank, 0, Wc), TMPS.iv(0, Wc)],
                w=[SILU.iv(c * CH, c * CH + Wc)])

        if dbg and ti == DBG_TI:
            for nm, buf in [("val", VAL), ("gate", GATE), ("silu", SILU),
                            ("alow", ALOW), ("zw", ZW), ("ln1x", LN1X),
                            ("g", G), ("bias", BIAS)]:
                rr = buf.t.shape[0]
                if nm == "bias":
                    rr = 1
                cc = min(buf.ncols, 4096)
                pl.emit("dve", lambda e, buf=buf, rr=rr, cc=cc: e.tensor_copy(
                    out=TF[0:rr, 0:cc], in_=buf[0:rr, 0:cc]),
                    r=[buf.iv()], w=[TF.iv()], tag="dbgc" + nm)
                dma("sync", "d_o", dbt[nm][:, :], TF[0:rr, 0:cc],
                    r=[TF.iv()], w=[DRAM1("dbg_" + nm)])
        for ec in range(16):
            par = slab["m"] % 3
            slab["m"] += 1
            dma("sync", f"d_mb{par}",
                MB[:, par * 2048:(par + 1) * 2048],
                m_scr[ec],
                r=[DRAM1("m_scr")], w=[MB.iv(par * 2048, (par + 1) * 2048)])
            bank = rot()
            pl.emit("pe", lambda e, ec=ec, bank=bank, Wc=Wc: e.matmul(
                P[bank][0:128, 0:Wc], G[0:16, 128 * ec:128 * (ec + 1)],
                ALOW[0:16, 0:Wc], start=True, stop=False),
                r=[G.iv(), ALOW.iv(0, Wc)], w=[piv(bank, 0, Wc)])
            for pc in range(16):
                pl.emit("pe", lambda e, pc=pc, par=par, bank=bank, Wc=Wc:
                        e.matmul(
                            P[bank][0:128, 0:Wc],
                            MB[:, par * 2048 + pc * 128:
                               par * 2048 + pc * 128 + 128],
                            SILU[:, pc * CH:pc * CH + Wc],
                            start=False, stop=False),
                        r=[MB.iv(par * 2048, (par + 1) * 2048),
                           SILU.iv(pc * CH, pc * CH + Wc)],
                        w=[piv(bank, 0, Wc)])
            pl.emit("pe", lambda e, ec=ec, bank=bank, Wc=Wc: e.matmul(
                P[bank][0:128, 0:Wc], BIAS[0:1, 128 * ec:128 * (ec + 1)],
                ONESROW[0:1, 0:Wc], start=False, stop=True),
                r=[BIAS.iv(), ONESROW.iv()], w=[piv(bank, 0, Wc)])
            pl.emit("dve", lambda e, ec=ec, bank=bank, Wc=Wc: e.tensor_tensor(
                out=YG[:, ec * CH:ec * CH + Wc], in0=P[bank][0:128, 0:Wc],
                in1=GATE[:, ec * CH:ec * CH + Wc], op=ALU.mult),
                r=[piv(bank, 0, Wc), GATE.iv(ec * CH, ec * CH + Wc)],
                w=[YG.iv(ec * CH, ec * CH + Wc)])

        for dc in range(8):
            par = slab["ow"] % 2
            slab["ow"] += 1
            dma("sync", f"d_ow{par}",
                OWB[:, par * 2048:(par + 1) * 2048],
                ow_scr[dc],
                r=[DRAM1("ow_scr")], w=[OWB.iv(par * 2048, (par + 1) * 2048)])
            bank = rot()
            for ec in range(16):
                pl.emit("pe", lambda e, ec=ec, par=par, bank=bank, Wc=Wc:
                        e.matmul(P[bank][0:128, 0:Wc],
                                 OWB[:, par * 2048 + 128 * ec:
                                     par * 2048 + 128 * (ec + 1)],
                                 YG[:, ec * CH:ec * CH + Wc],
                                 start=(ec == 0), stop=False),
                        r=[OWB.iv(par * 2048, (par + 1) * 2048),
                           YG.iv(ec * CH, ec * CH + Wc)],
                        w=[piv(bank, 0, Wc)])
            pl.emit("pe", lambda e, dc=dc, bank=bank, Wc=Wc: e.matmul(
                P[bank][0:128, 0:Wc], IDENT[:, :],
                LN1X[:, dc * CH:dc * CH + Wc], start=False, stop=True),
                r=[IDENT.iv(), LN1X.iv(dc * CH, dc * CH + Wc)],
                w=[piv(bank, 0, Wc)])
            pl.emit("act", lambda e, dc=dc, bank=bank, Wc=Wc: e.activation(
                out=ATTN[:, dc * CH:dc * CH + Wc], in_=P[bank][0:128, 0:Wc],
                func=AF.Identity, bias=OUTB[:, dc:dc + 1]),
                r=[piv(bank, 0, Wc), OUTB.iv()],
                w=[ATTN.iv(dc * CH, dc * CH + Wc)])

        if dbg and ti == DBG_TI:
            dma("sync", "d_o", dbt["attn"][:, :], ATTN[:, :],
                r=[ATTN.iv()], w=[DRAM1("dbg_attn")])
        norm_stats(ATTN, Wc, do_ln1=False, cast_eng="act")
        for c in range(NCH):
            pl.emit("dve", lambda e, c=c, Wc=Wc: e.tensor_tensor(
                out=TF[:, c * CH:c * CH + Wc], in0=ATTN[:, c * CH:c * CH + Wc],
                in1=P[2][0:128, 0:Wc], op=ALU.subtract),
                r=[ATTN.iv(c * CH, c * CH + Wc), piv(2, 0, Wc)],
                w=[TF.iv(c * CH, c * CH + Wc)])
            pl.emit("dve", lambda e, c=c, Wc=Wc: e.tensor_tensor(
                out=UA[:, c * CH:c * CH + Wc], in0=TF[:, c * CH:c * CH + Wc],
                in1=P[7][0:128, 0:Wc], op=ALU.mult),
                r=[TF.iv(c * CH, c * CH + Wc), piv(7, 0, Wc)],
                w=[UA.iv(c * CH, c * CH + Wc)])
            pl.emit("pool", lambda e, c=c, Wc=Wc: e.tensor_tensor(
                out=X1[:, c * CH:c * CH + Wc], in0=UA[:, c * CH:c * CH + Wc],
                in1=XW[:, c * CH + 4:c * CH + 4 + Wc], op=ALU.add),
                r=[UA.iv(c * CH, c * CH + Wc), XW.iv(c * CH, c * CH + CH)],
                w=[X1.iv(c * CH, c * CH + Wc)])

        if ti + 1 < n_tiles:
            emit_xload(ti + 1)

        if ti > 0:
            pl.emit("pool", lambda e: e.tensor_copy(
                out=WARN.rearrange("p (c t) -> p c t", t=CH)[:, :, 0:1],
                in_=WCARRY.rearrange("p (c o) -> p c o", o=1)),
                r=[WCARRY.iv()], w=[WARN.iv()])
        norm_stats(X1, Wc, do_ln1=False, cast_eng="act")

        def emit_mix(c, Wc):
            pl.emit("pool", lambda e, c=c, Wc=Wc: e.tensor_tensor(
                out=UA[:, c * CH:c * CH + Wc],
                in0=WARN[:, c * CH + 1:c * CH + 1 + Wc],
                in1=WARN[:, c * CH:c * CH + Wc], op=ALU.subtract),
                r=[WARN.iv(c * CH, c * CH + 1 + Wc)],
                w=[UA.iv(c * CH, c * CH + Wc)])

        def emit_xk(c, Wc):
            pl.emit("dve", lambda e, c=c, Wc=Wc: e.scalar_tensor_tensor(
                out=XK[:, c * CH:c * CH + Wc], in0=UA[:, c * CH:c * CH + Wc],
                scalar=MUK[:, c:c + 1], in1=WARN[:, c * CH:c * CH + Wc],
                op0=ALU.mult, op1=ALU.add),
                r=[UA.iv(c * CH, c * CH + Wc), MUK.iv(),
                   WARN.iv(c * CH, c * CH + Wc)],
                w=[XK.iv(c * CH, c * CH + Wc)])

        def emit_xr(c, Wc):
            pl.emit("dve", lambda e, c=c, Wc=Wc: e.scalar_tensor_tensor(
                out=XR[:, c * CH:c * CH + Wc], in0=UA[:, c * CH:c * CH + Wc],
                scalar=MUR[:, c:c + 1], in1=WARN[:, c * CH:c * CH + Wc],
                op0=ALU.mult, op1=ALU.add),
                r=[UA.iv(c * CH, c * CH + Wc), MUR.iv(),
                   WARN.iv(c * CH, c * CH + Wc)],
                w=[XR.iv(c * CH, c * CH + Wc)])

        for c in range(NCH):
            pl.emit("dve", lambda e, c=c, Wc=Wc: e.tensor_tensor(
                out=TF[:, c * CH:c * CH + Wc], in0=X1[:, c * CH:c * CH + Wc],
                in1=P[2][0:128, 0:Wc], op=ALU.subtract),
                r=[X1.iv(c * CH, c * CH + Wc), piv(2, 0, Wc)],
                w=[TF.iv(c * CH, c * CH + Wc)])
            pl.emit("dve", lambda e, c=c, Wc=Wc: e.tensor_tensor(
                out=WARN[:, c * CH + 1:c * CH + 1 + Wc],
                in0=TF[:, c * CH:c * CH + Wc],
                in1=P[7][0:128, 0:Wc], op=ALU.mult),
                r=[TF.iv(c * CH, c * CH + Wc), piv(7, 0, Wc)],
                w=[WARN.iv(c * CH + 1, c * CH + 1 + Wc)])
            emit_mix(c, Wc)
            if c >= 1:
                emit_xk(c - 1, Wc)
        emit_xk(NCH - 1, Wc)
        for c in range(NCH):
            emit_xr(c, Wc)
        prev_wc = Wc

        pl.emit("pool", lambda e, Wc=Wc: e.tensor_copy(
            out=WCARRY.rearrange("p (c o) -> p c o", o=1),
            in_=WARN.rearrange("p (c t) -> p c t", t=CH)[:, :, Wc:Wc + 1]),
            r=[WARN.iv()], w=[WCARRY.iv()])

        def ffn_pass(scr_name, rhs_buf, evac):
            for q in range(4):
                par = slab["fb"] % 3
                slab["fb"] += 1
                dma("sync", f"d_fb{par}",
                    FBB[:, par * 2048:(par + 1) * 2048],
                    ffn_scr[scr_name][128 * q:128 * (q + 1), :],
                    r=[DRAM1(scr_name + "_scr")],
                    w=[FBB.iv(par * 2048, (par + 1) * 2048)])
                for ds in range(2):
                    dch = 2 * q + ds
                    bank = rot()
                    for c in range(NCH):
                        pl.emit("pe", lambda e, par=par, ds=ds, c=c, bank=bank,
                                Wc=Wc, rhs_buf=rhs_buf: e.matmul(
                                    P[bank][0:128, 0:Wc],
                                    FBB[:, par * 2048 + c * 256 + ds * 128:
                                        par * 2048 + c * 256 + ds * 128 + 128],
                                    rhs_buf[:, c * CH:c * CH + Wc],
                                    start=(c == 0), stop=(c == NCH - 1)),
                                r=[FBB.iv(par * 2048, (par + 1) * 2048),
                                   rhs_buf.iv(c * CH, c * CH + Wc)],
                                w=[piv(bank, 0, Wc)])
                    evac(dch, bank)

        def r_evac(dch, bank):
            pl.emit("act", lambda e, dch=dch, bank=bank, Wc=Wc: e.activation(
                out=RS[:, dch * CH:dch * CH + Wc], in_=P[bank][0:128, 0:Wc],
                func=AF.Sigmoid),
                r=[piv(bank, 0, Wc)], w=[RS.iv(dch * CH, dch * CH + Wc)])

        def k_evac(dch, bank):
            pl.emit("act", lambda e, dch=dch, bank=bank, Wc=Wc: e.activation(
                out=KRELU[:, dch * CH:dch * CH + Wc], in_=P[bank][0:128, 0:Wc],
                func=AF.Relu),
                r=[piv(bank, 0, Wc)], w=[KRELU.iv(dch * CH, dch * CH + Wc)])
            pl.emit("dve", lambda e, dch=dch, Wc=Wc: e.tensor_tensor(
                out=KACT[:, dch * CH:dch * CH + Wc],
                in0=KRELU[:, dch * CH:dch * CH + Wc],
                in1=KRELU[:, dch * CH:dch * CH + Wc], op=ALU.mult),
                r=[KRELU.iv(dch * CH, dch * CH + Wc)],
                w=[KACT.iv(dch * CH, dch * CH + Wc)])

        def v_evac(dch, bank):
            pl.emit("dve", lambda e, dch=dch, bank=bank, Wc=Wc: e.tensor_tensor(
                out=FFO[:, dch * CH:dch * CH + Wc], in0=P[bank][0:128, 0:Wc],
                in1=RS[:, dch * CH:dch * CH + Wc], op=ALU.mult),
                r=[piv(bank, 0, Wc), RS.iv(dch * CH, dch * CH + Wc)],
                w=[FFO.iv(dch * CH, dch * CH + Wc)])

        ffn_pass("wk", XK, k_evac)
        ffn_pass("wr", XR, r_evac)
        if ti + 1 < n_tiles:
            emit_early_cast(ti + 1)
            emit_early(ti + 1)
        ffn_pass("wv", KACT, v_evac)

        if dbg and ti == DBG_TI:
            pl.emit("dve", lambda e: e.tensor_copy(
                out=TF[:, 0:4096], in_=XR[:, :]), r=[XR.iv()], w=[TF.iv()])
            dma("sync", "d_o", dbt["xr"][:, :], TF[:, 0:4096],
                r=[TF.iv()], w=[DRAM1("dbg_xr")])
            dma("sync", "d_o", dbt["ffo"][:, :], FFO[:, :],
                r=[FFO.iv()], w=[DRAM1("dbg_ffo")])
        norm_stats(FFO, Wc, do_ln1=False, qb=SQB2, qs=SQS2, stage=1)
        if ti + 1 < n_tiles:
            emit_s3(ti + 1)

        norm_stats(FFO, Wc, do_ln1=False, qb=SQB2, qs=SQS2, mb_=0, sb_=1,
                   stage=2)
        for c in range(NCH):
            pl.emit("dve", lambda e, c=c, Wc=Wc: e.tensor_tensor(
                out=T4P[c % 2][:, 0:Wc], in0=FFO[:, c * CH:c * CH + Wc],
                in1=P[0][0:128, 0:Wc], op=ALU.subtract),
                r=[FFO.iv(c * CH, c * CH + Wc), piv(0, 0, Wc)],
                w=[T4P[c % 2].iv()])
            pl.emit("dve", lambda e, c=c, Wc=Wc: e.tensor_tensor(
                out=X2[:, c * CH:c * CH + Wc], in0=T4P[c % 2][:, 0:Wc],
                in1=P[1][0:128, 0:Wc], op=ALU.mult),
                r=[T4P[c % 2].iv(), piv(1, 0, Wc)],
                w=[X2.iv(c * CH, c * CH + Wc)])
            pl.emit("pool", lambda e, c=c, Wc=Wc: e.tensor_tensor(
                out=X2[:, c * CH:c * CH + Wc], in0=X2[:, c * CH:c * CH + Wc],
                in1=X1[:, c * CH:c * CH + Wc], op=ALU.add),
                r=[X2.iv(c * CH, c * CH + Wc), X1.iv(c * CH, c * CH + Wc)],
                w=[X2.iv(c * CH, c * CH + Wc)])

        dma("act", "d_o",
            outdr.rearrange("(c p) t -> p c t", p=128)[:, :, s:s + Wc],
            X2.rearrange("p (c t) -> p c t", t=CH)[:, :, 0:Wc],
            r=[X2.iv()], w=[DRAM1("out")], tag=f"out{ti}")

    pl.emit("sync", lambda e: e.nop(), r=[DRAM1("out")])
    pl.finalize()

    with contextlib.ExitStack() as st:
        sems = {}
        for name in ["d_ld", "d_ldp", "d_dbg", "d_sf", "d_siw",
                     "d_sm0", "d_sm1", "d_sd0", "d_sd1", "d_fuse",
                     "d_hbw", "d_pww", "d_ha0", "d_ha1",
                     "d_inw0", "d_inw1", "d_inw2", "d_mb0", "d_mb1", "d_mb2", "d_ow0", "d_ow1",
                     "d_fb0", "d_fb1", "d_fb2", "d_dg0", "d_dg1", "d_dg2",
                     "d_xw", "d_o",
                     "e_pe", "e_act", "e_dve", "e_pool", "e_sync"]:
            sems[name] = st.enter_context(nc.semaphore(name))
        block = st.enter_context(nc.Block())
        block.tensor(lambda e: pl.run("pe", e, sems))
        block.scalar(lambda e: pl.run("act", e, sems))
        block.vector(lambda e: pl.run("dve", e, sems))
        block.gpsimd(lambda e: pl.run("pool", e, sems))
        block.sync(lambda e: pl.run("sync", e, sems))

    return nc


def _host_inputs(inp):
    asc = np.ascontiguousarray
    f = np.float32

    def colchunks(v, nch):
        return asc(np.asarray(v, f).reshape(nch, 128).T)

    w = {}
    w["inwT"] = asc(np.asarray(inp["L0_attn_in_w"], f).T
                    .reshape(8, 128, 16, 256).transpose(2, 1, 0, 3)
                    .reshape(2048, 2048))
    w["normw"] = colchunks(inp["L0_attn_norm_w"], NCH)
    w["inb"] = colchunks(inp["L0_attn_in_b"], 32)
    dk = np.asarray(inp["L0_attn_dwa_k"], f)[:, 0, :]
    w["dwataps"] = asc(dk.reshape(16, 128, 4).transpose(1, 0, 2).reshape(128, 64))
    w["dwab"] = colchunks(inp["L0_attn_dwa_b"], 16)
    w["abuf"] = asc(np.asarray(inp["L0_attn_A"], f))
    w["bmT"] = asc(np.asarray(inp["L0_attn_Bm"], f).T)
    sk = np.asarray(inp["L0_attn_sym_k"], f)[:, 0, :]
    w["symtaps"] = asc(sk.reshape(16, 128, 9).transpose(1, 0, 2).reshape(128, 144))
    w["symbrow"] = asc(np.asarray(inp["L0_attn_sym_b"], f).reshape(1, DI))
    w["pww"] = asc(np.asarray(inp["L0_attn_pw_w"], f))
    w["pwb"] = colchunks(inp["L0_attn_pw_b"], 16)
    w["hawT"] = asc(np.asarray(inp["L0_attn_ha_w"], f).T)
    w["hab"] = colchunks(inp["L0_attn_ha_b"], 8)
    w["hbw"] = asc(np.asarray(inp["L0_attn_hb_w"], f))
    w["hbb"] = colchunks(inp["L0_attn_hb_b"], 8)
    w["fuseT"] = asc(np.asarray(inp["L0_attn_fuse_w"], f).T)
    w["fbrow"] = asc(np.asarray(inp["L0_attn_fuse_b"], f).reshape(1, DI))
    w["outwT"] = asc(np.asarray(inp["L0_attn_out_w"], f).T
                     .reshape(16, 128, 8, 128).transpose(2, 1, 0, 3)
                     .reshape(1024, 2048))
    w["outb"] = colchunks(inp["L0_attn_out_b"], 8)
    def _ffnlay(a):
        return asc(np.asarray(a, f).T.reshape(8, 128, 4, 256)
                   .transpose(2, 1, 0, 3).reshape(512, 2048))
    w["wrT"] = _ffnlay(inp["L0_ffn_W_r"])
    w["wkT"] = _ffnlay(inp["L0_ffn_W_k"])
    w["wvT"] = _ffnlay(inp["L0_ffn_W_v"])
    w["mur"] = colchunks(inp["L0_ffn_mu_r"], 8)
    w["muk"] = colchunks(inp["L0_ffn_mu_k"], 8)
    w["ident"] = np.eye(128, dtype=ml_dtypes.bfloat16)
    return w


def kernel(**inp):
    x = np.asarray(inp["L0_attn_x"], np.float32)
    B, L, _ = x.shape
    w = _host_inputs(inp)
    nc = build_nc(L)
    in_maps = []
    for b in range(B):
        m = dict(w)
        m["x"] = np.ascontiguousarray(x[b].T)
        in_maps.append(m)
    res = run_bass_kernel_spmd(nc, in_maps, core_ids=list(range(B)))
    out = np.stack([np.ascontiguousarray(r["out"].T) for r in res.results])
    return out.astype(np.float32)


# version 24
# speedup vs baseline: 1.0664x; 1.0459x over previous
import contextlib
import numpy as np
import ml_dtypes
import concourse.bass as bass
import concourse.mybir as mybir
from concourse.bass_utils import run_bass_kernel_spmd

F32 = mybir.dt.float32
BF16 = mybir.dt.bfloat16
AF = mybir.ActivationFunctionType
ALU = mybir.AluOpType

D = 1024
DI = 2048
NCH = 8
WC = 504
CH = 512
EPS_LN = 1e-5
DBG_TI = 0

ENG_SEM = {"pe": "e_pe", "act": "e_act", "dve": "e_dve", "pool": "e_pool",
           "sync": "e_sync"}


class Op:
    __slots__ = ("idx", "eng", "fn", "deps", "inc", "waits", "is_dma",
                 "dma_sem", "marker", "tag")


class Plan:

    ALWAYS_FINAL = {"d_ld", "d_ldp", "d_sf", "d_siw", "d_fuse", "d_hbw",
                    "d_pww"}
    AFTER_LAST = {"d_sm0", "d_sm1",
                  "d_sd0", "d_sd1"}

    def __init__(self):
        self.ops = []
        self.spaces = {}

    def emit(self, eng, fn, r=(), w=(), dma=None, tag=""):
        idx = len(self.ops)
        deps = set()
        for iv in r:
            sp = self.spaces.setdefault(iv[0], [])
            lo, hi = iv[1], iv[2]
            for e in sp:
                if e[0] < hi and lo < e[1]:
                    if e[2] is not None:
                        deps.add(e[2])
                    e[3].append(idx)
        for iv in w:
            sp = self.spaces.setdefault(iv[0], [])
            lo, hi = iv[1], iv[2]
            keep = []
            for e in sp:
                if e[0] < hi and lo < e[1]:
                    if e[2] is not None:
                        deps.add(e[2])
                    deps.update(e[3])
                    if e[0] < lo:
                        keep.append([e[0], lo, e[2], list(e[3])])
                    if hi < e[1]:
                        keep.append([hi, e[1], e[2], list(e[3])])
                else:
                    keep.append(e)
            keep.append([lo, hi, idx, []])
            self.spaces[iv[0]] = keep
        op = Op()
        op.idx = idx
        op.eng = eng
        op.fn = fn
        op.deps = deps
        op.is_dma = dma is not None
        op.dma_sem = dma
        op.tag = tag
        self.ops.append(op)
        return idx

    def finalize(self):
        referenced = set()
        for op in self.ops:
            pruned = set()
            for d in op.deps:
                p = self.ops[d]
                if (not p.is_dma) and p.eng == op.eng:
                    continue
                pruned.add(d)
            op.deps = pruned
            referenced.update(pruned)
        ctr = {}
        for op in self.ops:
            if op.is_dma:
                ctr[op.dma_sem] = ctr.get(op.dma_sem, 0) + 16
                op.marker = (op.dma_sem, ctr[op.dma_sem])
                op.inc = (op.dma_sem, 16)
            elif op.idx in referenced:
                s = ENG_SEM[op.eng]
                ctr[s] = ctr.get(s, 0) + 1
                op.marker = (s, ctr[s])
                op.inc = (s, 1)
            else:
                op.marker = None
                op.inc = None
        self.eng_ops = {}
        for op in self.ops:
            self.eng_ops.setdefault(op.eng, []).append(op)
        totals = dict(ctr)
        last_dma = {}
        for op in self.ops:
            if op.is_dma:
                last_dma[op.dma_sem] = op.idx
        for eng, ops in self.eng_ops.items():
            observed = {}
            for op in ops:
                ws = {}
                for d in op.deps:
                    s, v = self.ops[d].marker
                    if s in self.ALWAYS_FINAL or (
                            s in self.AFTER_LAST and op.idx > last_dma[s]):
                        v = totals[s]
                    if observed.get(s, 0) < v:
                        ws[s] = max(ws.get(s, 0), v)
                for s, v in ws.items():
                    observed[s] = v
                op.waits = sorted(ws.items())

    def run(self, eng_name, eng, sems):
        for op in self.eng_ops.get(eng_name, []):
            for s, v in op.waits:
                eng.wait_ge(sems[s], v)
            inst = op.fn(eng)
            if op.inc is not None:
                inst.then_inc(sems[op.inc[0]], op.inc[1])


class SBuf:
    def __init__(self, nc, name, shape, dtype, off):
        self.t = nc.alloc_sbuf_tensor_at(name, shape, dtype, offset=off)
        self.off = off
        self.esz = 2 if dtype == BF16 else 4
        n = 1
        for s in shape[1:]:
            n *= s
        self.ncols = n

    def __getitem__(self, key):
        return self.t[key]

    def rearrange(self, *a, **k):
        return self.t.rearrange(*a, **k)

    def iv(self, a=0, b=None):
        if b is None:
            b = self.ncols
        return ("sbuf", self.off + a * self.esz, self.off + b * self.esz)


def build_nc(L, dbg=False):
    n_tiles = (L + WC - 1) // WC
    nc = bass.Bass("TRN2", target_bir_lowering=False)

    def din(name, shape, dt=F32):
        return nc.dram_tensor(name, shape, dt, kind="ExternalInput")

    xdr = din("x", [D, L])
    inwT = din("inwT", [2048, 2048])
    normw = din("normw", [128, NCH])
    inb = din("inb", [128, 32])
    dwataps = din("dwataps", [128, 64])
    dwab = din("dwab", [128, 16])
    abuf = din("abuf", [DI, 16])
    bmT = din("bmT", [DI, 16])
    symtaps = din("symtaps", [128, 144])
    symbrow = din("symbrow", [1, DI])
    pww = din("pww", [DI, DI])
    pwb = din("pwb", [128, 16])
    hawT = din("hawT", [DI, D])
    hab = din("hab", [128, 8])
    hbw = din("hbw", [D, DI])
    hbb = din("hbb", [128, 8])
    fuseT = din("fuseT", [DI, DI])
    fbrow = din("fbrow", [1, DI])
    outwT = din("outwT", [1024, 2048])
    outb = din("outb", [128, 8])
    wrT = din("wrT", [4 * 128, 2048])
    wkT = din("wkT", [4 * 128, 2048])
    wvT = din("wvT", [4 * 128, 2048])
    mur = din("mur", [128, 8])
    muk = din("muk", [128, 8])
    ident_dr = din("ident", [128, 128], BF16)

    outdr = nc.dram_tensor("out", [D, L], F32, kind="ExternalOutput")
    dbt = {}
    if dbg:
        for nm, rows, cols in [("val", 128, 4096), ("gate", 128, 4096),
                               ("silu", 128, 4096), ("alow", 16, 512),
                               ("yg", 128, 4096), ("attn", 128, 4096),
                               ("x1", 128, 4096), ("war", 128, 4096),
                               ("xr", 128, 4096), ("ffo", 128, 4096),
                               ("zw", 128, 4096), ("ln1x", 128, 4096),
                               ("g", 16, 2048), ("bias", 1, 2048),
                               ("t2", 128, 4096), ("fuseb", 128, 4096),
                               ("esqb", 128, 4096), ("erows", 3, 512),
                               ("ezw", 128, 4096), ("exw", 128, 4096)]:
            dbt[nm] = nc.dram_tensor("dbg_" + nm, [rows, cols], F32,
                                     kind="ExternalOutput")

    inw_scr = nc.dram_tensor("inw_scr", [16, 128, 2048], BF16, kind="Internal")
    m_scr = nc.dram_tensor("m_scr", [16, 128, 2048], BF16, kind="Internal")
    ow_scr = nc.dram_tensor("ow_scr", [8, 128, 2048], BF16, kind="Internal")
    wr_scr = nc.dram_tensor("wr_scr", [4 * 128, 2048], BF16, kind="Internal")
    wk_scr = nc.dram_tensor("wk_scr", [4 * 128, 2048], BF16, kind="Internal")
    wv_scr = nc.dram_tensor("wv_scr", [4 * 128, 2048], BF16, kind="Internal")
    dg_scr = nc.dram_tensor("dg_scr", [16, 128, 1152], BF16, kind="Internal")
    ffn_scr = {"wr": wr_scr, "wk": wk_scr, "wv": wv_scr}

    off = 16512

    def nxt(sz):
        nonlocal off
        o = off
        off += (sz + 31) & ~31
        return o

    CONSTS = SBuf(nc, "consts", [128, 8], F32, nxt(32))
    ONESC = SBuf(nc, "onesc", [128, 1], BF16, nxt(32))
    ONES128 = SBuf(nc, "ones128", [1, 128], BF16, nxt(256))
    ONESROW = SBuf(nc, "onesrow", [33, CH], BF16, nxt(1024))
    IDENT = SBuf(nc, "identsb", [128, 128], BF16, nxt(256))
    NORMW = SBuf(nc, "normwsb", [128, NCH], F32, nxt(32))
    INB = SBuf(nc, "inbsb", [128, 32], F32, nxt(128))
    OUTB = SBuf(nc, "outbsb", [128, 8], F32, nxt(32))
    MUR = SBuf(nc, "mursb", [128, 8], F32, nxt(32))
    MUK = SBuf(nc, "muksb", [128, 8], F32, nxt(32))
    DWAT = SBuf(nc, "dwatsb", [128, 64], F32, nxt(256))
    SYMT = SBuf(nc, "symtsb", [128, 144], BF16, nxt(288))
    ABF = SBuf(nc, "abfsb", [128, 256], BF16, nxt(512))
    AJ = SBuf(nc, "ajsb", [128, 1024], BF16, nxt(2048))
    G = SBuf(nc, "gsb", [16, DI], BF16, nxt(4096))
    BIAS = SBuf(nc, "biassb", [33, DI], BF16, nxt(4096))
    ALOW = SBuf(nc, "alowsb", [16, CH], BF16, nxt(1024))
    DWABBF = SBuf(nc, "dwabbf", [128, 16], BF16, nxt(32))
    HABBF = SBuf(nc, "habbf", [128, 8], BF16, nxt(32))
    HBBBF = SBuf(nc, "hbbbf", [128, 8], BF16, nxt(32))
    PWBBF = SBuf(nc, "pwbbf", [128, 16], BF16, nxt(32))
    VA = SBuf(nc, "vasb", [16, 1], BF16, nxt(32))
    INVNW = SBuf(nc, "invnw", [128, 8], F32, nxt(32))
    CONST1 = SBuf(nc, "const1", [128, 8], F32, nxt(32))
    WCARRY = SBuf(nc, "wcarry", [128, 8], F32, nxt(32))
    MROW = SBuf(nc, "mrow", [1, CH], BF16, nxt(1024))
    SROW1 = SBuf(nc, "srow1", [1, CH], BF16, nxt(1024))
    SROW2 = SBuf(nc, "srow2", [1, CH], BF16, nxt(1024))
    SROW3 = SBuf(nc, "srow3", [1, CH], BF16, nxt(1024))
    MSQR = SBuf(nc, "msqr", [1, CH], F32, nxt(2048))
    VROW = SBuf(nc, "vrow", [1, CH], F32, nxt(2048))
    LROW = SBuf(nc, "lrow", [1, CH], F32, MSQR.off)

    AR = 16384
    o_XW = nxt(AR)
    o_VALSQ = nxt(AR)
    o_R1 = nxt(AR)
    o_R2 = nxt(AR)
    o_R3 = nxt(AR)
    o_R4 = nxt(AR)
    o_X1 = nxt(AR)
    o_WAR = nxt(AR)

    XW = SBuf(nc, "xw", [128, 4096], F32, o_XW)
    LROW2 = SBuf(nc, "lrow2", [1, CH], F32, o_VALSQ + 8192)
    VAL = SBuf(nc, "val", [128, 8192], BF16, o_VALSQ)
    SQB = SBuf(nc, "sqb", [128, 4096], BF16, o_VALSQ)
    SQS = SBuf(nc, "sqs", [128, 4096], BF16, o_VALSQ + 8192)
    TF = SBuf(nc, "tf", [128, 4096], F32, o_VALSQ)
    SILU = SBuf(nc, "silu", [128, 8192], BF16, o_R1)
    RS = SBuf(nc, "rs", [128, 4096], BF16, o_R1 + 8192)
    UA = SBuf(nc, "ua", [128, 4096], F32, o_R1)
    FFO = SBuf(nc, "ffo", [128, 4096], F32, o_R1)
    YG = SBuf(nc, "yg", [128, 8192], BF16, o_R2)
    KRELU = SBuf(nc, "krelu", [128, 4096], BF16, o_R2)
    KACT = SBuf(nc, "kact", [128, 4096], BF16, o_R2 + 8192)
    GATE = SBuf(nc, "gate", [128, 8192], BF16, o_R3)
    XR = SBuf(nc, "xr", [128, 4096], BF16, o_R3)
    XK = SBuf(nc, "xk", [128, 4096], BF16, o_R3 + 8192)
    U4 = SBuf(nc, "u4", [128, 4096], F32, o_R3)
    ATTN = SBuf(nc, "attn", [128, 4096], F32, o_R4)
    LN1X = SBuf(nc, "ln1x", [128, 4096], BF16, o_R4 + 8192)
    X2 = SBuf(nc, "x2", [128, 4096], F32, o_WAR)
    X1 = SBuf(nc, "x1", [128, 4096], F32, o_X1)
    WARN = SBuf(nc, "war", [128, 4096], F32, o_WAR)

    INWB = SBuf(nc, "inwb", [128, 2 * 2048], BF16, nxt(8192))
    INWB3 = SBuf(nc, "inwb3", [128, 2048], BF16, o_R2)
    MB = SBuf(nc, "mbuf", [128, 3 * 2048], BF16, nxt(12288))
    OWB = SBuf(nc, "owb", [128, 2 * 2048], BF16, nxt(8192))
    FBB = SBuf(nc, "fbb", [128, 3 * 2048], BF16, nxt(12288))
    DGB = SBuf(nc, "dgb", [128, 3 * 1152], BF16, nxt(6912))
    TMPS = SBuf(nc, "tmps", [128, 512], BF16, nxt(1024))
    SQB2 = SBuf(nc, "sqb2", [128, 4096], BF16, o_R4)
    FBB4 = SBuf(nc, "fbb4", [128, 2048], BF16, o_R4 + 8192)
    SQS2 = SBuf(nc, "sqs2", [128, 4096], BF16, MB.off)
    T4P = [SBuf(nc, f"t4p{i}", [128, 512], F32, OWB.off + 2048 * i)
           for i in range(2)]
    ZW = SBuf(nc, "zw2", [128, 4096], BF16, nxt(8192))
    tile_top = off
    assert tile_top <= 229344, f"SBUF over budget: {tile_top}"

    o_p = o_XW
    FUSEB = SBuf(nc, "fuseb", [128, 16 * 2048], BF16, o_p)
    HBWB = SBuf(nc, "hbwb", [128, 8 * 2048], BF16, o_p + 65536)
    HAB_ = SBuf(nc, "habuf", [128, 2 * 1024], BF16, o_p + 98304)
    T1 = SBuf(nc, "t1", [16, 1024], BF16, o_p + 102400)
    T1T = SBuf(nc, "t1t", [128, 128], BF16, o_p + 104448)
    BMTB = SBuf(nc, "bmtb", [128, 256], BF16, o_p + 104704)
    FBROW = SBuf(nc, "fbrowsb", [1, 2048], BF16, o_p + 105216)
    PWWB = SBuf(nc, "pwwb", [128, 16 * 2048], BF16, o_p)
    T2B = SBuf(nc, "t2b", [128, 16 * 2048], BF16, o_WAR)
    MPB = [SBuf(nc, f"mpb_{i}", [128, 2048], BF16, o_p + 65536 + i * 4096)
           for i in range(2)]

    P = [nc.alloc_psum_tensor(f"ps{i}", [128, 512], F32) for i in range(8)]
    PBF16 = P[7].bitcast(BF16)

    def piv(b, a=0, e=512):
        return ("psum", b * 2048 + a * 4, b * 2048 + e * 4)

    nc.const_aps.register(F32, EPS_LN, CONSTS[:, 0:1])
    nc.const_aps.register(F32, -0.5, CONSTS[:, 1:2])
    nc.const_aps.register(F32, 1.000001, CONSTS[:, 2:3])
    nc.const_aps.register(F32, 1e-11, CONSTS[:, 3:4])
    nc.const_aps.register(F32, 0.5, CONSTS[:, 4:5])
    civ = CONSTS.iv()

    pl = Plan()
    rot_state = [0]

    def rot():
        rot_state[0] = (rot_state[0] + 1) % 4
        return 3 + rot_state[0]

    def DRAM1(name):
        return (name, 0, 1)

    def dma(queue, sem, out_ap, in_ap, r, w, tag=""):
        return pl.emit(queue, lambda e, o=out_ap, i=in_ap: e.dma_start(out=o, in_=i),
                       r=r, w=w, dma=sem, tag=tag)

    for col, val in [(0, EPS_LN), (1, -0.5), (2, 1.000001), (3, 1e-11),
                     (4, 0.5)]:
        pl.emit("pool", lambda e, c=col, v=val: e.memset(CONSTS[:, c:c + 1], v),
                w=[CONSTS.iv(col, col + 1)])
    pl.emit("pool", lambda e: e.memset(ONESC[:, :], 1.0 / 1024.0), w=[ONESC.iv()])
    pl.emit("pool", lambda e: e.memset(ONES128[:, :], 1.0), w=[ONES128.iv()])
    pl.emit("pool", lambda e: e.memset(ONESROW[:, :], 1.0), w=[ONESROW.iv()])

    for dst, src in [(IDENT, ident_dr), (NORMW, normw), (INB, inb), (OUTB, outb),
                     (MUR, mur), (MUK, muk), (DWAT, dwataps)]:
        dma("sync", "d_ld", dst[:, :], src[:, :], r=[], w=[dst.iv()])
    dma("pool", "d_ldp", BIAS[32:33, :], symbrow[:, :], r=[], w=[BIAS.iv()])
    for dst, src in [(SYMT, symtaps),
                     (DWABBF, dwab), (HABBF, hab), (HBBBF, hbb), (PWBBF, pwb),
                     (FBROW, fbrow)]:
        dma("pool", "d_ldp", dst[:, :], src[:, :], r=[], w=[dst.iv()])
    dma("pool", "d_ldp", ABF.rearrange("p (c s) -> p c s", s=16),
        abuf.rearrange("(c p) s -> p c s", p=128), r=[], w=[ABF.iv()])
    dma("pool", "d_ldp", BMTB.rearrange("p (c s) -> p c s", s=16),
        bmT.rearrange("(c p) s -> p c s", p=128), r=[], w=[BMTB.iv()])

    pl.emit("act", lambda e: e.activation(
        out=CONST1[:, :], in_=NORMW[:, :], func=AF.Ln),
        r=[NORMW.iv(), civ], w=[CONST1.iv()])
    pl.emit("act", lambda e: e.activation(
        out=INVNW[:, :], in_=CONST1[:, :], func=AF.Exp, scale=-1.0),
        r=[CONST1.iv(), civ], w=[INVNW.iv()])

    for k in range(16):
        p = k % 2
        dma("pool", f"d_ha{p}", HAB_[:, p * 1024:(p + 1) * 1024],
            hawT[128 * k:128 * (k + 1), :],
            r=[], w=[HAB_.iv(p * 1024, (p + 1) * 1024)])
        for n in range(2):
            pl.emit("pe", lambda e, k=k, p=p, n=n: e.matmul(
                P[n][0:16, 0:512], BMTB[:, 16 * k:16 * (k + 1)],
                HAB_[:, p * 1024 + 512 * n: p * 1024 + 512 * (n + 1)],
                start=(k == 0), stop=(k == 15)),
                r=[HAB_.iv(p * 1024, (p + 1) * 1024), BMTB.iv()], w=[piv(n)])
    dma("pool", "d_fuse", FUSEB.rearrange("p (c q) -> p c q", q=2048),
        fuseT.rearrange("(c p) q -> p c q", p=128), r=[], w=[FUSEB.iv()])
    dma("pool", "d_hbw", HBWB.rearrange("p (c q) -> p c q", q=2048),
        hbw.rearrange("(c p) q -> p c q", p=128), r=[], w=[HBWB.iv()])

    for n in range(2):
        pl.emit("act", lambda e, n=n: e.copy(
            out=T1[0:16, 512 * n:512 * (n + 1)], in_=P[n][0:16, 0:512]),
            r=[piv(n)], w=[T1.iv(512 * n, 512 * (n + 1))])
    for g in range(8):
        pl.emit("pe", lambda e, g=g: e.transpose(
            PBF16[0:128, 0:16], T1[0:16, 128 * g:128 * (g + 1)],
            IDENT[0:16, 0:16]),
            r=[T1.iv(), IDENT.iv()], w=[piv(7, 0, 64)])
        pl.emit("dve", lambda e, g=g: e.tensor_copy(
            out=T1T[:, 16 * g:16 * (g + 1)], in_=PBF16[0:128, 0:16]),
            r=[piv(7, 0, 64)], w=[T1T.iv(16 * g, 16 * (g + 1))])
    for n in range(4):
        for k in range(8):
            pl.emit("pe", lambda e, n=n, k=k: e.matmul(
                P[n][0:16, 0:512], T1T[:, 16 * k:16 * (k + 1)],
                FUSEB[:, 2048 * k + 512 * n: 2048 * k + 512 * (n + 1)],
                start=(k == 0), stop=(k == 7)),
                r=[T1T.iv(), FUSEB.iv()], w=[piv(n)])
        pl.emit("dve", lambda e, n=n: e.tensor_copy(
            out=G[0:16, 512 * n:512 * (n + 1)], in_=P[n][0:16, 0:512]),
            r=[piv(n)], w=[G.iv(512 * n, 512 * (n + 1))])

    for k in range(8):
        for n in range(4):
            pl.emit("pe", lambda e, n=n, k=k: e.matmul(
                P[4 + n][0:1, 0:512], HABBF[:, k:k + 1],
                FUSEB[:, 2048 * k + 512 * n: 2048 * k + 512 * (n + 1)],
                start=(k == 0), stop=False),
                r=[HABBF.iv(), FUSEB.iv()], w=[piv(4 + n, 0, 512)])
    for k in range(8):
        for n in range(4):
            pl.emit("pe", lambda e, n=n, k=k: e.matmul(
                P[4 + n][0:1, 0:512], HBBBF[:, k:k + 1],
                FUSEB[:, 2048 * (8 + k) + 512 * n: 2048 * (8 + k) + 512 * (n + 1)],
                start=False, stop=False),
                r=[HBBBF.iv(), FUSEB.iv()], w=[piv(4 + n, 0, 512)])
    for n in range(4):
        pl.emit("pe", lambda e, n=n: e.matmul(
            P[4 + n][0:1, 0:512], ONES128[0:1, 0:1],
            FBROW[0:1, 512 * n:512 * (n + 1)], start=False, stop=False),
            r=[ONES128.iv(), FBROW.iv()], w=[piv(4 + n, 0, 512)])

    for c in range(16):
        for n in range(4):
            bank = (c * 4 + n) % 2
            for d in range(8):
                pl.emit("pe", lambda e, c=c, n=n, d=d, bank=bank: e.matmul(
                    P[bank][0:128, 0:512],
                    HBWB[:, 2048 * d + 128 * c: 2048 * d + 128 * (c + 1)],
                    FUSEB[:, 2048 * (8 + d) + 512 * n: 2048 * (8 + d) + 512 * (n + 1)],
                    start=(d == 0), stop=(d == 7)),
                    r=[HBWB.iv(), FUSEB.iv()], w=[piv(bank)])
            pl.emit("act", lambda e, c=c, n=n, bank=bank: e.copy(
                out=T2B[:, 2048 * c + 512 * n: 2048 * c + 512 * (n + 1)],
                in_=P[bank][0:128, 0:512]),
                r=[piv(bank)],
                w=[T2B.iv(2048 * c + 512 * n, 2048 * c + 512 * (n + 1))])

    if dbg:
        pl.emit("dve", lambda e: e.tensor_copy(
            out=TF[:, 0:4096], in_=T2B[0:128, 0:4096]),
            r=[T2B.iv(0, 4096)], w=[TF.iv()])
        dma("sync", "d_o", dbt["t2"][:, :], TF[:, 0:4096],
            r=[TF.iv()], w=[DRAM1("dbg_t2")])

    if dbg:
        dma("pool", "d_dbg", dbt["fuseb"][:, :], FUSEB[:, 24576:28672],
            r=[FUSEB.iv(24576, 28672)], w=[DRAM1("dbg_fuseb")])

    for cc in range(16):
        for n in range(4):
            pl.emit("pe", lambda e, cc=cc, n=n: e.matmul(
                P[4 + n][0:1, 0:512], PWBBF[:, cc:cc + 1],
                T2B[:, 2048 * cc + 512 * n: 2048 * cc + 512 * (n + 1)],
                start=False, stop=False),
                r=[PWBBF.iv(), T2B.iv()], w=[piv(4 + n, 0, 512)])

    dma("pool", "d_pww", PWWB.rearrange("p (c q) -> p c q", q=2048),
        pww.rearrange("(c p) q -> p c q", p=128), r=[], w=[PWWB.iv()])

    for pc in range(16):
        mp = pc % 2
        for n in range(4):
            bank = (pc * 4 + n) % 2
            for cc in range(16):
                pl.emit("pe", lambda e, pc=pc, n=n, cc=cc, bank=bank: e.matmul(
                    P[bank][0:128, 0:512],
                    PWWB[:, 2048 * cc + 128 * pc: 2048 * cc + 128 * (pc + 1)],
                    T2B[:, 2048 * cc + 512 * n: 2048 * cc + 512 * (n + 1)],
                    start=(cc == 0), stop=(cc == 15)),
                    r=[PWWB.iv(), T2B.iv()], w=[piv(bank)])
            pl.emit("act", lambda e, n=n, mp=mp, bank=bank: e.copy(
                out=MPB[mp][:, 512 * n:512 * (n + 1)], in_=P[bank][0:128, 0:512]),
                r=[piv(bank)], w=[MPB[mp].iv(512 * n, 512 * (n + 1))])
        dma("act", f"d_sm{mp}",
            m_scr.rearrange("e p (c q) -> p e c q", q=128)[:, :, pc, :],
            MPB[mp].rearrange("p (e q) -> p e q", q=128),
            r=[MPB[mp].iv()], w=[DRAM1("m_scr")])

    for k in range(16):
        pl.emit("pe", lambda e, k=k: e.matmul(
            P[2][0:16, 0:1], ABF[:, 16 * k:16 * (k + 1)], DWABBF[:, k:k + 1],
            start=(k == 0), stop=(k == 15)),
            r=[ABF.iv(), DWABBF.iv()], w=[piv(2, 0, 1)])
    pl.emit("dve", lambda e: e.tensor_copy(out=VA[0:16, 0:1], in_=P[2][0:16, 0:1]),
            r=[piv(2, 0, 1)], w=[VA.iv()])
    for n in range(4):
        pl.emit("pe", lambda e, n=n: e.matmul(
            P[4 + n][0:1, 0:512], VA[0:16, 0:1],
            G[0:16, 512 * n:512 * (n + 1)], start=False, stop=True),
            r=[VA.iv(), G.iv()], w=[piv(4 + n, 0, 512)])
    for n in range(4):
        pl.emit("act", lambda e, n=n: e.copy(
            out=BIAS[0:1, 512 * n:512 * (n + 1)], in_=P[4 + n][0:1, 0:512]),
            r=[piv(4 + n, 0, 512)], w=[BIAS.iv(512 * n, 512 * (n + 1))])

    for c in range(16):
        for j in range(4):
            pl.emit("dve", lambda e, c=c, j=j: e.tensor_scalar(
                out=AJ[:, (c * 4 + j) * 16:(c * 4 + j + 1) * 16],
                in0=ABF[:, 16 * c:16 * (c + 1)],
                scalar1=DWAT[:, c * 4 + j:c * 4 + j + 1], scalar2=None,
                op0=ALU.mult),
                r=[ABF.iv(), DWAT.iv()],
                w=[AJ.iv((c * 4 + j) * 16, (c * 4 + j + 1) * 16)])

    DGP = [SBuf(nc, f"dgp{i}", [128, 1152], BF16, ZW.off + 448 + 2304 * i)
           for i in range(2)]
    for c in range(16):
        dp = c % 2
        for j in range(9):
            pl.emit("pool", lambda e, c=c, j=j, dp=dp: e.tensor_tensor(
                out=DGP[dp][:, 128 * j:128 * (j + 1)],
                in0=IDENT[:, :],
                in1=SYMT[:, c * 9 + j:c * 9 + j + 1].broadcast_to([128, 128]),
                op=ALU.mult),
                r=[IDENT.iv(), SYMT.iv()],
                w=[DGP[dp].iv(128 * j, 128 * (j + 1))])
        dma("sync", f"d_sd{dp}", dg_scr[c],
            DGP[dp][:, :],
            r=[DGP[dp].iv()], w=[DRAM1("dg_scr")])

    dma("pool", "d_siw", inw_scr.rearrange("o p q -> (o p) q"), inwT[:, :],
        r=[], w=[DRAM1("inw_scr")])
    dma("pool", "d_sf", wk_scr[:, :], wkT[:, :], r=[], w=[DRAM1("wk_scr")])
    dma("pool", "d_sf", wr_scr[:, :], wrT[:, :], r=[], w=[DRAM1("wr_scr")])
    dma("pool", "d_siw", ow_scr.rearrange("d p q -> (d p) q"), outwT[:, :],
        r=[], w=[DRAM1("ow_scr")])
    dma("pool", "d_sf", wv_scr[:, :], wvT[:, :], r=[], w=[DRAM1("wv_scr")])

    pl.emit("pool", lambda e: e.memset(
        WARN.rearrange("p (c t) -> p c t", t=CH)[:, :, 0:1], 0.0),
        w=[WARN.iv()])

    slab = {"inw": 0, "m": 0, "ow": 0, "fb": 0, "dg": 0, "tmps": 0}
    dg_base = [0]

    def norm_stats(src, width, do_ln1, qb=None, qs=None, mb_=2, sb_=7,
                   stage=0):
        qb = qb if qb is not None else SQB
        qs = qs if qs is not None else SQS
        w8 = width
        for h in (() if stage == 2 else range(2)):
            pl.emit("dve", lambda e, h=h: e.tensor_copy(
                out=qb.rearrange("p (c t) -> p c t", t=CH)[:, 4 * h:4 * h + 4, 0:w8],
                in_=src.rearrange("p (c t) -> p c t", t=CH)[:, 4 * h:4 * h + 4, 0:w8]),
                r=[src.iv(4 * h * CH, (4 * h + 4) * CH)],
                w=[qb.iv(4 * h * CH, (4 * h + 4) * CH)])
            pl.emit("act", lambda e, h=h: e.activation(
                out=qs.rearrange("p (c t) -> p c t", t=CH)[:, 4 * h:4 * h + 4, 0:w8],
                in_=src.rearrange("p (c t) -> p c t", t=CH)[:, 4 * h:4 * h + 4, 0:w8],
                func=AF.Square),
                r=[src.iv(4 * h * CH, (4 * h + 4) * CH)],
                w=[qs.iv(4 * h * CH, (4 * h + 4) * CH)])
        if stage == 1:
            return
        for half in range(2):
            for c in range(4 * half, 4 * half + 4):
                pl.emit("pe", lambda e, c=c: e.matmul(
                    P[0][0:1, 0:w8], ONESC[:, :], qb[:, c * CH:c * CH + w8],
                    start=(c == 0), stop=(c == NCH - 1)),
                    r=[ONESC.iv(), qb.iv(c * CH, c * CH + w8)], w=[piv(0, 0, w8)])
            for c in range(4 * half, 4 * half + 4):
                pl.emit("pe", lambda e, c=c: e.matmul(
                    P[1][0:1, 0:w8], ONESC[:, :], qs[:, c * CH:c * CH + w8],
                    start=(c == 0), stop=(c == NCH - 1)),
                    r=[ONESC.iv(), qs.iv(c * CH, c * CH + w8)], w=[piv(1, 0, w8)])
        pl.emit("dve", lambda e: e.tensor_copy(out=MROW[0:1, 0:w8],
                                               in_=P[0][0:1, 0:w8]),
                r=[piv(0, 0, w8)], w=[MROW.iv(0, w8)])
        pl.emit("dve", lambda e: e.tensor_tensor(
            out=MSQR[0:1, 0:w8], in0=MROW[0:1, 0:w8], in1=MROW[0:1, 0:w8],
            op=ALU.mult),
            r=[MROW.iv(0, w8)], w=[MSQR.iv(0, w8)])
        pl.emit("dve", lambda e: e.tensor_tensor(
            out=VROW[0:1, 0:w8], in0=P[1][0:1, 0:w8], in1=MSQR[0:1, 0:w8],
            op=ALU.subtract), r=[piv(1, 0, w8), MSQR.iv(0, w8)],
            w=[VROW.iv(0, w8)])
        pl.emit("act", lambda e: e.activation(
            out=LROW[0:1, 0:w8], in_=VROW[0:1, 0:w8], func=AF.Ln, bias=EPS_LN),
            r=[VROW.iv(0, w8), civ], w=[LROW.iv(0, w8)])
        pl.emit("act", lambda e: e.activation(
            out=SROW1[0:1, 0:w8], in_=LROW[0:1, 0:w8], func=AF.Exp, scale=-0.5),
            r=[LROW.iv(0, w8), civ], w=[SROW1.iv(0, w8)])
        if do_ln1:
            pl.emit("act", lambda e: e.activation(
                out=LROW[0:1, 0:w8], in_=VROW[0:1, 0:w8], func=AF.Ln,
                bias=1e-11, scale=1.000001),
                r=[VROW.iv(0, w8), civ], w=[LROW.iv(0, w8)])
            pl.emit("act", lambda e: e.activation(
                out=SROW2[0:1, 0:w8], in_=LROW[0:1, 0:w8], func=AF.Exp,
                scale=-0.5),
                r=[LROW.iv(0, w8), civ], w=[SROW2.iv(0, w8)])
        pl.emit("pe", lambda e: e.matmul(
            P[mb_][0:128, 0:w8], ONES128[:, :], MROW[0:1, 0:w8],
            start=True, stop=True),
            r=[ONES128.iv(), MROW.iv(0, w8)], w=[piv(mb_, 0, w8)])
        pl.emit("pe", lambda e: e.matmul(
            P[sb_][0:128, 0:w8], ONES128[:, :], SROW1[0:1, 0:w8],
            start=True, stop=True),
            r=[ONES128.iv(), SROW1.iv(0, w8)], w=[piv(sb_, 0, w8)])
        if do_ln1:
            pl.emit("pe", lambda e: e.matmul(
                P[0][0:128, 0:w8], ONES128[:, :], SROW2[0:1, 0:w8],
                start=True, stop=True),
                r=[ONES128.iv(), SROW2.iv(0, w8)], w=[piv(0, 0, w8)])

    def tile_geom(ti):
        s = ti * WC
        Wc = min(WC, L - s)
        Ww = Wc + 8
        w0 = s - 4
        vlo = max(0, w0) - w0
        vhi = min(L, w0 + Ww) - w0
        return s, Wc, Ww, w0, vlo, vhi

    def emit_xload(ti):
        s, Wc, Ww, w0, vlo, vhi = tile_geom(ti)
        dma("sync", "d_xw",
            XW.rearrange("p (c t) -> p c t", t=CH)[:, :, vlo:vhi],
            xdr.rearrange("(c p) t -> p c t", p=128)[:, :, w0 + vlo:w0 + vhi],
            r=[], w=[XW.iv()], tag=f"x{ti}")
        if vlo > 0:
            pl.emit("pool", lambda e, vlo=vlo: e.memset(
                XW.rearrange("p (c t) -> p c t", t=CH)[:, :, 0:vlo], 0.0),
                w=[XW.iv()])
        if vhi < Ww:
            pl.emit("pool", lambda e, vhi=vhi, Ww=Ww: e.memset(
                XW.rearrange("p (c t) -> p c t", t=CH)[:, :, vhi:Ww], 0.0),
                w=[XW.iv()])

    def emit_early_cast(tj):
        s, Wc, Ww, w0, vlo, vhi = tile_geom(tj)
        for h in range(2):
            pl.emit("dve", lambda e, h=h, Ww=Ww: e.tensor_copy(
                out=SQB.rearrange("p (c t) -> p c t", t=CH)[:, 4 * h:4 * h + 4, 0:Ww],
                in_=XW.rearrange("p (c t) -> p c t", t=CH)[:, 4 * h:4 * h + 4, 0:Ww]),
                r=[XW.iv(4 * h * CH, (4 * h + 4) * CH)],
                w=[SQB.iv(4 * h * CH, (4 * h + 4) * CH)])
            pl.emit("act", lambda e, h=h, Ww=Ww: e.activation(
                out=SQS.rearrange("p (c t) -> p c t", t=CH)[:, 4 * h:4 * h + 4, 0:Ww],
                in_=XW.rearrange("p (c t) -> p c t", t=CH)[:, 4 * h:4 * h + 4, 0:Ww],
                func=AF.Square),
                r=[XW.iv(4 * h * CH, (4 * h + 4) * CH)],
                w=[SQS.iv(4 * h * CH, (4 * h + 4) * CH)])

    def emit_early(tj):
        s, Wc, Ww, w0, vlo, vhi = tile_geom(tj)
        for c in range(NCH):
            pl.emit("pe", lambda e, c=c, Ww=Ww: e.matmul(
                P[0][0:1, 0:Ww], ONESC[:, :], SQB[:, c * CH:c * CH + Ww],
                start=(c == 0), stop=(c == NCH - 1)),
                r=[ONESC.iv(), SQB.iv(c * CH, c * CH + Ww)], w=[piv(0, 0, Ww)])
        for c in range(NCH):
            pl.emit("pe", lambda e, c=c, Ww=Ww: e.matmul(
                P[1][0:1, 0:Ww], ONESC[:, :], SQS[:, c * CH:c * CH + Ww],
                start=(c == 0), stop=(c == NCH - 1)),
                r=[ONESC.iv(), SQS.iv(c * CH, c * CH + Ww)], w=[piv(1, 0, Ww)])
        pl.emit("act", lambda e, Ww=Ww: e.copy(out=MROW[0:1, 0:Ww],
                                               in_=P[0][0:1, 0:Ww]),
                r=[piv(0, 0, Ww)], w=[MROW.iv(0, Ww)])
        pl.emit("act", lambda e, Ww=Ww: e.activation(
            out=MSQR[0:1, 0:Ww], in_=MROW[0:1, 0:Ww], func=AF.Square),
            r=[MROW.iv(0, Ww)], w=[MSQR.iv(0, Ww)])
        pl.emit("dve", lambda e, Ww=Ww: e.tensor_tensor(
            out=VROW[0:1, 0:Ww], in0=P[1][0:1, 0:Ww], in1=MSQR[0:1, 0:Ww],
            op=ALU.subtract), r=[piv(1, 0, Ww), MSQR.iv(0, Ww)],
            w=[VROW.iv(0, Ww)])
        pl.emit("act", lambda e, Ww=Ww: e.activation(
            out=LROW[0:1, 0:Ww], in_=VROW[0:1, 0:Ww], func=AF.Ln, bias=EPS_LN),
            r=[VROW.iv(0, Ww), civ], w=[LROW.iv(0, Ww)])
        pl.emit("act", lambda e, Ww=Ww: e.activation(
            out=LROW2[0:1, 0:Ww], in_=VROW[0:1, 0:Ww], func=AF.Ln,
            bias=1e-11, scale=1.000001),
            r=[VROW.iv(0, Ww), civ], w=[LROW2.iv(0, Ww)])
        pl.emit("act", lambda e, Ww=Ww: e.activation(
            out=SROW2[0:1, 0:Ww], in_=LROW2[0:1, 0:Ww], func=AF.Exp,
            scale=-0.5),
            r=[LROW2.iv(0, Ww), civ], w=[SROW2.iv(0, Ww)])
        pl.emit("dve", lambda e, Ww=Ww: e.tensor_tensor(
            out=VROW[0:1, 0:Ww], in0=LROW2[0:1, 0:Ww], in1=LROW[0:1, 0:Ww],
            op=ALU.subtract),
            r=[LROW2.iv(0, Ww), LROW.iv(0, Ww)], w=[VROW.iv(0, Ww)])
        pl.emit("act", lambda e, Ww=Ww: e.activation(
            out=SROW3[0:1, 0:Ww], in_=VROW[0:1, 0:Ww], func=AF.Exp, scale=0.5),
            r=[VROW.iv(0, Ww), civ], w=[SROW3.iv(0, Ww)])
        pl.emit("pe", lambda e, Ww=Ww: e.matmul(
            P[2][0:128, 0:Ww], ONES128[:, :], MROW[0:1, 0:Ww],
            start=True, stop=True),
            r=[ONES128.iv(), MROW.iv(0, Ww)], w=[piv(2, 0, Ww)])
        pl.emit("pe", lambda e, Ww=Ww: e.matmul(
            P[0][0:128, 0:Ww], ONES128[:, :], SROW2[0:1, 0:Ww],
            start=True, stop=True),
            r=[ONES128.iv(), SROW2.iv(0, Ww)], w=[piv(0, 0, Ww)])
        for c in range(NCH):
            pl.emit("dve", lambda e, c=c, Ww=Ww: e.tensor_tensor(
                out=TF[:, c * CH:c * CH + Ww], in0=XW[:, c * CH:c * CH + Ww],
                in1=P[2][0:128, 0:Ww], op=ALU.subtract),
                r=[XW.iv(c * CH, c * CH + Ww), piv(2, 0, Ww)],
                w=[TF.iv(c * CH, c * CH + Ww)])
        for c in range(NCH):
            pl.emit("dve", lambda e, c=c, Ww=Ww: e.scalar_tensor_tensor(
                out=ZW[:, c * CH:c * CH + Ww], in0=TF[:, c * CH:c * CH + Ww],
                scalar=NORMW[:, c:c + 1], in1=P[0][0:128, 0:Ww],
                op0=ALU.mult, op1=ALU.mult),
                r=[TF.iv(c * CH, c * CH + Ww), NORMW.iv(), piv(0, 0, Ww)],
                w=[ZW.iv(c * CH, c * CH + Ww)])
        if dbg and tj == DBG_TI:
            dma("pool", "d_dbg", dbt["erows"][0:1, :], MROW[0:1, :],
                r=[MROW.iv()], w=[DRAM1("dbg_erows")])
            dma("pool", "d_dbg", dbt["erows"][1:2, :], SROW2[0:1, :],
                r=[SROW2.iv()], w=[DRAM1("dbg_erows")])
            dma("pool", "d_dbg", dbt["erows"][2:3, :], SROW3[0:1, :],
                r=[SROW3.iv()], w=[DRAM1("dbg_erows")])
            dma("pool", "d_dbg", dbt["ezw"][:, :], ZW[:, :],
                r=[ZW.iv()], w=[DRAM1("dbg_ezw")])

    def emit_ln1x(ti):
        s, Wc, Ww, w0, vlo, vhi = tile_geom(ti)
        pl.emit("pe", lambda e, Ww=Ww: e.matmul(
            P[1][0:128, 0:Ww], ONES128[:, :], SROW3[0:1, 0:Ww],
            start=True, stop=True),
            r=[ONES128.iv(), SROW3.iv(0, Ww)], w=[piv(1, 0, Ww)])
        for c in range(NCH):
            pl.emit("dve", lambda e, c=c, Wc=Wc: e.scalar_tensor_tensor(
                out=LN1X[:, c * CH:c * CH + Wc],
                in0=ZW[:, c * CH + 4:c * CH + 4 + Wc],
                scalar=INVNW[:, c:c + 1], in1=P[1][0:128, 4:4 + Wc],
                op0=ALU.mult, op1=ALU.mult),
                r=[ZW.iv(c * CH, c * CH + CH), INVNW.iv(), piv(1)],
                w=[LN1X.iv(c * CH, c * CH + Wc)])

    def emit_s3(ti):
        s, Wc, Ww, w0, vlo, vhi = tile_geom(ti)
        emit_ln1x(ti)

        oc_order = list(range(16, 32)) + list(range(0, 16))
        for pi in range(16):
            oc0 = oc_order[2 * pi]
            par = slab["inw"] % 3
            slab["inw"] += 1
            ibuf = INWB if par < 2 else INWB3
            ibase = par * 2048 if par < 2 else 0
            dma("sync", f"d_inw{par}",
                ibuf[:, ibase:ibase + 2048],
                inw_scr[oc0 // 2],
                r=[DRAM1("inw_scr")], w=[ibuf.iv(ibase, ibase + 2048)])
            for sub in range(2):
                oc = oc0 + sub
                bank = rot()
                for k in range(NCH):
                    pl.emit("pe", lambda e, ibuf=ibuf, ibase=ibase, sub=sub,
                            k=k, bank=bank, Ww=Ww: e.matmul(
                                P[bank][0:128, 0:Ww],
                                ibuf[:, ibase + k * 256 + sub * 128:
                                     ibase + k * 256 + sub * 128 + 128],
                                ZW[:, k * CH:k * CH + Ww],
                                start=(k == 0), stop=(k == NCH - 1)),
                            r=[ibuf.iv(ibase, ibase + 2048),
                               ZW.iv(k * CH, k * CH + Ww)],
                            w=[piv(bank, 0, Ww)])
                if oc >= 16:
                    cc = oc - 16
                    pl.emit("act", lambda e, cc=cc, oc=oc, bank=bank, Ww=Ww:
                            e.activation(out=VAL[:, cc * CH:cc * CH + Ww],
                                         in_=P[bank][0:128, 0:Ww], func=AF.Identity,
                                         bias=INB[:, oc:oc + 1]),
                            r=[piv(bank, 0, Ww), INB.iv()],
                            w=[VAL.iv(cc * CH, cc * CH + Ww)])
                else:
                    pl.emit("act", lambda e, oc=oc, bank=bank, Wc=Wc:
                            e.activation(out=GATE[:, oc * CH:oc * CH + Wc],
                                         in_=P[bank][0:128, 4:4 + Wc],
                                         func=AF.Sigmoid, bias=INB[:, oc:oc + 1]),
                            r=[piv(bank, 0, CH), INB.iv()],
                            w=[GATE.iv(oc * CH, oc * CH + Wc)])
        if vlo > 0:
            pl.emit("pool", lambda e, vlo=vlo: e.memset(
                VAL.rearrange("p (c t) -> p c t", t=CH)[:, :, 0:vlo], 0.0),
                w=[VAL.iv()])
        if vhi < Ww:
            pl.emit("pool", lambda e, vhi=vhi, Ww=Ww: e.memset(
                VAL.rearrange("p (c t) -> p c t", t=CH)[:, :, vhi:Ww], 0.0),
                w=[VAL.iv()])
        dg_base[0] = slab["dg"]
        for c in range(3):
            par = slab["dg"] % 3
            slab["dg"] += 1
            dma("sync", f"d_dg{par}",
                DGB[:, par * 1152:(par + 1) * 1152], dg_scr[c],
                r=[DRAM1("dg_scr")], w=[DGB.iv(par * 1152, (par + 1) * 1152)])

    emit_xload(0)
    emit_early_cast(0)
    emit_early(0)
    emit_s3(0)
    prev_wc = WC
    for ti in range(n_tiles):
        s, Wc, Ww, w0, vlo, vhi = tile_geom(ti)

        namm = [0]
        for c in range(16):
            for j in range(4):
                namm[0] += 1
                first = namm[0] == 1
                last = namm[0] == 64
                d = j - 1
                pl.emit("pe", lambda e, c=c, j=j, d=d, first=first, last=last,
                        Wc=Wc: e.matmul(
                            P[7][0:16, 0:Wc],
                            AJ[:, (c * 4 + j) * 16:(c * 4 + j + 1) * 16],
                            VAL[:, c * CH + 4 + d:c * CH + 4 + d + Wc],
                            start=first, stop=last),
                        r=[AJ.iv(), VAL.iv(c * CH, c * CH + CH)],
                        w=[piv(7, 0, Wc)])
        pl.emit("dve", lambda e, Wc=Wc: e.tensor_copy(
            out=ALOW[0:16, 0:Wc], in_=P[7][0:16, 0:Wc]),
            r=[piv(7, 0, Wc)], w=[ALOW.iv(0, Wc)])

        for c in range(16):
            par = (dg_base[0] + c) % 3
            if c >= 3:
                slab["dg"] += 1
                dma("sync", f"d_dg{par}",
                    DGB[:, par * 1152:(par + 1) * 1152], dg_scr[c],
                    r=[DRAM1("dg_scr")], w=[DGB.iv(par * 1152, (par + 1) * 1152)])
            bank = rot()
            for j in range(9):
                d = j - 4
                pl.emit("pe", lambda e, c=c, j=j, d=d, par=par, bank=bank,
                        Wc=Wc: e.matmul(
                            P[bank][0:128, 0:Wc],
                            DGB[:, par * 1152 + 128 * j:par * 1152 + 128 * (j + 1)],
                            VAL[:, c * CH + 4 + d:c * CH + 4 + d + Wc],
                            start=(j == 0), stop=False),
                        r=[DGB.iv(par * 1152, (par + 1) * 1152),
                           VAL.iv(c * CH, c * CH + CH)],
                        w=[piv(bank, 0, Wc)])
            pl.emit("pe", lambda e, c=c, bank=bank, Wc=Wc: e.matmul(
                P[bank][0:128, 0:Wc], BIAS[32:33, 128 * c:128 * (c + 1)],
                ONESROW[32:33, 0:Wc], start=False, stop=True),
                r=[BIAS.iv(), ONESROW.iv()], w=[piv(bank, 0, Wc)])
            pl.emit("act", lambda e, bank=bank, Wc=Wc: e.activation(
                out=TMPS[:, 0:Wc], in_=P[bank][0:128, 0:Wc],
                func=AF.Sigmoid),
                r=[piv(bank, 0, Wc)], w=[TMPS.iv(0, Wc)])
            pl.emit("dve", lambda e, c=c, bank=bank, Wc=Wc: e.tensor_tensor(
                out=SILU[:, c * CH:c * CH + Wc], in0=P[bank][0:128, 0:Wc],
                in1=TMPS[:, 0:Wc], op=ALU.mult),
                r=[piv(bank, 0, Wc), TMPS.iv(0, Wc)],
                w=[SILU.iv(c * CH, c * CH + Wc)])

        if dbg and ti == DBG_TI:
            for nm, buf in [("val", VAL), ("gate", GATE), ("silu", SILU),
                            ("alow", ALOW), ("zw", ZW), ("ln1x", LN1X),
                            ("g", G), ("bias", BIAS)]:
                rr = buf.t.shape[0]
                if nm == "bias":
                    rr = 1
                cc = min(buf.ncols, 4096)
                pl.emit("dve", lambda e, buf=buf, rr=rr, cc=cc: e.tensor_copy(
                    out=TF[0:rr, 0:cc], in_=buf[0:rr, 0:cc]),
                    r=[buf.iv()], w=[TF.iv()], tag="dbgc" + nm)
                dma("sync", "d_o", dbt[nm][:, :], TF[0:rr, 0:cc],
                    r=[TF.iv()], w=[DRAM1("dbg_" + nm)])
        for ec in range(16):
            par = slab["m"] % 3
            slab["m"] += 1
            dma("sync", f"d_mb{par}",
                MB[:, par * 2048:(par + 1) * 2048],
                m_scr[ec],
                r=[DRAM1("m_scr")], w=[MB.iv(par * 2048, (par + 1) * 2048)])
            bank = rot()
            pl.emit("pe", lambda e, ec=ec, bank=bank, Wc=Wc: e.matmul(
                P[bank][0:128, 0:Wc], G[0:16, 128 * ec:128 * (ec + 1)],
                ALOW[0:16, 0:Wc], start=True, stop=False),
                r=[G.iv(), ALOW.iv(0, Wc)], w=[piv(bank, 0, Wc)])
            for pc in range(16):
                pl.emit("pe", lambda e, pc=pc, par=par, bank=bank, Wc=Wc:
                        e.matmul(
                            P[bank][0:128, 0:Wc],
                            MB[:, par * 2048 + pc * 128:
                               par * 2048 + pc * 128 + 128],
                            SILU[:, pc * CH:pc * CH + Wc],
                            start=False, stop=False),
                        r=[MB.iv(par * 2048, (par + 1) * 2048),
                           SILU.iv(pc * CH, pc * CH + Wc)],
                        w=[piv(bank, 0, Wc)])
            pl.emit("pe", lambda e, ec=ec, bank=bank, Wc=Wc: e.matmul(
                P[bank][0:128, 0:Wc], BIAS[0:1, 128 * ec:128 * (ec + 1)],
                ONESROW[0:1, 0:Wc], start=False, stop=True),
                r=[BIAS.iv(), ONESROW.iv()], w=[piv(bank, 0, Wc)])
            pl.emit("dve", lambda e, ec=ec, bank=bank, Wc=Wc: e.tensor_tensor(
                out=YG[:, ec * CH:ec * CH + Wc], in0=P[bank][0:128, 0:Wc],
                in1=GATE[:, ec * CH:ec * CH + Wc], op=ALU.mult),
                r=[piv(bank, 0, Wc), GATE.iv(ec * CH, ec * CH + Wc)],
                w=[YG.iv(ec * CH, ec * CH + Wc)])

        for dc in range(8):
            par = slab["ow"] % 2
            slab["ow"] += 1
            dma("sync", f"d_ow{par}",
                OWB[:, par * 2048:(par + 1) * 2048],
                ow_scr[dc],
                r=[DRAM1("ow_scr")], w=[OWB.iv(par * 2048, (par + 1) * 2048)])
            bank = rot()
            for ec in range(16):
                pl.emit("pe", lambda e, ec=ec, par=par, bank=bank, Wc=Wc:
                        e.matmul(P[bank][0:128, 0:Wc],
                                 OWB[:, par * 2048 + 128 * ec:
                                     par * 2048 + 128 * (ec + 1)],
                                 YG[:, ec * CH:ec * CH + Wc],
                                 start=(ec == 0), stop=False),
                        r=[OWB.iv(par * 2048, (par + 1) * 2048),
                           YG.iv(ec * CH, ec * CH + Wc)],
                        w=[piv(bank, 0, Wc)])
            pl.emit("pe", lambda e, dc=dc, bank=bank, Wc=Wc: e.matmul(
                P[bank][0:128, 0:Wc], IDENT[:, :],
                LN1X[:, dc * CH:dc * CH + Wc], start=False, stop=True),
                r=[IDENT.iv(), LN1X.iv(dc * CH, dc * CH + Wc)],
                w=[piv(bank, 0, Wc)])
            pl.emit("act", lambda e, dc=dc, bank=bank, Wc=Wc: e.activation(
                out=ATTN[:, dc * CH:dc * CH + Wc], in_=P[bank][0:128, 0:Wc],
                func=AF.Identity, bias=OUTB[:, dc:dc + 1]),
                r=[piv(bank, 0, Wc), OUTB.iv()],
                w=[ATTN.iv(dc * CH, dc * CH + Wc)])

        if dbg and ti == DBG_TI:
            dma("sync", "d_o", dbt["attn"][:, :], ATTN[:, :],
                r=[ATTN.iv()], w=[DRAM1("dbg_attn")])
        norm_stats(ATTN, Wc, do_ln1=False)
        for c in range(NCH):
            pl.emit("dve", lambda e, c=c, Wc=Wc: e.tensor_tensor(
                out=TF[:, c * CH:c * CH + Wc], in0=ATTN[:, c * CH:c * CH + Wc],
                in1=P[2][0:128, 0:Wc], op=ALU.subtract),
                r=[ATTN.iv(c * CH, c * CH + Wc), piv(2, 0, Wc)],
                w=[TF.iv(c * CH, c * CH + Wc)])
            pl.emit("dve", lambda e, c=c, Wc=Wc: e.tensor_tensor(
                out=UA[:, c * CH:c * CH + Wc], in0=TF[:, c * CH:c * CH + Wc],
                in1=P[7][0:128, 0:Wc], op=ALU.mult),
                r=[TF.iv(c * CH, c * CH + Wc), piv(7, 0, Wc)],
                w=[UA.iv(c * CH, c * CH + Wc)])
            pl.emit("pool", lambda e, c=c, Wc=Wc: e.tensor_tensor(
                out=X1[:, c * CH:c * CH + Wc], in0=UA[:, c * CH:c * CH + Wc],
                in1=XW[:, c * CH + 4:c * CH + 4 + Wc], op=ALU.add),
                r=[UA.iv(c * CH, c * CH + Wc), XW.iv(c * CH, c * CH + CH)],
                w=[X1.iv(c * CH, c * CH + Wc)])

        if ti + 1 < n_tiles:
            emit_xload(ti + 1)

        if ti > 0:
            pl.emit("pool", lambda e: e.tensor_copy(
                out=WARN.rearrange("p (c t) -> p c t", t=CH)[:, :, 0:1],
                in_=WCARRY.rearrange("p (c o) -> p c o", o=1)),
                r=[WCARRY.iv()], w=[WARN.iv()])
        norm_stats(X1, Wc, do_ln1=False)

        def emit_mix(c, Wc):
            pl.emit("pool", lambda e, c=c, Wc=Wc: e.tensor_tensor(
                out=UA[:, c * CH:c * CH + Wc],
                in0=WARN[:, c * CH + 1:c * CH + 1 + Wc],
                in1=WARN[:, c * CH:c * CH + Wc], op=ALU.subtract),
                r=[WARN.iv(c * CH, c * CH + 1 + Wc)],
                w=[UA.iv(c * CH, c * CH + Wc)])

        def emit_xk(c, Wc):
            pl.emit("dve", lambda e, c=c, Wc=Wc: e.scalar_tensor_tensor(
                out=XK[:, c * CH:c * CH + Wc], in0=UA[:, c * CH:c * CH + Wc],
                scalar=MUK[:, c:c + 1], in1=WARN[:, c * CH:c * CH + Wc],
                op0=ALU.mult, op1=ALU.add),
                r=[UA.iv(c * CH, c * CH + Wc), MUK.iv(),
                   WARN.iv(c * CH, c * CH + Wc)],
                w=[XK.iv(c * CH, c * CH + Wc)])

        def emit_xr(c, Wc):
            pl.emit("dve", lambda e, c=c, Wc=Wc: e.scalar_tensor_tensor(
                out=XR[:, c * CH:c * CH + Wc], in0=UA[:, c * CH:c * CH + Wc],
                scalar=MUR[:, c:c + 1], in1=WARN[:, c * CH:c * CH + Wc],
                op0=ALU.mult, op1=ALU.add),
                r=[UA.iv(c * CH, c * CH + Wc), MUR.iv(),
                   WARN.iv(c * CH, c * CH + Wc)],
                w=[XR.iv(c * CH, c * CH + Wc)])

        for c in range(NCH):
            pl.emit("dve", lambda e, c=c, Wc=Wc: e.tensor_tensor(
                out=TF[:, c * CH:c * CH + Wc], in0=X1[:, c * CH:c * CH + Wc],
                in1=P[2][0:128, 0:Wc], op=ALU.subtract),
                r=[X1.iv(c * CH, c * CH + Wc), piv(2, 0, Wc)],
                w=[TF.iv(c * CH, c * CH + Wc)])
            pl.emit("dve", lambda e, c=c, Wc=Wc: e.tensor_tensor(
                out=WARN[:, c * CH + 1:c * CH + 1 + Wc],
                in0=TF[:, c * CH:c * CH + Wc],
                in1=P[7][0:128, 0:Wc], op=ALU.mult),
                r=[TF.iv(c * CH, c * CH + Wc), piv(7, 0, Wc)],
                w=[WARN.iv(c * CH + 1, c * CH + 1 + Wc)])
            emit_mix(c, Wc)
            if c >= 1:
                emit_xk(c - 1, Wc)
        emit_xk(NCH - 1, Wc)
        for c in range(NCH):
            emit_xr(c, Wc)
        prev_wc = Wc

        pl.emit("pool", lambda e, Wc=Wc: e.tensor_copy(
            out=WCARRY.rearrange("p (c o) -> p c o", o=1),
            in_=WARN.rearrange("p (c t) -> p c t", t=CH)[:, :, Wc:Wc + 1]),
            r=[WARN.iv()], w=[WCARRY.iv()])

        def ffn_pass(scr_name, rhs_buf, evac):
            for q in range(4):
                par = slab["fb"] % 4
                slab["fb"] += 1
                fbuf = FBB if par < 3 else FBB4
                fbase = par * 2048 if par < 3 else 0
                dma("sync", f"d_fb{par}",
                    fbuf[:, fbase:fbase + 2048],
                    ffn_scr[scr_name][128 * q:128 * (q + 1), :],
                    r=[DRAM1(scr_name + "_scr")],
                    w=[fbuf.iv(fbase, fbase + 2048)])
                for ds in range(2):
                    dch = 2 * q + ds
                    bank = rot()
                    for c in range(NCH):
                        pl.emit("pe", lambda e, fbuf=fbuf, fbase=fbase, ds=ds,
                                c=c, bank=bank,
                                Wc=Wc, rhs_buf=rhs_buf: e.matmul(
                                    P[bank][0:128, 0:Wc],
                                    fbuf[:, fbase + c * 256 + ds * 128:
                                         fbase + c * 256 + ds * 128 + 128],
                                    rhs_buf[:, c * CH:c * CH + Wc],
                                    start=(c == 0), stop=(c == NCH - 1)),
                                r=[fbuf.iv(fbase, fbase + 2048),
                                   rhs_buf.iv(c * CH, c * CH + Wc)],
                                w=[piv(bank, 0, Wc)])
                    evac(dch, bank)

        def r_evac(dch, bank):
            pl.emit("act", lambda e, dch=dch, bank=bank, Wc=Wc: e.activation(
                out=RS[:, dch * CH:dch * CH + Wc], in_=P[bank][0:128, 0:Wc],
                func=AF.Sigmoid),
                r=[piv(bank, 0, Wc)], w=[RS.iv(dch * CH, dch * CH + Wc)])

        def k_evac(dch, bank):
            pl.emit("act", lambda e, dch=dch, bank=bank, Wc=Wc: e.activation(
                out=KRELU[:, dch * CH:dch * CH + Wc], in_=P[bank][0:128, 0:Wc],
                func=AF.Relu),
                r=[piv(bank, 0, Wc)], w=[KRELU.iv(dch * CH, dch * CH + Wc)])
            pl.emit("dve", lambda e, dch=dch, Wc=Wc: e.tensor_tensor(
                out=KACT[:, dch * CH:dch * CH + Wc],
                in0=KRELU[:, dch * CH:dch * CH + Wc],
                in1=KRELU[:, dch * CH:dch * CH + Wc], op=ALU.mult),
                r=[KRELU.iv(dch * CH, dch * CH + Wc)],
                w=[KACT.iv(dch * CH, dch * CH + Wc)])

        def v_evac(dch, bank):
            pl.emit("dve", lambda e, dch=dch, bank=bank, Wc=Wc: e.tensor_tensor(
                out=FFO[:, dch * CH:dch * CH + Wc], in0=P[bank][0:128, 0:Wc],
                in1=RS[:, dch * CH:dch * CH + Wc], op=ALU.mult),
                r=[piv(bank, 0, Wc), RS.iv(dch * CH, dch * CH + Wc)],
                w=[FFO.iv(dch * CH, dch * CH + Wc)])

        ffn_pass("wk", XK, k_evac)
        ffn_pass("wr", XR, r_evac)
        if ti + 1 < n_tiles:
            emit_early_cast(ti + 1)
            emit_early(ti + 1)
        ffn_pass("wv", KACT, v_evac)

        if dbg and ti == DBG_TI:
            pl.emit("dve", lambda e: e.tensor_copy(
                out=TF[:, 0:4096], in_=XR[:, :]), r=[XR.iv()], w=[TF.iv()])
            dma("sync", "d_o", dbt["xr"][:, :], TF[:, 0:4096],
                r=[TF.iv()], w=[DRAM1("dbg_xr")])
            dma("sync", "d_o", dbt["ffo"][:, :], FFO[:, :],
                r=[FFO.iv()], w=[DRAM1("dbg_ffo")])
        norm_stats(FFO, Wc, do_ln1=False, qb=SQB2, qs=SQS2, stage=1)
        if ti + 1 < n_tiles:
            emit_s3(ti + 1)

        norm_stats(FFO, Wc, do_ln1=False, qb=SQB2, qs=SQS2, mb_=0, sb_=1,
                   stage=2)
        for c in range(NCH):
            pl.emit("dve", lambda e, c=c, Wc=Wc: e.tensor_tensor(
                out=T4P[c % 2][:, 0:Wc], in0=FFO[:, c * CH:c * CH + Wc],
                in1=P[0][0:128, 0:Wc], op=ALU.subtract),
                r=[FFO.iv(c * CH, c * CH + Wc), piv(0, 0, Wc)],
                w=[T4P[c % 2].iv()])
            pl.emit("dve", lambda e, c=c, Wc=Wc: e.tensor_tensor(
                out=X2[:, c * CH:c * CH + Wc], in0=T4P[c % 2][:, 0:Wc],
                in1=P[1][0:128, 0:Wc], op=ALU.mult),
                r=[T4P[c % 2].iv(), piv(1, 0, Wc)],
                w=[X2.iv(c * CH, c * CH + Wc)])
            pl.emit("pool", lambda e, c=c, Wc=Wc: e.tensor_tensor(
                out=X2[:, c * CH:c * CH + Wc], in0=X2[:, c * CH:c * CH + Wc],
                in1=X1[:, c * CH:c * CH + Wc], op=ALU.add),
                r=[X2.iv(c * CH, c * CH + Wc), X1.iv(c * CH, c * CH + Wc)],
                w=[X2.iv(c * CH, c * CH + Wc)])

        dma("act", "d_o",
            outdr.rearrange("(c p) t -> p c t", p=128)[:, :, s:s + Wc],
            X2.rearrange("p (c t) -> p c t", t=CH)[:, :, 0:Wc],
            r=[X2.iv()], w=[DRAM1("out")], tag=f"out{ti}")

    pl.emit("sync", lambda e: e.nop(), r=[DRAM1("out")])
    pl.finalize()

    with contextlib.ExitStack() as st:
        sems = {}
        for name in ["d_ld", "d_ldp", "d_dbg", "d_sf", "d_siw",
                     "d_sm0", "d_sm1", "d_sd0", "d_sd1", "d_fuse",
                     "d_hbw", "d_pww", "d_ha0", "d_ha1",
                     "d_inw0", "d_inw1", "d_inw2", "d_mb0", "d_mb1", "d_mb2", "d_ow0", "d_ow1",
                     "d_fb0", "d_fb1", "d_fb2", "d_fb3", "d_dg0", "d_dg1", "d_dg2",
                     "d_xw", "d_o",
                     "e_pe", "e_act", "e_dve", "e_pool", "e_sync"]:
            sems[name] = st.enter_context(nc.semaphore(name))
        block = st.enter_context(nc.Block())
        block.tensor(lambda e: pl.run("pe", e, sems))
        block.scalar(lambda e: pl.run("act", e, sems))
        block.vector(lambda e: pl.run("dve", e, sems))
        block.gpsimd(lambda e: pl.run("pool", e, sems))
        block.sync(lambda e: pl.run("sync", e, sems))

    return nc


def _host_inputs(inp):
    asc = np.ascontiguousarray
    f = np.float32

    def colchunks(v, nch):
        return asc(np.asarray(v, f).reshape(nch, 128).T)

    w = {}
    w["inwT"] = asc(np.asarray(inp["L0_attn_in_w"], f).T
                    .reshape(8, 128, 16, 256).transpose(2, 1, 0, 3)
                    .reshape(2048, 2048))
    w["normw"] = colchunks(inp["L0_attn_norm_w"], NCH)
    w["inb"] = colchunks(inp["L0_attn_in_b"], 32)
    dk = np.asarray(inp["L0_attn_dwa_k"], f)[:, 0, :]
    w["dwataps"] = asc(dk.reshape(16, 128, 4).transpose(1, 0, 2).reshape(128, 64))
    w["dwab"] = colchunks(inp["L0_attn_dwa_b"], 16)
    w["abuf"] = asc(np.asarray(inp["L0_attn_A"], f))
    w["bmT"] = asc(np.asarray(inp["L0_attn_Bm"], f).T)
    sk = np.asarray(inp["L0_attn_sym_k"], f)[:, 0, :]
    w["symtaps"] = asc(sk.reshape(16, 128, 9).transpose(1, 0, 2).reshape(128, 144))
    w["symbrow"] = asc(np.asarray(inp["L0_attn_sym_b"], f).reshape(1, DI))
    w["pww"] = asc(np.asarray(inp["L0_attn_pw_w"], f))
    w["pwb"] = colchunks(inp["L0_attn_pw_b"], 16)
    w["hawT"] = asc(np.asarray(inp["L0_attn_ha_w"], f).T)
    w["hab"] = colchunks(inp["L0_attn_ha_b"], 8)
    w["hbw"] = asc(np.asarray(inp["L0_attn_hb_w"], f))
    w["hbb"] = colchunks(inp["L0_attn_hb_b"], 8)
    w["fuseT"] = asc(np.asarray(inp["L0_attn_fuse_w"], f).T)
    w["fbrow"] = asc(np.asarray(inp["L0_attn_fuse_b"], f).reshape(1, DI))
    w["outwT"] = asc(np.asarray(inp["L0_attn_out_w"], f).T
                     .reshape(16, 128, 8, 128).transpose(2, 1, 0, 3)
                     .reshape(1024, 2048))
    w["outb"] = colchunks(inp["L0_attn_out_b"], 8)
    def _ffnlay(a):
        return asc(np.asarray(a, f).T.reshape(8, 128, 4, 256)
                   .transpose(2, 1, 0, 3).reshape(512, 2048))
    w["wrT"] = _ffnlay(inp["L0_ffn_W_r"])
    w["wkT"] = _ffnlay(inp["L0_ffn_W_k"])
    w["wvT"] = _ffnlay(inp["L0_ffn_W_v"])
    w["mur"] = colchunks(inp["L0_ffn_mu_r"], 8)
    w["muk"] = colchunks(inp["L0_ffn_mu_k"], 8)
    w["ident"] = np.eye(128, dtype=ml_dtypes.bfloat16)
    return w


def kernel(**inp):
    x = np.asarray(inp["L0_attn_x"], np.float32)
    B, L, _ = x.shape
    w = _host_inputs(inp)
    nc = build_nc(L)
    in_maps = []
    for b in range(B):
        m = dict(w)
        m["x"] = np.ascontiguousarray(x[b].T)
        in_maps.append(m)
    res = run_bass_kernel_spmd(nc, in_maps, core_ids=list(range(B)))
    out = np.stack([np.ascontiguousarray(r["out"].T) for r in res.results])
    return out.astype(np.float32)
